# version 12
# baseline (speedup 1.0000x reference)
import numpy as np
import ml_dtypes
import concourse.bass as bass
import concourse.mybir as mybir
from concourse.bass_utils import run_bass_kernel_spmd
from contextlib import ExitStack

F32 = mybir.dt.float32
BF16 = mybir.dt.bfloat16
AF = mybir.ActivationFunctionType
ALU = mybir.AluOpType
AX = mybir.AxisListType
BF = ml_dtypes.bfloat16

C = 1024
NHC = 4
LNEPS = 1e-5
ENGS = ("sync", "gpsimd", "scalar", "vector", "tensor")


class Op:
    __slots__ = ("eng", "fn", "deps", "dma", "manual", "seq", "needed", "tag", "slot", "serial")

    def __init__(self, eng, fn, deps, dma, manual, tag, serial=False):
        self.eng, self.fn, self.deps = eng, fn, list(deps)
        self.dma, self.manual, self.tag = dma, manual, tag
        self.serial = serial
        self.seq = 0
        self.needed = False


class Plan:
    def __init__(self):
        self.ops = []

    def add(self, eng, fn, deps=(), dma=0, manual=0, tag="", serial=False):
        op = Op(eng, fn, [d for d in deps if d is not None], dma, manual, tag,
                serial)
        self.ops.append(op)
        return op

    NSLOT = 8

    def emit(self, nc, blk, sems):
        for op in self.ops:
            for d in op.deps:
                d.needed = True
        dslot = {e: 0 for e in ENGS}
        cnt = {}
        for op in self.ops:
            if op.dma or op.manual:
                op.slot = dslot[op.eng]
                dslot[op.eng] = (dslot[op.eng] + 1) % Plan.NSLOT
                k = (op.eng, 'd', op.slot)
                cnt[k] = cnt.get(k, 0) + (op.manual if op.manual
                                          else 16 * op.dma)
                op.seq = cnt[k]
            else:
                op.slot = None
                k = (op.eng, 'c')
                if op.serial:
                    cnt[k] = cnt.get(k, 0) + len(op.fn)
                elif op.needed:
                    cnt[k] = cnt.get(k, 0) + 1
                op.seq = cnt.get(k, 0)

        def body(eng_name):
            def f(eng):
                last_wait = {}
                for op in self.ops:
                    if op.eng != eng_name:
                        continue
                    for d in op.deps:
                        if d.seq == 0 or not (d.needed or d.manual):
                            continue
                        dk = ((d.eng, 'd', d.slot) if d.slot is not None
                              else (d.eng, 'c'))
                        if d.seq > last_wait.get(dk, 0):
                            eng.wait_ge(sems[dk], d.seq)
                            last_wait[dk] = d.seq
                    if op.manual:
                        n = op.fn(eng, sems[(eng_name, 'd', op.slot)])
                        assert n == op.manual, (op.tag, n, op.manual)
                    elif op.serial:
                        base = op.seq - len(op.fn)
                        semc = sems[(eng_name, 'c')]
                        for i2, f2 in enumerate(op.fn):
                            ins = f2(eng)
                            ins.then_inc(semc, 1)
                            if i2 < len(op.fn) - 1:
                                eng.wait_ge(semc, base + i2 + 1)
                        last_wait[(eng_name, 'c')] = op.seq - 1
                    else:
                        instrs = op.fn(eng)
                        if op.needed:
                            if op.dma:
                                assert len(instrs) == op.dma, (
                                    op.tag, len(instrs), op.dma)
                                for ins in instrs:
                                    ins.then_inc(
                                        sems[(eng_name, 'd', op.slot)], 16)
                            else:
                                instrs[-1].then_inc(sems[(eng_name, 'c')], 1)
            return f

        blk.sync(body("sync"))
        blk.gpsimd(body("gpsimd"))
        blk.scalar(body("scalar"))
        blk.vector(body("vector"))
        blk.tensor(body("tensor"))


def build(T=8192, dbg=None):
    if dbg is None:
        dbg = T < 8192
    NT = T // 128
    NSUP = T // 256
    NTCH = T // 1024
    SLICE = T // 4
    CHALF = min(1024, SLICE)
    NHALF = SLICE // CHALF
    TPH = CHALF // 128

    nc = bass.Bass("TRN2", target_bir_lowering=False,
                   detect_race_conditions=False)

    def din(name, shape, dt):
        return nc.dram_tensor(name, shape, dt, kind="ExternalInput").ap()
    xb = din("xb", [T, C], F32)
    xsl = din("xsl", [SLICE, C], F32)
    xb0 = din("xb0", [1, C], F32)
    wqv = din("wqv", [C, 512], F32)
    wkg = din("wkg", [C, 260], F32)
    wr_d = din("wr", [C, C], F32)
    wk_d = din("wk2", [C, C], F32)
    wv_d = din("wv2", [C, C], F32)
    mur = din("mur", [128, 8], F32)
    muk = din("muk", [128, 8], F32)
    identb = din("identb", [128, 128], BF16)
    identf = din("identf", [128, 128], F32)
    triu_d = din("triu", [128, 128], BF16)
    onesrow = din("onesrow", [1, 128], F32)
    out = nc.dram_tensor("out", [SLICE, C], F32, kind="ExternalOutput").ap()
    if dbg:
        dbgy = nc.dram_tensor("dbgy", [T, 256], BF16,
                              kind="ExternalOutput").ap()
        dbgl = nc.dram_tensor("dbgl", [128, NT * 4 + 12],
                              F32, kind="ExternalOutput").ap()
        dbg2o = nc.dram_tensor("dbg2o", [128, 2048], F32,
                               kind="ExternalOutput").ap()
        dbg3o = nc.dram_tensor("dbg3o", [128, 1536], F32,
                               kind="ExternalOutput").ap()

    cc_in_h = nc.dram_tensor("cc_in", [T, 256], BF16, kind="Internal")
    cc_out_h = nc.dram_tensor("cc_out", [4 * T, 256], BF16, kind="Internal")

    per = ExitStack()
    sems = {}
    for e in ENGS:
        sems[(e, 'c')] = per.enter_context(nc.semaphore("s_" + e + "_c"))
        for sl in range(Plan.NSLOT):
            sems[(e, 'd', sl)] = per.enter_context(
                nc.semaphore(f"s_{e}_d{sl}"))

    def sb(name, shape, dt):
        return per.enter_context(nc.sbuf_tensor(name, shape, dt))

    s_idb = sb("s_idb", [128, 128], BF16)
    s_idf = sb("s_idf", [128, 128], F32)
    s_triu = sb("s_triu", [128, 128], BF16)
    s_ones = sb("s_ones", [1, 128], F32)
    s_mur = sb("s_mur", [128, 8], F32)
    s_muk = sb("s_muk", [128, 8], F32)
    E = sb("E", [128, 6400], F32)
    dummy_bf = sb("dummy_bf", [128, 2048], BF16)
    LGcol = sb("LGcol", [128, NT * 4], F32)
    statR = sb("statR", [128, NT], F32)
    statQ = sb("statQ", [128, NT], F32)
    statM = sb("statM", [128, NT], F32)
    statS = sb("statS", [128, NT], F32)
    statB = sb("statB", [128, NT], F32)
    crow = sb("crow", [1, 4 * NT], F32)
    Chalf = sb("Chalf", [1, 4 * NSUP], F32)
    Erow = sb("Erow", [1, 4 * NT], F32)
    Frow = sb("Frow", [1, 4 * NT], F32)
    fcol = sb("fcol", [128, 12], F32)
    ybnd = sb("ybnd", [1, C], BF16)
    yscr = sb("yscr", [1, 256], BF16)
    xb0s = sb("xb0s", [1, C], F32)
    x1b = sb("x1b", [1, C], F32)
    xn2b = sb("xn2b", [1, C], F32)
    bnds = sb("bnds", [1, 8], F32)
    kcol = sb("kcol", [128, 2], F32)
    dbg2 = sb("dbg2", [128, 2048], F32) if dbg else None
    dbg3 = sb("dbg3", [128, 1536], F32) if dbg else None

    blk = per.enter_context(nc.Block())
    P = Plan()

    ab = ExitStack()

    def sba(name, shape, dt):
        return ab.enter_context(nc.sbuf_tensor(name, shape, dt))

    Wqv = sba("Wqv", [128, 8 * 512], BF16)
    Wkg = sba("Wkg", [128, 8 * 260], BF16)
    q_tok = sba("q_tok", [128, NT * 256], BF16)
    Kfm = [sba("Kfm0", [128, T], BF16), sba("Kfm1", [128, T], BF16)]
    V2 = sba("V2", [128, NT * 260], BF16)
    xn_tok = sba("xn_tok", [128, 2 * 1024], BF16)
    xn_fm = [sba("xn_fm0", [128, 8 * 512], BF16),
             sba("xn_fm1", [128, 8 * 512], BF16)]
    SS = [sba("SS0", [128, NHC * NT], F32),
          sba("SS1", [128, NHC * NT], F32)]
    Fbc = sba("Fbc", [128, 4 * NT], F32)
    Cbc = sba("Cbc", [128, 4 * NSUP], F32)
    wt = sba("wt", [128, 4 * NHC * 256], BF16)
    qt_sc = [sba("qt_sc0", [128, 512], BF16),
             sba("qt_sc1", [128, 512], BF16)]
    qfm = [[sba("qfm00", [128, 256], BF16), sba("qfm01", [128, 256], BF16)],
           [sba("qfm10", [128, 256], BF16), sba("qfm11", [128, 256], BF16)]]
    Uev = sba("Uev", [65, 1024], F32)
    rncol = sba("rncol", [128, 2], F32)
    ybuf = sba("ybuf", [128, 512], BF16)

    psA = [ab.enter_context(nc.psum_tensor(f"psA{i}", [128, 512], F32))
           for i in range(2)]
    psU = [ab.enter_context(nc.psum_tensor(f"psU{i}", [128, 512], F32))
           for i in range(4)]
    pk = ab.enter_context(nc.psum_tensor("pk", [128, 1024], F32))

    v2v = V2[:].rearrange("p (j h x) -> p j h x", j=NT, h=NHC)

    op_const = P.add("sync", lambda e: [
        e.dma_start(out=s_idb[:], in_=identb),
        e.dma_start(out=s_idf[:], in_=identf),
        e.dma_start(out=s_triu[:], in_=triu_d),
        e.dma_start(out=s_ones[:], in_=onesrow),
        e.dma_start(out=s_mur[:], in_=mur),
        e.dma_start(out=s_muk[:], in_=muk),
        e.dma_start(out=xb0s[:], in_=xb0),
    ], dma=7, tag="consts")
    op_wload = P.add("gpsimd", lambda e: [
        e.dma_start(out=Wqv[:].rearrange("p (cc o) -> p cc o", cc=8),
                    in_=wqv.rearrange("(cc p) o -> p cc o", p=128)),
        e.dma_start(out=Wkg[:].rearrange("p (cc o) -> p cc o", cc=8),
                    in_=wkg.rearrange("(cc p) o -> p cc o", p=128)),
    ], dma=2, tag="wload")
    op_v2ones = P.add("gpsimd", lambda e: [
        e.memset(v2v[:, :, :, 64:65], 1.0)], tag="v2ones")
    op_ybnd0 = P.add("gpsimd", lambda e: [e.memset(ybnd[:], 0.0)], tag="ybnd0")
    op_kcol = P.add("gpsimd", lambda e: [
        e.memset(kcol[:, 0:1], LNEPS), e.memset(kcol[:, 1:2], 1.0),
        e.memset(SS[0][:], 0.0), e.memset(SS[1][:], 0.0)] +
        ([e.memset(dbg2[:], 0.0), e.memset(dbg3[:], 0.0)] if dbg else []),
        tag="kcol")
    op_zeros = P.add("vector", lambda e: [e.memset(E[0:4, 5376:6400], 0.0)],
                     tag="scan_zeros")

    op_red, op_sq = {}, {}
    for tt in range(NT):
        par = tt % 2
        xt = E[:, par * 1024:(par + 1) * 1024]
        deps = [op_red[tt - 2], op_sq[tt - 2]] if tt >= 2 else []
        xd = P.add("gpsimd", (lambda xt=xt, tt=tt: lambda e: [
            e.dma_start(out=xt, in_=xb[tt * 128:(tt + 1) * 128, :])])(),
            deps=deps, dma=1, tag=f"x1_{tt}")
        op_red[tt] = P.add("vector", (lambda xt=xt, tt=tt: lambda e: [
            e.tensor_reduce(out=statR[:, tt:tt + 1], in_=xt, axis=AX.X,
                            op=ALU.add)])(),
            deps=[xd], tag=f"red{tt}")
        op_sq[tt] = P.add("scalar", (lambda xt=xt, tt=tt: lambda e: [
            e.activation(out=dummy_bf[:, 0:1024], in_=xt, func=AF.Square,
                         accum_out=statQ[:, tt:tt + 1])])(),
            deps=[xd, op_sq.get(tt - 1)], tag=f"sq{tt}")

    cap_a = P.add("vector", lambda e: [
        e.tensor_copy(out=dbg3[:, 0:256], in_=E[:, 0:256]),
        e.tensor_copy(out=dbg3[:, 256:256 + NT], in_=statR[:, 0:NT]),
        e.tensor_copy(out=dbg3[:, 256 + NT:256 + 2 * NT], in_=statQ[:, 0:NT]),
    ], deps=[op_red[NT - 1], op_sq[NT - 1], op_kcol],
        tag="cap_a") if dbg else None

    op_a2v = P.add("vector", [
        lambda e: e.tensor_scalar(out=statM[:], in0=statR[:], scalar1=1.0 / C,
                                  scalar2=None, op0=ALU.mult),
        lambda e: e.tensor_scalar(out=statQ[:], in0=statQ[:], scalar1=1.0 / C,
                                  scalar2=None, op0=ALU.mult),
        lambda e: e.tensor_mul(out=statR[:], in0=statM[:], in1=statM[:]),
        lambda e: e.tensor_sub(out=statQ[:], in0=statQ[:], in1=statR[:]),
    ], deps=[op_red[NT - 1], op_sq[NT - 1], cap_a], tag="a2v", serial=True)
    op_a2s = P.add("scalar", [
        lambda e: e.activation(out=statQ[:], in_=statQ[:], func=AF.Ln,
                               bias=kcol[:, 0:1]),
        lambda e: e.activation(out=statS[:], in_=statQ[:], func=AF.Exp,
                               scale=-0.5),
    ], deps=[op_a2v, op_kcol], tag="a2s", serial=True)
    op_a2b = P.add("vector", [
        lambda e: e.tensor_scalar(out=statM[:], in0=statM[:], scalar1=-1.0,
                                  scalar2=None, op0=ALU.mult),
        lambda e: e.tensor_mul(out=statB[:], in0=statM[:], in1=statS[:]),
    ] + ([
        lambda e: e.tensor_copy(out=dbg3[:, 256 + 2 * NT:256 + 3 * NT],
                                in_=statS[:, 0:NT]),
        lambda e: e.tensor_copy(out=dbg3[:, 256 + 3 * NT:256 + 4 * NT],
                                in_=statQ[:, 0:NT]),
    ] if dbg else []), deps=[op_a2s], tag="a2b", serial=True)

    NCH = T // 512
    op_norm, op_tr, op_qv, op_qvev = {}, {}, {}, {}
    op_kg, op_kev, op_gev, op_lgev = {}, {}, {}, {}
    op_scan = {}
    for c5 in range(NCH):
        buf = c5 % 2
        xnb = xn_fm[buf]
        for tl4 in range(4):
            tt = c5 * 4 + tl4
            par = tt % 2
            xt = E[:, par * 1024:(par + 1) * 1024]
            dd = [op_norm[tt - 2]] if tt >= 2 else [op_a2b]
            xd = P.add("sync", (lambda xt=xt, tt=tt: lambda e: [
                e.dma_start(out=xt, in_=xb[tt * 128:(tt + 1) * 128, :])])(),
                deps=dd, dma=1, tag=f"x2_{tt}")
            ndeps = [xd, op_a2b, op_tr.get(tt - 2)]
            op_norm[tt] = P.add("vector", (lambda xt=xt, tt=tt, par=par:
                                           lambda e: [
                e.tensor_scalar(out=xn_tok[:, par * 1024:(par + 1) * 1024],
                                in0=xt, scalar1=statS[:, tt:tt + 1],
                                scalar2=statB[:, tt:tt + 1],
                                op0=ALU.mult, op1=ALU.add)])(),
                deps=ndeps, tag=f"norm{tt}")
            trdeps = [op_norm[tt]]
            if c5 >= 2:
                trdeps.append(op_kg[(c5 - 2, "g")])
            op_tr[tt] = P.add("sync" if tt % 2 == 0 else "scalar",
                              (lambda tt=tt, tl4=tl4, par=par, xnb=xnb:
                               lambda e: [
                e.dma_start(out=xnb[:].rearrange(
                                "p (c t) -> p c t",
                                c=8)[:, :, tl4 * 128:(tl4 + 1) * 128],
                            in_=xn_tok[:, par * 1024:(par + 1) * 1024]
                            .rearrange("p (c q) -> p c q", c=8),
                            transpose=True)])(),
                deps=trdeps, dma=1, tag=f"tr{tt}")
        trd = [op_tr[c5 * 4 + t2] for t2 in range(4)]
        if c5 == 0 and dbg:
            cap_b = P.add("vector", (lambda: lambda e: [
                e.tensor_copy(out=dbg3[:, 640:896], in_=xn_tok[:, 0:256]),
                e.tensor_copy(out=dbg3[:, 896:1152], in_=xn_fm[0][:, 0:256]),
            ])(), deps=trd + [op_norm[0]], tag="cap_b")
            trd = trd + [cap_b]
        for tl4 in range(4):
            tt = c5 * 4 + tl4
            pq = psU[tl4 % 2]
            deps = trd + [op_wload, op_qvev.get(tt - 2)]
            op_qv[tt] = P.add("tensor", (lambda tt=tt, tl4=tl4, pq=pq,
                                         xnb=xnb: lambda e: [
                e.matmul(out=pq[:], lhsT=xnb[:, cc * 512 + tl4 * 128:
                                            cc * 512 + (tl4 + 1) * 128],
                         rhs=Wqv[:, cc * 512:(cc + 1) * 512],
                         start=(cc == 0), stop=(cc == 7))
                for cc in range(8)])(),
                deps=deps, tag=f"qv{tt}")
            op_qvev[tt] = P.add("vector", (lambda tt=tt, pq=pq: lambda e: [
                e.tensor_copy(out=q_tok[:, tt * 256:(tt + 1) * 256],
                              in_=pq[:, 0:256]),
                e.tensor_copy(
                    out=V2[:, tt * 260:(tt + 1) * 260].rearrange(
                        "p (h x) -> p h x", h=NHC)[:, :, 0:64],
                    in_=pq[:, 256:512].rearrange("p (h d) -> p h d", h=NHC)),
            ])(), deps=[op_qv[tt], op_v2ones], tag=f"qvev{tt}")
        for pair in range(2):
            kwar = op_kev.get((c5 - 1, pair))
            op_kg[(c5, pair)] = P.add("tensor", (lambda c5=c5, pair=pair,
                                                 xnb=xnb: lambda e: [
                e.matmul(out=pk[:, pair * 512:(pair + 1) * 512],
                         lhsT=Wkg[:, cc * 260 + pair * 128:
                                  cc * 260 + (pair + 1) * 128],
                         rhs=xnb[:, cc * 512:(cc + 1) * 512],
                         start=(cc == 0), stop=(cc == 7))
                for cc in range(8)])(),
                deps=trd + [op_wload, kwar], tag=f"k{c5}_{pair}")
            op_kev[(c5, pair)] = P.add("vector", (lambda c5=c5, pair=pair:
                                                  lambda e: [
                e.tensor_copy(out=Kfm[pair][:, c5 * 512:(c5 + 1) * 512],
                              in_=pk[:, pair * 512:(pair + 1) * 512])])(),
                deps=[op_kg[(c5, pair)]], tag=f"kev{c5}_{pair}")
        gwar = op_gev.get(c5 - 1) if c5 % 2 else op_lgev.get(c5 // 2 - 1)
        op_kg[(c5, "g")] = P.add("tensor", (lambda c5=c5, xnb=xnb: lambda e: [
            e.matmul(out=psU[2][0:4, 0:512],
                     lhsT=Wkg[:, cc * 260 + 256:cc * 260 + 260],
                     rhs=xnb[:, cc * 512:(cc + 1) * 512],
                     start=(cc == 0), stop=(cc == 7))
            for cc in range(8)])(),
            deps=trd + [op_wload, gwar, op_gev.get(c5 - 1)], tag=f"g{c5}")
        op_gev[c5] = P.add("vector", (lambda c5=c5: lambda e: [
            e.tensor_copy(out=E[0:4, 2048 + (c5 % 2) * 512:
                                2560 + (c5 % 2) * 512],
                          in_=psU[2][0:4, 0:512])])(),
            deps=[op_kg[(c5, "g")], (op_lgev.get(c5 // 2 - 1)
                                     if c5 % 2 == 0 else None)],
            tag=f"gev{c5}")

        if c5 % 2 == 0:
            continue
        ch = c5 // 2
        graw = E[0:4, 2048:3072]
        tscr = E[0:4, 3072:4096]
        lgr = E[0:4, 4096:5120]
        if ch == 0 and dbg:
            cap_g = P.add("vector", (lambda graw=graw: lambda e: [
                e.tensor_copy(out=dbg3[0:4, 1152:1408], in_=graw[:, 0:256])
            ])(), deps=[op_gev[c5]], tag="cap_g")
        elif ch == 0:
            cap_g = None
        op_sp1 = P.add("vector", (lambda: lambda e: [
            e.tensor_scalar(out=lgr, in0=graw, scalar1=0.0, scalar2=None,
                            op0=ALU.min),
        ])(), deps=[op_gev[c5], op_gev[c5 - 1]], tag=f"sp1_{ch}")
        op_sp2 = P.add("scalar", [
            (lambda tscr=tscr, graw=graw: lambda e:
             e.activation(out=tscr, in_=graw, func=AF.Abs))(),
            (lambda tscr=tscr: lambda e:
             e.activation(out=tscr, in_=tscr, func=AF.Exp, scale=-1.0))(),
            (lambda tscr=tscr: lambda e:
             e.activation(out=tscr, in_=tscr, func=AF.Ln,
                          bias=kcol[0:4, 1:2]))(),
        ], deps=[op_gev[c5], op_gev[c5 - 1], op_sp1, op_kcol],
            tag=f"sp2_{ch}", serial=True)
        if ch == 0 and dbg:
            cap_g2 = P.add("vector", (lambda lgr=lgr, tscr=tscr: lambda e: [
                e.tensor_copy(out=dbg3[0:4, 1408:1472], in_=lgr[:, 0:64]),
                e.tensor_copy(out=dbg3[0:4, 1472:1536], in_=tscr[:, 0:64]),
            ])(), deps=[op_sp2, op_sp1], tag="cap_g2")
        elif ch == 0:
            cap_g2 = None
        carry = E[0:4, 5120 + ch:5121 + ch]
        pcarry = E[0:4, 5119 + ch:5120 + ch]
        op_scan[ch] = P.add("vector", (
            [(lambda lgr=lgr, tscr=tscr: lambda e:
              e.tensor_sub(out=lgr, in0=lgr, in1=tscr))(),
             (lambda graw=graw, lgr=lgr: lambda e:
              e.tensor_tensor_scan(out=graw, data0=E[0:4, 5376:6400],
                                   data1=lgr, initial=0.0,
                                   op0=ALU.add, op1=ALU.add))()] +
            ([(lambda graw=graw, pcarry=pcarry: lambda e:
               e.tensor_scalar(out=graw, in0=graw, scalar1=pcarry,
                               scalar2=None, op0=ALU.add))()] if ch > 0
             else []) +
            [(lambda carry=carry, graw=graw: lambda e:
              e.tensor_copy(out=carry, in_=graw[:, 1023:1024]))()]
        ), deps=[op_sp2, op_zeros, (cap_g if ch == 0 else None),
                 (cap_g2 if ch == 0 else None)],
            tag=f"scan{ch}", serial=True)
        op_lgtr = P.add("tensor", (lambda graw=graw: lambda e: [
            e.transpose(out=psA[0][:, tl2 * 4:tl2 * 4 + 4],
                        in_=graw[:, tl2 * 128:(tl2 + 1) * 128],
                        identity=s_idf[0:4, 0:4])
            for tl2 in range(8)])(),
            deps=[op_scan[ch], op_gev[c5], op_lgev.get(ch - 1)],
            tag=f"lgtr{ch}")
        op_lgev[ch] = P.add("vector", (lambda ch=ch: lambda e: [
            e.tensor_copy(out=LGcol[:, ch * 32:(ch + 1) * 32],
                          in_=psA[0][:, 0:32])])(),
            deps=[op_lgtr], tag=f"lgev{ch}")

    lg_done = op_lgev[NTCH - 1]
    a_done_v = lg_done
    a_done_t = op_kg[(NTCH - 1, "g")]

    op_cx = P.add("sync", lambda e: (
        [e.dma_start(out=crow[0:1, h * NT:(h + 1) * NT],
                     in_=LGcol[64:65, h::4][:, 0:NT],
                     allow_slow_non_contiguous=True) for h in range(NHC)] +
        [e.dma_start(out=Chalf[0:1, h * NSUP:(h + 1) * NSUP],
                     in_=LGcol[0:1, 4 + h::8][:, 0:NSUP],
                     allow_slow_non_contiguous=True) for h in range(NHC)]
    ), deps=[lg_done], dma=8, tag="cx")
    crv = crow[0:1, :].rearrange("a (h i) -> a h i", h=NHC)
    Erv = Erow[0:1, :].rearrange("a (h i) -> a h i", h=NHC)
    Chv = Chalf[0:1, :].rearrange("a (h i) -> a h i", h=NHC)
    op_erow = P.add("vector", [
        lambda e: e.tensor_sub(out=Erv[:, :, 0::2], in0=crv[:, :, 0::2],
                               in1=Chv[:]),
        lambda e: e.tensor_sub(out=Erv[:, :, 1::2], in0=crv[:, :, 1::2],
                               in1=Chv[:]),
        lambda e: e.tensor_scalar(out=Chalf[:], in0=Chalf[:], scalar1=0.5,
                                  scalar2=None, op0=ALU.mult),
    ], deps=[op_cx], tag="erow", serial=True)
    op_frow = P.add("scalar", lambda e: [
        e.activation(out=Frow[:], in_=Erow[:], func=AF.Exp, scale=0.5),
    ], deps=[op_erow], tag="frow")


    Frv = Frow[0:1, :].rearrange("a (h i) -> a h i", h=NHC)
    op_fbc = P.add("tensor", lambda e: [
        e.matmul(out=psU[0][:, 0:4 * NT], lhsT=s_ones[:], rhs=Frow[0:1, :],
                 start=True, stop=True),
        e.matmul(out=psU[1][:, 0:4 * NSUP], lhsT=s_ones[:], rhs=Chalf[0:1, :],
                 start=True, stop=True),
    ], deps=[op_frow, op_qvev[NT - 1], op_lgev[NTCH - 1], op_erow],
        tag="fbc")
    op_fbcev = P.add("vector", lambda e: [
        e.tensor_copy(out=Fbc[:], in_=psU[0][:, 0:4 * NT]),
        e.tensor_copy(out=Cbc[:], in_=psU[1][:, 0:4 * NSUP]),
    ], deps=[op_fbc], tag="fbcev")

    op_S, op_evA, op_evD, op_mask, op_PV = {}, {}, {}, {}, {}
    op_yts, op_ydma = {}, {}
    op_ssI, op_qsc, op_qtr = {}, {}, {}
    prev_yts_last = None
    prev_ydma = None
    prev_uev = None
    last_evD = {}

    def emit_prep(I):
        pi = I % 2
        jmax = 2 * I + 2
        op_ssI[I] = P.add("scalar", (lambda I=I, jmax=jmax, pi=pi: lambda e: [
            e.activation(out=SS[pi][:, h * NT:h * NT + jmax],
                         in_=LGcol[:, h::4][:, 0:jmax],
                         func=AF.Exp, scale=-0.5,
                         bias=Cbc[:, h * NSUP + I:h * NSUP + I + 1])
            for h in range(NHC)])(),
            deps=[op_fbcev, op_ssI.get(I - 2),
                  op_evA.get((I - 2, 2 * I - 3)) if I >= 2 else None,
                  last_evD.get(I - 2)],
            tag=f"ss{I}")
        op_qsc[I] = P.add("vector", (lambda I=I, pi=pi: lambda e: [
            e.tensor_scalar(
                out=qt_sc[pi][:, (h // 2) * 256 + sub * 128 + (h % 2) * 64:
                              (h // 2) * 256 + sub * 128 + (h % 2) * 64 + 64],
                in0=q_tok[:, (2 * I + sub) * 256 + h * 64:
                          (2 * I + sub) * 256 + (h + 1) * 64],
                scalar1=Fbc[:, h * NT + 2 * I + sub:h * NT + 2 * I + sub + 1],
                scalar2=None, op0=ALU.mult)
            for sub in range(2) for h in range(NHC)])(),
            deps=[op_fbcev, op_qtr.get(I - 2)], tag=f"qsc{I}")
        op_qtr[I] = P.add("scalar", (lambda I=I, pi=pi: lambda e: [
            e.dma_start(out=qfm[pi][p][:].rearrange("p (s t) -> p s t", s=2),
                        in_=qt_sc[pi][:, p * 256:(p + 1) * 256].rearrange(
                            "p (s q) -> p s q", s=2),
                        transpose=True)
            for p in range(2)])(),
            deps=[op_qsc[I],
                  op_S.get((I - 2, 2 * I - 3)) if I >= 2 else None],
            dma=2, tag=f"qtr{I}")

    def mk_S(I, j):
        jmax = 2 * I + 2
        par = j % 2
        if par == 0:
            bA = lambda hq: psA[0][:, hq * 256:(hq + 1) * 256]
            bB = lambda hq: psA[1][:, hq * 256:(hq + 1) * 256]
        else:
            bA = lambda hq: pk[:, hq * 256:(hq + 1) * 256]
            bB = lambda hq: pk[:, 512 + hq * 256:512 + (hq + 1) * 256]
        sdeps = [op_qtr[I], op_evA.get((I, j - 2)), op_evD.get((I, j - 2))]
        if j < 2 and I > 0:
            sdeps += [op_evA.get((I - 1, 2 * I - 2 + j)),
                      op_evD.get((I - 1, 2 * I - 2 + j))]
            if j == 1:
                sdeps.append(op_yts.get((I - 1, 7)))
        if I == 0 and j == 0:
            sdeps += [op_qvev[NT - 1], op_lgev[NTCH - 1]]
        pi = I % 2
        op_S[(I, j)] = P.add("tensor", (lambda I=I, j=j, bA=bA, bB=bB, pi=pi:
                                        lambda e: [
            m for p in range(2) for m in (
                e.matmul(out=bA(p),
                         lhsT=Kfm[p][0:64, j * 128:(j + 1) * 128],
                         rhs=qfm[pi][p][0:64, :], start=True, stop=True,
                         tile_position=(0, 0)),
                e.matmul(out=bB(p),
                         lhsT=Kfm[p][64:128, j * 128:(j + 1) * 128],
                         rhs=qfm[pi][p][64:128, :], start=True, stop=True,
                         tile_position=(64, 0)),
            )])(), deps=sdeps, tag=f"S{I}_{j}")
        return bA, bB

    sbanks = {}
    pair_order = []
    emit_prep(0)
    emit_prep(1)
    for I in range(NSUP):
        jmax = 2 * I + 2
        pi = I % 2
        if (I, 0) not in sbanks:
            sbanks[(I, 0)] = mk_S(I, 0)
        for j in range(jmax):
            par = len(pair_order) % 4
            pair_order.append((I, j))
            if j + 1 < jmax:
                sbanks[(I, j + 1)] = mk_S(I, j + 1)
            bA, bB = sbanks[(I, j)]
            if j == 1 and I + 1 < NSUP and (I + 1) not in op_ssI:
                emit_prep(I + 1)
            evdeps = [op_S[(I, j)]]
            if len(pair_order) >= 5:
                evdeps.append(op_PV[pair_order[-5]])
            so2 = 128 if j == 2 * I + 1 else 0
            op_evA[(I, j)] = P.add("scalar", (lambda I=I, j=j, par=par,
                                              so2=so2, bA=bA, pi=pi:
                                              lambda e: [
                e.activation(out=wt[:, par * 1024 + h * 256 + so2:
                                   par * 1024 + (h + 1) * 256],
                             in_=bA(h // 2)[:, so2:],
                             func=AF.Square,
                             scale=SS[pi][:, h * NT + j:h * NT + j + 1])
                for h in (0, 2)])(),
                deps=evdeps + [op_ssI[I]], tag=f"evA{I}_{j}")
            op_evDa = P.add("vector", (lambda I=I, j=j, par=par,
                                       so2=so2, bB=bB, pi=pi: lambda e: [
                e.tensor_scalar(out=wt[:, par * 1024 + h * 256 + so2:
                                      par * 1024 + (h + 1) * 256],
                                in0=bB(h // 2)[:, so2:],
                                scalar1=SS[pi][:, h * NT + j:h * NT + j + 1],
                                scalar2=None, op0=ALU.mult)
                for h in (1, 3)])(),
                deps=evdeps + [op_ssI[I]], tag=f"evDa{I}_{j}")
            op_evDb = P.add("vector", (lambda par=par, so2=so2: lambda e: [
                e.tensor_mul(out=wt[:, par * 1024 + 3 * 256 + so2:
                                   par * 1024 + 4 * 256],
                             in0=wt[:, par * 1024 + 3 * 256 + so2:
                                   par * 1024 + 4 * 256],
                             in1=wt[:, par * 1024 + 3 * 256 + so2:
                                   par * 1024 + 4 * 256])])(),
                deps=[], tag=f"evDb{I}_{j}")
            op_evDp = P.add("gpsimd", (lambda par=par, so2=so2: lambda e: [
                e.tensor_mul(out=wt[:, par * 1024 + 1 * 256 + so2:
                                   par * 1024 + 2 * 256],
                             in0=wt[:, par * 1024 + 1 * 256 + so2:
                                   par * 1024 + 2 * 256],
                             in1=wt[:, par * 1024 + 1 * 256 + so2:
                                   par * 1024 + 2 * 256])])(),
                deps=[op_evDa], tag=f"evDp{I}_{j}")
            op_evD[(I, j)] = op_evDa
            last_evD[I] = op_evDa
            pvdeps = [op_evA[(I, j)], op_evDb, op_evDp]
            if j == 2 * I:
                op_mask[(I, j)] = P.add("gpsimd", (lambda par=par: lambda e: [
                    e.tensor_mul(out=wt[:, par * 1024 + h * 256:
                                       par * 1024 + h * 256 + 128],
                                 in0=wt[:, par * 1024 + h * 256:
                                       par * 1024 + h * 256 + 128],
                                 in1=s_triu[:])
                    for h in range(NHC)])(),
                    deps=[op_evA[(I, j)], op_evDb, op_evDp],
                    tag=f"mk{I}_{j}")
                pvdeps.append(op_mask[(I, j)])
            elif j == 2 * I + 1:
                op_mask[(I, j)] = P.add("gpsimd", (lambda par=par: lambda e: [
                    x for h in range(NHC) for x in (
                        e.memset(wt[:, par * 1024 + h * 256:
                                   par * 1024 + h * 256 + 128], 0.0),
                        e.tensor_mul(out=wt[:, par * 1024 + h * 256 + 128:
                                           par * 1024 + (h + 1) * 256],
                                     in0=wt[:, par * 1024 + h * 256 + 128:
                                           par * 1024 + (h + 1) * 256],
                                     in1=s_triu[:]),
                    )])(),
                    deps=[op_evA[(I, j)], op_evDb, op_evDp],
                    tag=f"mk{I}_{j}")
                pvdeps.append(op_mask[(I, j)])
            if j == 0:
                pvdeps.append(prev_uev)
                pvdeps.append(op_fbcev)
                if I == 0:
                    pvdeps.append(op_lgev[NTCH - 1])
            op_PV[(I, j)] = P.add("tensor", (lambda I=I, j=j, par=par,
                                             jmax=jmax: lambda e: [
                e.matmul(out=psU[h][0:65, 0:256],
                         lhsT=V2[:, j * 260 + h * 65:j * 260 + (h + 1) * 65],
                         rhs=wt[:, par * 1024 + h * 256:
                               par * 1024 + (h + 1) * 256],
                         start=(j == 0), stop=(j == jmax - 1))
                for h in range(NHC)])(),
                deps=pvdeps, tag=f"PV{I}_{j}")

        op_uev = P.add("vector", (lambda: lambda e: [
            e.tensor_copy(out=Uev[0:65, h * 256:(h + 1) * 256],
                          in_=psU[h][0:65, 0:256])
            for h in range(NHC)])(),
            deps=[op_PV[(I, jmax - 1)]], tag=f"uev{I}")
        prev_uev = op_uev
        if I + 2 < NSUP and (I + 2) not in op_ssI:
            emit_prep(I + 2)
        if I + 1 < NSUP:
            sbanks[(I + 1, 0)] = mk_S(I + 1, 0)
        op_ut = {}
        for kidx in range(8):
            h = kidx // 2
            sub = kidx % 2
            ping = kidx % 2
            utdeps = [op_uev, op_yts.get((I, kidx - 2)),
                      (prev_yts_last if I > 0 and kidx < 2 else None),
                      op_evA.get((I, jmax - 1)), op_evD.get((I, jmax - 1))]
            op_ut[kidx] = P.add("tensor", (lambda h=h, sub=sub,
                                           ping=ping: lambda e: [
                e.transpose(out=pk[:, ping * 512:ping * 512 + 65],
                            in_=Uev[0:65, h * 256 + sub * 128:
                                    h * 256 + (sub + 1) * 128],
                            identity=s_idf[0:65, 0:65])])(),
                deps=utdeps, tag=f"ut{I}_{kidx}")
            ytsdeps = [op_ut[kidx], prev_ydma]
            op_yts[(I, kidx)] = P.add("vector", [
                (lambda ping=ping: lambda e:
                 e.reciprocal(out=rncol[:, ping:ping + 1],
                              in_=pk[:, ping * 512 + 64:ping * 512 + 65]))(),
                (lambda h=h, sub=sub, ping=ping: lambda e:
                 e.tensor_scalar(out=ybuf[:, sub * 256 + h * 64:
                                         sub * 256 + (h + 1) * 64],
                                 in0=pk[:, ping * 512:ping * 512 + 64],
                                 scalar1=rncol[:, ping:ping + 1],
                                 scalar2=None, op0=ALU.mult))(),
            ], deps=ytsdeps, tag=f"yts{I}_{kidx}", serial=True)
        prev_yts_last = op_yts[(I, 7)]
        op_ydma[I] = P.add("sync", (lambda I=I: lambda e: [
            e.dma_start(out=cc_in_h.ap()[256 * I:256 * (I + 1), :]
                        .rearrange("(i p) o -> p i o", p=128),
                        in_=ybuf[:].rearrange("p (i o) -> p i o", i=2))])(),
            deps=[op_yts[(I, k)] for k in range(8)], dma=1, tag=f"ydma{I}")
        prev_ydma = op_ydma[I]

    op_cc = P.add("gpsimd", (lambda: lambda e: [
        e.collective_compute(
            "AllGather", ALU.bypass,
            replica_groups=[[0, 1, 2, 3], [4, 5, 6, 7]],
            ins=[cc_in_h.ap().opt()],
            outs=[cc_out_h.ap().opt()])])(),
        deps=[op_ydma[NSUP - 1]], tag="cc")

    op_cap = prev_yts_last
    if dbg:
        op_cap = P.add("vector", (lambda: lambda e: [
        e.tensor_copy(out=dbg2[:, 0:NT], in_=statS[:, 0:NT]),
        e.tensor_copy(out=dbg2[:, NT:2 * NT], in_=statB[:, 0:NT]),
        e.tensor_copy(out=dbg2[:, 2 * NT:2 * NT + 256],
                      in_=q_tok[:, 0:256]),
        e.tensor_copy(out=dbg2[:, 2 * NT + 256:2 * NT + 512],
                      in_=Kfm[0][:, 0:256]),
        e.tensor_copy(out=dbg2[:, 2 * NT + 512:2 * NT + 772],
                      in_=V2[:, 0:260]),
        e.tensor_copy(out=dbg2[:, 2 * NT + 772:2 * NT + 772 + NHC * NT],
                      in_=SS[0][:]),
            e.tensor_copy(out=dbg2[:, 2 * NT + 772 + NHC * NT:
                                  2 * NT + 772 + NHC * NT + 512],
                          in_=wt[:, 0:512]),
        ])(), deps=[prev_yts_last, op_ssI[NSUP - 1]], tag="cap")

    b_last = {
        "tensor": op_ut[7], "vector": op_cap,
        "scalar": op_evA[(NSUP - 1, 2 * NSUP - 1)],
        "sync": op_ydma[NSUP - 1], "gpsimd": op_cc,
    }
    ab.close()

    cst = ExitStack()

    def sbc(name, shape, dt):
        return cst.enter_context(nc.sbuf_tensor(name, shape, dt))

    Wr_sb = sbc("Wr_sb", [128, 8 * 1024], BF16)
    Wk_sb = sbc("Wk_sb", [128, 8 * 1024], BF16)
    Wv_sb = sbc("Wv_sb", [128, 8 * 1024], BF16)
    xn2fm = sbc("xn2fm", [128, 8 * CHALF], BF16)
    prevcol = sbc("prevcol", [128, 8], BF16)
    xrfm = sbc("xrfm", [128, 8 * CHALF], BF16)
    xkfm = sbc("xkfm", [128, 8 * CHALF], BF16)
    rslab = sbc("rslab", [128, TPH * 1024], BF16)
    kact = sbc("kact", [128, 8 * CHALF], BF16)
    y_sb = sbc("y_sb", [128, TPH * 1024], BF16)
    pcR = cst.enter_context(nc.psum_tensor("pcR", [128, 1024], F32))
    pcK = cst.enter_context(nc.psum_tensor("pcK", [128, 1024], F32))
    pbnd = cst.enter_context(nc.psum_tensor("pbnd", [128, 8], F32))

    blast = list(b_last.values())
    op_wffn = P.add("gpsimd", lambda e: [
        e.dma_start(out=Wr_sb[:].rearrange("p (cc o) -> p cc o", cc=8),
                    in_=wr_d.rearrange("(cc p) o -> p cc o", p=128)),
        e.dma_start(out=Wk_sb[:].rearrange("p (cc o) -> p cc o", cc=8),
                    in_=wk_d.rearrange("(cc p) o -> p cc o", p=128)),
        e.dma_start(out=Wv_sb[:].rearrange("p (cc o) -> p cc o", cc=8),
                    in_=wv_d.rearrange("(cc p) o -> p cc o", p=128)),
    ], deps=blast, dma=3, tag="wffn")

    xn2v = xn2fm[:].rearrange("p (cc t) -> p cc t", cc=8)
    prev_fin, prev_out, prev_tr2_last, prev_o2ev = None, None, None, None
    prev_norm2 = {}
    prev_mix_last = None
    for hf in range(NHALF):
        ysdeps = [op_cc, b_last["sync"], prev_fin]

        def mk_ysb(hf=hf):
            def f(e, sem):
                nd = 4 + (4 if hf == 0 else 0)
                for arm in e.switch_core_id(n=128):
                    q = arm.logical % 4
                    t0g = q * SLICE + hf * CHALF
                    for r in range(4):
                        ins = e.dma_start(
                            out=y_sb[:].rearrange(
                                "p (i x) -> p i x",
                                i=TPH)[:, :, r * 256:(r + 1) * 256],
                            in_=cc_out_h.ap()[r * T + t0g:r * T + t0g + CHALF,
                                              :].rearrange(
                                "(i p) o -> p i o", p=128))
                        ins.then_inc(sem, 16)
                    if hf == 0:
                        for r in range(4):
                            if q == 0:
                                ins = e.dma_start(out=yscr[:],
                                                  in_=cc_out_h.ap()[r * T:
                                                                    r * T + 1, :])
                            else:
                                ins = e.dma_start(
                                    out=ybnd[0:1, r * 256:(r + 1) * 256],
                                    in_=cc_out_h.ap()[r * T + t0g - 1:
                                                      r * T + t0g, :])
                            ins.then_inc(sem, 16)
                return 16 * nd
            return f
        op_ysb = P.add("sync", mk_ysb(), deps=ysdeps + [op_ybnd0],
                       manual=16 * (8 if hf == 0 else 4), tag=f"ysb{hf}")

        if hf == 0:
            op_x1b = P.add("vector", lambda e: [
                e.tensor_add(out=x1b[:], in0=xb0s[:], in1=ybnd[:])],
                deps=[op_ysb, op_const, blast[1]], tag="x1b")
            op_bred = P.add("vector", lambda e: [
                e.tensor_reduce(out=bnds[0:1, 0:1], in_=x1b[:], axis=AX.X,
                                op=ALU.add)], deps=[op_x1b], tag="bred")
            op_bsq = P.add("scalar", lambda e: [
                e.activation(out=dummy_bf[0:1, 0:1024], in_=x1b[:], func=AF.Square,
                             accum_out=bnds[0:1, 1:2])],
                deps=[op_x1b, b_last["scalar"]], tag="bsq")
            op_bst = P.add("vector", [
                lambda e: e.tensor_scalar(out=bnds[0:1, 2:3],
                                          in0=bnds[0:1, 0:1],
                                          scalar1=1.0 / C, scalar2=None,
                                          op0=ALU.mult),
                lambda e: e.tensor_scalar(out=bnds[0:1, 3:4],
                                          in0=bnds[0:1, 1:2],
                                          scalar1=1.0 / C, scalar2=None,
                                          op0=ALU.mult),
                lambda e: e.tensor_mul(out=bnds[0:1, 4:5], in0=bnds[0:1, 2:3],
                                       in1=bnds[0:1, 2:3]),
                lambda e: e.tensor_sub(out=bnds[0:1, 3:4], in0=bnds[0:1, 3:4],
                                       in1=bnds[0:1, 4:5]),
            ], deps=[op_bred, op_bsq], tag="bst", serial=True)
            op_bs2 = P.add("scalar", [
                lambda e: e.activation(out=bnds[0:1, 3:4], in_=bnds[0:1, 3:4],
                                       func=AF.Ln, bias=kcol[0:1, 0:1]),
                lambda e: e.activation(out=bnds[0:1, 5:6], in_=bnds[0:1, 3:4],
                                       func=AF.Exp, scale=-0.5),
            ], deps=[op_bst], tag="bs2", serial=True)
            op_bs3 = P.add("vector", [
                lambda e: e.tensor_scalar(out=bnds[0:1, 6:7],
                                          in0=bnds[0:1, 2:3], scalar1=-1.0,
                                          scalar2=None, op0=ALU.mult),
                lambda e: e.tensor_mul(out=bnds[0:1, 7:8], in0=bnds[0:1, 6:7],
                                       in1=bnds[0:1, 5:6]),
            ], deps=[op_bs2], tag="bs3", serial=True)
            op_bnrm = P.add("vector", lambda e: [
                e.tensor_scalar(out=xn2b[:], in0=x1b[:],
                                scalar1=bnds[0:1, 5:6],
                                scalar2=bnds[0:1, 7:8],
                                op0=ALU.mult, op1=ALU.add)],
                deps=[op_bs3], tag="bnrm")
            op_btr = P.add("tensor", lambda e: [
                e.transpose(out=pbnd[:, cc:cc + 1],
                            in_=xn2b[0:1, cc * 128:(cc + 1) * 128],
                            identity=s_idf[0:1, 0:1])
                for cc in range(8)],
                deps=[op_bnrm, b_last["tensor"]], tag="btr")
            op_bev = P.add("vector", lambda e: [
                e.tensor_copy(out=prevcol[:], in_=pbnd[:, 0:8])],
                deps=[op_btr], tag="bev")
        else:
            op_bev = P.add("vector", lambda e: [
                e.tensor_copy(out=prevcol[:].rearrange("p (c x) -> p c x", x=1),
                              in_=xn2v[:, :, CHALF - 1:CHALF])],
                deps=[prev_tr2_last, prev_o2ev], tag=f"bcp{hf}")

        op_x1t, op_n2, op_tr2 = {}, {}, {}
        op_r2d, op_q2d = {}, {}
        for tl in range(TPH):
            gt = hf * TPH + tl
            par = tl % 2
            xt = E[:, par * 1024:(par + 1) * 1024]
            xd = P.add("sync", (lambda xt=xt, gt=gt: lambda e: [
                e.dma_start(out=xt, in_=xsl[gt * 128:(gt + 1) * 128, :])])(),
                deps=[op_x1t.get(tl - 2) if tl >= 2 else
                      (prev_fin if hf else b_last["scalar"]),
                      (b_last["vector"] if hf == 0 and tl < 2 else None)],
                dma=1, tag=f"cx{hf}_{tl}")
            op_x1t[tl] = P.add("vector", (lambda xt=xt, tl=tl: lambda e: [
                e.tensor_add(out=y_sb[:, tl * 1024:(tl + 1) * 1024], in0=xt,
                             in1=y_sb[:, tl * 1024:(tl + 1) * 1024])])(),
                deps=[xd, op_ysb,
                      (b_last["vector"] if hf == 0 and tl < 2 else None)],
                tag=f"x1t{hf}_{tl}")
            op_r2d[tl] = P.add("vector", (lambda tl=tl: lambda e: [
                e.tensor_reduce(out=statR[:, tl:tl + 1],
                                in_=y_sb[:, tl * 1024:(tl + 1) * 1024],
                                axis=AX.X, op=ALU.add)])(),
                deps=[op_x1t[tl]], tag=f"r2{hf}_{tl}")
            op_q2d[tl] = P.add("scalar", (lambda tl=tl: lambda e: [
                e.activation(out=rslab[:, 0:1024],
                             in_=y_sb[:, tl * 1024:(tl + 1) * 1024],
                             func=AF.Square,
                             accum_out=statQ[:, tl:tl + 1])])(),
                deps=[op_x1t[tl],
                      (op_bsq if hf == 0 and tl == 0 else None)],
                tag=f"q2{hf}_{tl}")
        op_s2 = P.add("vector", [
            lambda e: e.tensor_scalar(out=statM[:, 0:TPH], in0=statR[:, 0:TPH],
                                      scalar1=1.0 / C, scalar2=None,
                                      op0=ALU.mult),
            lambda e: e.tensor_scalar(out=statQ[:, 0:TPH], in0=statQ[:, 0:TPH],
                                      scalar1=1.0 / C, scalar2=None,
                                      op0=ALU.mult),
            lambda e: e.tensor_mul(out=statR[:, 0:TPH], in0=statM[:, 0:TPH],
                                   in1=statM[:, 0:TPH]),
            lambda e: e.tensor_sub(out=statQ[:, 0:TPH], in0=statQ[:, 0:TPH],
                                   in1=statR[:, 0:TPH]),
        ], deps=[op_r2d[TPH - 1], op_q2d[TPH - 1]],
            tag=f"s2{hf}", serial=True)
        op_s2s = P.add("scalar", [
            lambda e: e.activation(out=statQ[:, 0:TPH], in_=statQ[:, 0:TPH],
                                   func=AF.Ln, bias=kcol[:, 0:1]),
            lambda e: e.activation(out=statS[:, 0:TPH], in_=statQ[:, 0:TPH],
                                   func=AF.Exp, scale=-0.5),
        ], deps=[op_s2], tag=f"s2s{hf}", serial=True)
        op_s2b = P.add("vector", [
            lambda e: e.tensor_scalar(out=statM[:, 0:TPH], in0=statM[:, 0:TPH],
                                      scalar1=-1.0, scalar2=None,
                                      op0=ALU.mult),
            lambda e: e.tensor_mul(out=statB[:, 0:TPH], in0=statM[:, 0:TPH],
                                   in1=statS[:, 0:TPH]),
        ], deps=[op_s2s], tag=f"s2b{hf}", serial=True)
        for tl in range(TPH):
            par = tl % 2
            op_n2[tl] = P.add("vector", (lambda tl=tl, par=par: lambda e: [
                e.tensor_scalar(out=dummy_bf[:, par * 1024:(par + 1) * 1024],
                                in0=y_sb[:, tl * 1024:(tl + 1) * 1024],
                                scalar1=statS[:, tl:tl + 1],
                                scalar2=statB[:, tl:tl + 1],
                                op0=ALU.mult, op1=ALU.add)])(),
                deps=[op_s2b, op_tr2.get(tl - 2)], tag=f"n2{hf}_{tl}")
            trdeps = [op_n2[tl]]
            if tl == 0:
                trdeps.append(prev_mix_last)
            if tl == TPH - 1:
                trdeps.append(op_bev)
            op_tr2[tl] = P.add("sync", (lambda tl=tl, par=par: lambda e: [
                e.dma_start(out=xn2fm[:].rearrange(
                                "p (c t) -> p c t",
                                c=8)[:, :, tl * 128:(tl + 1) * 128],
                            in_=dummy_bf[:, par * 1024:(par + 1) * 1024]
                            .rearrange("p (c q) -> p c q", c=8),
                            transpose=True)])(),
                deps=trdeps, dma=1, tag=f"tr2{hf}_{tl}")
        prev_norm2 = op_n2
        prev_tr2_last = op_tr2[TPH - 1]

        op_mix = {}
        for cc in range(8):
            op_mix[cc] = P.add("vector", (lambda cc=cc: lambda e: [
                e.tensor_sub(out=xrfm[:, cc * CHALF:cc * CHALF + 1],
                             in0=xn2fm[:, cc * CHALF:cc * CHALF + 1],
                             in1=prevcol[:, cc:cc + 1]),
                e.scalar_tensor_tensor(
                    out=xkfm[:, cc * CHALF:cc * CHALF + 1],
                    in0=xrfm[:, cc * CHALF:cc * CHALF + 1],
                    scalar=s_muk[:, cc:cc + 1],
                    in1=prevcol[:, cc:cc + 1], op0=ALU.mult, op1=ALU.add),
                e.scalar_tensor_tensor(
                    out=xrfm[:, cc * CHALF:cc * CHALF + 1],
                    in0=xrfm[:, cc * CHALF:cc * CHALF + 1],
                    scalar=s_mur[:, cc:cc + 1],
                    in1=prevcol[:, cc:cc + 1], op0=ALU.mult, op1=ALU.add),
                e.tensor_sub(out=xrfm[:, cc * CHALF + 1:(cc + 1) * CHALF],
                             in0=xn2fm[:, cc * CHALF + 1:(cc + 1) * CHALF],
                             in1=xn2fm[:, cc * CHALF:(cc + 1) * CHALF - 1]),
                e.scalar_tensor_tensor(
                    out=xkfm[:, cc * CHALF + 1:(cc + 1) * CHALF],
                    in0=xrfm[:, cc * CHALF + 1:(cc + 1) * CHALF],
                    scalar=s_muk[:, cc:cc + 1],
                    in1=xn2fm[:, cc * CHALF:(cc + 1) * CHALF - 1],
                    op0=ALU.mult, op1=ALU.add),
                e.scalar_tensor_tensor(
                    out=xrfm[:, cc * CHALF + 1:(cc + 1) * CHALF],
                    in0=xrfm[:, cc * CHALF + 1:(cc + 1) * CHALF],
                    scalar=s_mur[:, cc:cc + 1],
                    in1=xn2fm[:, cc * CHALF:(cc + 1) * CHALF - 1],
                    op0=ALU.mult, op1=ALU.add),
            ])(), deps=[op_tr2[tl2] for tl2 in range(TPH)] + [op_bev,
                        op_const, prev_o2ev], tag=f"mix{hf}_{cc}")
        mixall = [op_mix[cc] for cc in range(8)]
        prev_mix_last = op_mix[7]

        op_rp, op_rev = {}, {}
        for tl in range(TPH):
            pr = pcR if tl % 2 == 0 else pcK
            op_rp[tl] = P.add("tensor", (lambda tl=tl, pr=pr: lambda e: [
                m for cc in range(8) for m in (
                    e.matmul(out=pr[:, 0:512],
                             lhsT=xrfm[:, cc * CHALF + tl * 128:
                                      cc * CHALF + (tl + 1) * 128],
                             rhs=Wr_sb[:, cc * 1024:cc * 1024 + 512],
                             start=(cc == 0), stop=(cc == 7)),
                    e.matmul(out=pr[:, 512:1024],
                             lhsT=xrfm[:, cc * CHALF + tl * 128:
                                      cc * CHALF + (tl + 1) * 128],
                             rhs=Wr_sb[:, cc * 1024 + 512:(cc + 1) * 1024],
                             start=(cc == 0), stop=(cc == 7)),
                )])(), deps=mixall + [op_wffn, op_rev.get(tl - 2),
                                      (b_last["tensor"] if hf == 0 and tl < 2
                                       else None),
                                      (op_btr if hf == 0 else None)],
                tag=f"rp{hf}_{tl}")
            op_rev[tl] = P.add("scalar", (lambda tl=tl, pr=pr: lambda e: [
                e.activation(out=rslab[:, tl * 1024:(tl + 1) * 1024],
                             in_=pr[:], func=AF.Sigmoid)])(),
                deps=[op_rp[tl]], tag=f"rev{hf}_{tl}")

        op_kp, op_ksq = {}, {}
        for dc in range(8):
            pr = pcR if (TPH + dc) % 2 == 0 else pcK
            nspl = max(1, CHALF // 512)
            wspl = CHALF // nspl
            op_kp[dc] = P.add("tensor", (lambda dc=dc, pr=pr, nspl=nspl,
                                         wspl=wspl: lambda e: [
                e.matmul(out=pr[:, sp * wspl:(sp + 1) * wspl],
                         lhsT=Wk_sb[:, cc * 1024 + dc * 128:
                                   cc * 1024 + (dc + 1) * 128],
                         rhs=xkfm[:, cc * CHALF + sp * wspl:
                                 cc * CHALF + (sp + 1) * wspl],
                         start=(cc == 0), stop=(cc == 7))
                for cc in range(8) for sp in range(nspl)])(),
                deps=mixall + [op_wffn,
                                      op_ksq.get(dc - 2) or op_rev.get(TPH - 2 + dc),
                                      op_rev.get(TPH - 1) if dc == 1 else None],
                tag=f"kp{hf}_{dc}")
            op_krl = P.add("scalar", (lambda dc=dc, pr=pr: lambda e: [
                e.activation(out=kact[:, dc * CHALF:(dc + 1) * CHALF],
                             in_=pr[:, 0:CHALF], func=AF.Relu)])(),
                deps=[op_kp[dc]], tag=f"krl{hf}_{dc}")
            op_ksq[dc] = P.add("vector", (lambda dc=dc: lambda e: [
                e.tensor_mul(out=kact[:, dc * CHALF:(dc + 1) * CHALF],
                             in0=kact[:, dc * CHALF:(dc + 1) * CHALF],
                             in1=kact[:, dc * CHALF:(dc + 1) * CHALF])])(),
                deps=[op_krl], tag=f"ksq{hf}_{dc}")

        fins = {}
        po_half = prev_fouts if hf else {}
        fouts = {}
        for tl in range(TPH):
            pr = pcR if (TPH + 8 + tl) % 2 == 0 else pcK
            o2deps = [op_ksq[dc] for dc in range(8)] + [op_wffn]
            o2deps.append(prev_o2ev if tl < 2 else None)
            o2deps.append(op_krl if tl == 0 else None)
            o2deps.append(fins.get(tl - 2))
            op_o2 = P.add("tensor", (lambda tl=tl, pr=pr: lambda e: [
                m for dc in range(8) for m in (
                    e.matmul(out=pr[:, 0:512],
                             lhsT=kact[:, dc * CHALF + tl * 128:
                                      dc * CHALF + (tl + 1) * 128],
                             rhs=Wv_sb[:, dc * 1024:dc * 1024 + 512],
                             start=(dc == 0), stop=(dc == 7)),
                    e.matmul(out=pr[:, 512:1024],
                             lhsT=kact[:, dc * CHALF + tl * 128:
                                      dc * CHALF + (tl + 1) * 128],
                             rhs=Wv_sb[:, dc * 1024 + 512:(dc + 1) * 1024],
                             start=(dc == 0), stop=(dc == 7)),
                )])(), deps=o2deps, tag=f"o2{hf}_{tl}")
            fpar = tl % 2
            fb = E[:, 2048 + fpar * 1024:3072 + fpar * 1024]
            op_fin = P.add("vector", [
                (lambda tl=tl, pr=pr, fb=fb: lambda e: e.tensor_mul(
                    out=fb,
                    in0=rslab[:, tl * 1024:(tl + 1) * 1024], in1=pr[:]))(),
                (lambda tl=tl, fb=fb: lambda e: e.tensor_add(
                    out=fb, in0=fb,
                    in1=y_sb[:, tl * 1024:(tl + 1) * 1024]))(),
            ], deps=[op_o2, op_rev[tl], fouts.get(tl - 2),
                     (po_half.get(TPH - 2 + tl) if tl < 2 else None)],
                tag=f"fin{hf}_{tl}", serial=True)
            fins[tl] = op_fin
            prev_o2ev = op_fin
            gt = hf * TPH + tl
            prev_out = P.add("sync", (lambda gt=gt, fb=fb: lambda e: [
                e.dma_start(out=out[gt * 128:(gt + 1) * 128, :],
                            in_=fb)])(),
                deps=[op_fin], dma=1, tag=f"odma{hf}_{tl}")
            fouts[tl] = prev_out
            prev_fin = op_fin
        prev_fouts = fouts

    if dbg:
        cap_c = P.add("vector", lambda e: [
            e.tensor_copy(out=dbg2[:, 1400:1528], in_=xn2fm[:, 0:128]),
            e.tensor_copy(out=dbg2[:, 1528:1656], in_=xrfm[:, 0:128]),
            e.tensor_copy(out=dbg2[:, 1656:1784], in_=rslab[:, 0:128]),
            e.tensor_copy(out=dbg2[:, 1784:1912], in_=kact[:, 0:128]),
            e.tensor_copy(out=dbg2[:, 1912:2040], in_=y_sb[:, 0:128]),
            e.tensor_copy(out=dbg2[:, 2040:2042], in_=statS[:, 0:2]),
            e.tensor_copy(out=dbg2[:, 2042:2044], in_=statB[:, 0:2]),
            e.tensor_copy(out=dbg2[:, 2044:2046], in_=statM[:, 0:2]),
            e.tensor_copy(out=dbg2[:, 2046:2048], in_=statQ[:, 0:2]),
        ], deps=[prev_mix_last, op_rev[TPH - 1], op_ksq[7], prev_fin],
            tag="cap_c")
        NDW = NT * 4 + 12
        dbg_sb = cst.enter_context(nc.sbuf_tensor("dbg_sb", [128, NDW], F32))
        op_dbg = P.add("vector", lambda e: [
            e.tensor_copy(out=dbg_sb[:, 0:NT * 4], in_=LGcol[:]),
            e.tensor_copy(out=dbg_sb[:, NT * 4:NT * 4 + 12],
                          in_=LGcol[:, 0:12]),
        ], deps=[prev_yts_last, op_cc, prev_fin, cap_c], tag="dbgv")
        op_dbgd = P.add("sync", lambda e: [
            e.dma_start(out=dbgl, in_=dbg_sb[:]),
            e.dma_start(out=dbgy, in_=cc_in_h.ap()),
            e.dma_start(out=dbg2o, in_=dbg2[:]),
            e.dma_start(out=dbg3o, in_=dbg3[:]),
        ], deps=[op_dbg, op_cc], dma=4, tag="dbgd")
    else:
        op_dbgd = None
    P.add("gpsimd", lambda e: [e.engine_nop()],
          deps=[prev_out, prev_fouts.get(TPH - 2), op_dbgd], tag="done")

    P.emit(nc, blk, sems)
    cst.close()
    per.close()
    return nc


def kernel(L0_attn_x, L0_attn_W, L0_ffn_W_r, L0_ffn_W_k, L0_ffn_W_v,
           L0_ffn_mu_r, L0_ffn_mu_k, _T=None):
    x = np.ascontiguousarray(L0_attn_x, dtype=np.float32)
    W = np.asarray(L0_attn_W, dtype=np.float32)
    B, T, _ = x.shape
    if _T is not None:
        T = _T
        x = x[:, :T]
    SLICE = T // 4

    nc = build(T)

    identb = np.eye(128, dtype=BF)
    identf = np.eye(128, dtype=np.float32)
    triu = np.triu(np.ones((128, 128), dtype=np.float32)).astype(BF)
    onesrow = np.ones((1, 128), dtype=np.float32)
    mur = np.ascontiguousarray(
        np.asarray(L0_ffn_mu_r, np.float32).reshape(8, 128).T)
    muk = np.ascontiguousarray(
        np.asarray(L0_ffn_mu_k, np.float32).reshape(8, 128).T)
    wr = np.ascontiguousarray(np.asarray(L0_ffn_W_r, np.float32).T)
    wk2 = np.ascontiguousarray(np.asarray(L0_ffn_W_k, np.float32).T)
    wv2 = np.ascontiguousarray(np.asarray(L0_ffn_W_v, np.float32).T)

    in_maps = []
    for c in range(8):
        b = c // 4
        hc = c % 4
        q = c % 4
        Wq = W[256 * hc:256 * (hc + 1), :]
        Wv = W[2048 + 256 * hc:2048 + 256 * (hc + 1), :]
        Wk = W[1024 + 256 * hc:1024 + 256 * (hc + 1), :]
        Wg = W[3072 + 4 * hc:3072 + 4 * (hc + 1), :]
        wqv = np.ascontiguousarray(
            np.concatenate([Wq.T, Wv.T], axis=1))
        wkg = np.ascontiguousarray(
            np.concatenate([Wk.T, Wg.T], axis=1))
        t0 = q * SLICE
        xb0 = (np.zeros((1, C), np.float32) if t0 == 0
               else np.ascontiguousarray(x[b, t0 - 1:t0]))
        in_maps.append({
            "xb": np.ascontiguousarray(x[b]),
            "xsl": np.ascontiguousarray(x[b, t0:t0 + SLICE]),
            "xb0": xb0,
            "wqv": wqv, "wkg": wkg,
            "wr": wr, "wk2": wk2, "wv2": wv2,
            "mur": mur, "muk": muk,
            "identb": identb, "identf": identf, "triu": triu,
            "onesrow": onesrow,
        })

    res = run_bass_kernel_spmd(nc, in_maps, core_ids=list(range(8)))
    outp = np.empty((B, T, C), dtype=np.float32)
    for c in range(8):
        b = c // 4
        q = c % 4
        outp[b, q * SLICE:(q + 1) * SLICE] = res.results[c]["out"]
    return outp


if __name__ == "__main__":
    build(1024)
    print("build ok")


# version 14
# speedup vs baseline: 1.1651x; 1.1651x over previous
import numpy as np
import ml_dtypes
import concourse.bass as bass
import concourse.mybir as mybir
from concourse.bass_utils import run_bass_kernel_spmd
from contextlib import ExitStack

F32 = mybir.dt.float32
BF16 = mybir.dt.bfloat16
AF = mybir.ActivationFunctionType
ALU = mybir.AluOpType
AX = mybir.AxisListType
BF = ml_dtypes.bfloat16

C = 1024
NHC = 4
LNEPS = 1e-5
ENGS = ("sync", "gpsimd", "scalar", "vector", "tensor")


class Op:
    __slots__ = ("eng", "fn", "deps", "dma", "manual", "seq", "needed", "tag", "slot", "serial")

    def __init__(self, eng, fn, deps, dma, manual, tag, serial=False):
        self.eng, self.fn, self.deps = eng, fn, list(deps)
        self.dma, self.manual, self.tag = dma, manual, tag
        self.serial = serial
        self.seq = 0
        self.needed = False


class Plan:
    def __init__(self):
        self.ops = []

    def add(self, eng, fn, deps=(), dma=0, manual=0, tag="", serial=False):
        op = Op(eng, fn, [d for d in deps if d is not None], dma, manual, tag,
                serial)
        self.ops.append(op)
        return op

    NSLOT = 8

    def emit(self, nc, blk, sems):
        for op in self.ops:
            for d in op.deps:
                d.needed = True
        dslot = {e: 0 for e in ENGS}
        cnt = {}
        for op in self.ops:
            if op.dma or op.manual:
                op.slot = dslot[op.eng]
                dslot[op.eng] = (dslot[op.eng] + 1) % Plan.NSLOT
                k = (op.eng, 'd', op.slot)
                cnt[k] = cnt.get(k, 0) + (op.manual if op.manual
                                          else 16 * op.dma)
                op.seq = cnt[k]
            else:
                op.slot = None
                k = (op.eng, 'c')
                if op.serial:
                    cnt[k] = cnt.get(k, 0) + len(op.fn)
                elif op.needed:
                    cnt[k] = cnt.get(k, 0) + 1
                op.seq = cnt.get(k, 0)

        def body(eng_name):
            def f(eng):
                last_wait = {}
                for op in self.ops:
                    if op.eng != eng_name:
                        continue
                    for d in op.deps:
                        if d.seq == 0 or not (d.needed or d.manual):
                            continue
                        dk = ((d.eng, 'd', d.slot) if d.slot is not None
                              else (d.eng, 'c'))
                        if d.seq > last_wait.get(dk, 0):
                            eng.wait_ge(sems[dk], d.seq)
                            last_wait[dk] = d.seq
                    if op.manual:
                        n = op.fn(eng, sems[(eng_name, 'd', op.slot)])
                        assert n == op.manual, (op.tag, n, op.manual)
                    elif op.serial:
                        base = op.seq - len(op.fn)
                        semc = sems[(eng_name, 'c')]
                        for i2, f2 in enumerate(op.fn):
                            ins = f2(eng)
                            ins.then_inc(semc, 1)
                            if i2 < len(op.fn) - 1:
                                eng.wait_ge(semc, base + i2 + 1)
                        last_wait[(eng_name, 'c')] = op.seq - 1
                    else:
                        instrs = op.fn(eng)
                        if op.needed:
                            if op.dma:
                                assert len(instrs) == op.dma, (
                                    op.tag, len(instrs), op.dma)
                                for ins in instrs:
                                    ins.then_inc(
                                        sems[(eng_name, 'd', op.slot)], 16)
                            else:
                                instrs[-1].then_inc(sems[(eng_name, 'c')], 1)
            return f

        blk.sync(body("sync"))
        blk.gpsimd(body("gpsimd"))
        blk.scalar(body("scalar"))
        blk.vector(body("vector"))
        blk.tensor(body("tensor"))


def build(T=8192, dbg=None):
    if dbg is None:
        dbg = T < 8192
    NT = T // 128
    NSUP = T // 256
    NTCH = T // 1024
    SLICE = T // 4
    CHALF = min(1024, SLICE)
    NHALF = SLICE // CHALF
    TPH = CHALF // 128

    nc = bass.Bass("TRN2", target_bir_lowering=False,
                   detect_race_conditions=False)

    def din(name, shape, dt):
        return nc.dram_tensor(name, shape, dt, kind="ExternalInput").ap()
    xb = din("xb", [T, C], F32)
    xsl = din("xsl", [SLICE, C], F32)
    xb0 = din("xb0", [1, C], F32)
    wqv = din("wqv", [C, 512], F32)
    wkg = din("wkg", [C, 260], F32)
    wr_d = din("wr", [C, C], F32)
    wk_d = din("wk2", [C, C], F32)
    wv_d = din("wv2", [C, C], F32)
    mur = din("mur", [128, 8], F32)
    muk = din("muk", [128, 8], F32)
    identb = din("identb", [128, 128], BF16)
    identf = din("identf", [128, 128], F32)
    triu_d = din("triu", [128, 128], BF16)
    onesrow = din("onesrow", [1, 128], F32)
    out = nc.dram_tensor("out", [SLICE, C], F32, kind="ExternalOutput").ap()
    if dbg:
        dbgy = nc.dram_tensor("dbgy", [T, 256], BF16,
                              kind="ExternalOutput").ap()
        dbgl = nc.dram_tensor("dbgl", [128, NT * 4 + 12],
                              F32, kind="ExternalOutput").ap()
        dbg2o = nc.dram_tensor("dbg2o", [128, 2048], F32,
                               kind="ExternalOutput").ap()
        dbg3o = nc.dram_tensor("dbg3o", [128, 1536], F32,
                               kind="ExternalOutput").ap()

    cc_in_h = nc.dram_tensor("cc_in", [T, 256], BF16, kind="Internal")
    cc_out_h = nc.dram_tensor("cc_out", [4 * T, 256], BF16, kind="Internal")

    per = ExitStack()
    sems = {}
    for e in ENGS:
        sems[(e, 'c')] = per.enter_context(nc.semaphore("s_" + e + "_c"))
        for sl in range(Plan.NSLOT):
            sems[(e, 'd', sl)] = per.enter_context(
                nc.semaphore(f"s_{e}_d{sl}"))

    def sb(name, shape, dt):
        return per.enter_context(nc.sbuf_tensor(name, shape, dt))

    s_idb = sb("s_idb", [128, 128], BF16)
    s_idf = sb("s_idf", [128, 128], F32)
    s_triu = sb("s_triu", [128, 128], BF16)
    s_ones = sb("s_ones", [1, 128], F32)
    s_mur = sb("s_mur", [128, 8], F32)
    s_muk = sb("s_muk", [128, 8], F32)
    E = sb("E", [128, 6400], F32)
    xA = sb("xA", [128, 3072], F32)
    LGcol = sb("LGcol", [128, NT * 4], F32)
    statR = sb("statR", [128, 8], F32)
    statQ = sb("statQ", [128, 8], F32)
    statM = sb("statM", [128, 8], F32)
    statS = sb("statS", [128, NT], F32)
    statB = sb("statB", [128, NT], F32)
    crow = sb("crow", [1, 4 * NT], F32)
    Chalf = sb("Chalf", [1, 4 * NSUP], F32)
    Erow = sb("Erow", [1, 4 * NT], F32)
    Frow = sb("Frow", [1, 4 * NT], F32)
    fcol = sb("fcol", [128, 12], F32)
    ybnd = sb("ybnd", [1, C], BF16)
    yscr = sb("yscr", [1, 256], BF16)
    xb0s = sb("xb0s", [1, C], F32)
    x1b = sb("x1b", [1, C], F32)
    xn2b = sb("xn2b", [1, C], F32)
    bnds = sb("bnds", [1, 8], F32)
    kcol = sb("kcol", [128, 2], F32)
    dbg2 = sb("dbg2", [128, 2048], F32) if dbg else None
    dbg3 = sb("dbg3", [128, 1536], F32) if dbg else None

    blk = per.enter_context(nc.Block())
    P = Plan()

    ab = ExitStack()

    def sba(name, shape, dt):
        return ab.enter_context(nc.sbuf_tensor(name, shape, dt))

    Wqv = sba("Wqv", [128, 8 * 512], BF16)
    Wkg = sba("Wkg", [128, 8 * 260], BF16)
    q_tok = sba("q_tok", [128, NT * 256], BF16)
    Kfm = [sba("Kfm0", [128, T], BF16), sba("Kfm1", [128, T], BF16)]
    V2 = sba("V2", [128, NT * 260], BF16)
    xn_tok = sba("xn_tok", [128, 2 * 1024], BF16)
    xn_fm = [sba("xn_fm0", [128, 8 * 512], BF16),
             sba("xn_fm1", [128, 8 * 512], BF16)]
    SS = [sba("SS0", [128, NHC * NT], F32),
          sba("SS1", [128, NHC * NT], F32)]
    Fbc = sba("Fbc", [128, 4 * NT], F32)
    Cbc = sba("Cbc", [128, 4 * NSUP], F32)
    wt = sba("wt", [128, 4 * NHC * 256], BF16)
    qt_sc = [sba("qt_sc0", [128, 512], BF16),
             sba("qt_sc1", [128, 512], BF16)]
    qfm = [[sba("qfm00", [128, 256], BF16), sba("qfm01", [128, 256], BF16)],
           [sba("qfm10", [128, 256], BF16), sba("qfm11", [128, 256], BF16)]]
    Uev = sba("Uev", [65, 1024], F32)
    rncol = sba("rncol", [128, 2], F32)
    ybuf = sba("ybuf", [128, 512], BF16)

    psA = [ab.enter_context(nc.psum_tensor(f"psA{i}", [128, 512], F32))
           for i in range(2)]
    psU = [ab.enter_context(nc.psum_tensor(f"psU{i}", [128, 512], F32))
           for i in range(4)]
    pk = ab.enter_context(nc.psum_tensor("pk", [128, 1024], F32))

    v2v = V2[:].rearrange("p (j h x) -> p j h x", j=NT, h=NHC)

    op_const = P.add("sync", lambda e: [
        e.dma_start(out=s_idb[:], in_=identb),
        e.dma_start(out=s_idf[:], in_=identf),
        e.dma_start(out=s_triu[:], in_=triu_d),
        e.dma_start(out=s_ones[:], in_=onesrow),
        e.dma_start(out=s_mur[:], in_=mur),
        e.dma_start(out=s_muk[:], in_=muk),
        e.dma_start(out=xb0s[:], in_=xb0),
    ], dma=7, tag="consts")
    op_wload = P.add("gpsimd", lambda e: [
        e.dma_start(out=Wqv[:].rearrange("p (cc o) -> p cc o", cc=8),
                    in_=wqv.rearrange("(cc p) o -> p cc o", p=128)),
        e.dma_start(out=Wkg[:].rearrange("p (cc o) -> p cc o", cc=8),
                    in_=wkg.rearrange("(cc p) o -> p cc o", p=128)),
    ], dma=2, tag="wload")
    op_v2ones = P.add("gpsimd", lambda e: [
        e.memset(v2v[:, :, :, 64:65], 1.0)], tag="v2ones")
    op_ybnd0 = P.add("gpsimd", lambda e: [e.memset(ybnd[:], 0.0)], tag="ybnd0")
    op_kcol = P.add("gpsimd", lambda e: [
        e.memset(kcol[:, 0:1], LNEPS), e.memset(kcol[:, 1:2], 1.0),
        e.memset(SS[0][:], 0.0), e.memset(SS[1][:], 0.0)] +
        ([e.memset(dbg2[:], 0.0), e.memset(dbg3[:], 0.0)] if dbg else []),
        tag="kcol")

    NCH = T // 512
    op_norm, op_tr, op_qv, op_qvev = {}, {}, {}, {}
    op_kg, op_kev, op_gev, op_lgev = {}, {}, {}, {}
    op_scan = {}
    op_red, op_sq, op_a2b = {}, {}, {}
    cap_a = None
    op_zeros = P.add("vector", lambda e: [e.memset(E[0:4, 5376:6400], 0.0)],
                     tag="scan_zeros")
    xq = [E[:, 0:1024], E[:, 1024:2048], xA[:, 0:1024], xA[:, 1024:2048],
          xA[:, 2048:3072]]
    for c5 in range(NCH):
        buf = c5 % 2
        xnb = xn_fm[buf]
        for tl4 in range(4):
            tt = c5 * 4 + tl4
            xt = xq[tt % 5]
            dd = ([op_norm[tt - 5], op_sq[tt - 5]] if tt >= 5 else [])
            xd = P.add("gpsimd" if tt % 2 == 0 else "sync",
                       (lambda xt=xt, tt=tt: lambda e: [
                e.dma_start(out=xt, in_=xb[tt * 128:(tt + 1) * 128, :])])(),
                deps=dd, dma=1, tag=f"x_{tt}")
            op_red[tt] = P.add("vector", (lambda xt=xt, tt=tt: lambda e: [
                e.tensor_reduce(out=statR[:, tt % 8:tt % 8 + 1], in_=xt,
                                axis=AX.X,
                                op=ALU.add)])(), deps=[xd], tag=f"red{tt}")
            op_sq[tt] = P.add("scalar", (lambda xt=xt, tt=tt: lambda e: [
                e.activation(out=wt[:, 0:1024], in_=xt, func=AF.Square,
                             accum_out=statQ[:, tt % 8:tt % 8 + 1])])(),
                deps=[xd, op_sq.get(tt - 1)], tag=f"sq{tt}")
        t0c, t1c = c5 * 4, c5 * 4 + 4
        q0, q1 = (c5 % 2) * 4, (c5 % 2) * 4 + 4
        if dbg and c5 == 0:
            cap_a = P.add("vector", lambda e: [
                e.tensor_copy(out=dbg3[:, 0:256], in_=E[:, 0:256]),
                e.tensor_copy(out=dbg3[:, 256:260], in_=statR[:, 0:4]),
                e.tensor_copy(out=dbg3[:, 256 + NT:256 + NT + 4],
                              in_=statQ[:, 0:4]),
            ], deps=[op_red[3], op_sq[3], op_kcol], tag="cap_a")
        op_a2v = P.add("vector", [
            (lambda q0=q0, q1=q1: lambda e: e.tensor_scalar(
                out=statM[:, q0:q1], in0=statR[:, q0:q1],
                scalar1=1.0 / C, scalar2=None, op0=ALU.mult))(),
            (lambda q0=q0, q1=q1: lambda e: e.tensor_scalar(
                out=statQ[:, q0:q1], in0=statQ[:, q0:q1],
                scalar1=1.0 / C, scalar2=None, op0=ALU.mult))(),
            (lambda q0=q0, q1=q1: lambda e: e.tensor_mul(
                out=statR[:, q0:q1], in0=statM[:, q0:q1],
                in1=statM[:, q0:q1]))(),
            (lambda q0=q0, q1=q1: lambda e: e.tensor_sub(
                out=statQ[:, q0:q1], in0=statQ[:, q0:q1],
                in1=statR[:, q0:q1]))(),
        ], deps=[op_red[t1c - 1], op_sq[t1c - 1], op_a2b.get(c5 - 2),
                 (cap_a if c5 == 0 else None)],
            tag=f"a2v{c5}", serial=True)
        op_a2s = P.add("scalar", [
            (lambda q0=q0, q1=q1: lambda e: e.activation(
                out=statQ[:, q0:q1], in_=statQ[:, q0:q1],
                func=AF.Ln, bias=kcol[:, 0:1]))(),
            (lambda t0c=t0c, t1c=t1c, q0=q0, q1=q1: lambda e: e.activation(
                out=statS[:, t0c:t1c], in_=statQ[:, q0:q1],
                func=AF.Exp, scale=-0.5))(),
        ], deps=[op_a2v, op_kcol], tag=f"a2s{c5}", serial=True)
        op_a2b[c5] = P.add("vector", [
            (lambda q0=q0, q1=q1: lambda e: e.tensor_scalar(
                out=statM[:, q0:q1], in0=statM[:, q0:q1],
                scalar1=-1.0, scalar2=None, op0=ALU.mult))(),
            (lambda t0c=t0c, t1c=t1c, q0=q0, q1=q1: lambda e: e.tensor_mul(
                out=statB[:, t0c:t1c], in0=statM[:, q0:q1],
                in1=statS[:, t0c:t1c]))(),
        ], deps=[op_a2s], tag=f"a2b{c5}", serial=True)
        for tl4 in range(4):
            tt = c5 * 4 + tl4
            par = tt % 2
            op_norm[tt] = P.add("vector", (lambda tl4=tl4, tt=tt, par=par:
                                           lambda e: [
                e.tensor_scalar(out=xn_tok[:, par * 1024:(par + 1) * 1024],
                                in0=xq[tt % 5], scalar1=statS[:, tt:tt + 1],
                                scalar2=statB[:, tt:tt + 1],
                                op0=ALU.mult, op1=ALU.add)])(),
                deps=[op_a2b[c5], op_tr.get(tt - 2)], tag=f"norm{tt}")
            trdeps = [op_norm[tt]]
            if c5 >= 2:
                trdeps.append(op_kg[(c5 - 2, "g")])
            op_tr[tt] = P.add("sync" if tt % 2 == 0 else "scalar",
                              (lambda tt=tt, tl4=tl4, par=par, xnb=xnb:
                               lambda e: [
                e.dma_start(out=xnb[:].rearrange(
                                "p (c t) -> p c t",
                                c=8)[:, :, tl4 * 128:(tl4 + 1) * 128],
                            in_=xn_tok[:, par * 1024:(par + 1) * 1024]
                            .rearrange("p (c q) -> p c q", c=8),
                            transpose=True)])(),
                deps=trdeps, dma=1, tag=f"tr{tt}")
        trd = [op_tr[c5 * 4 + t2] for t2 in range(4)]
        if c5 == 0 and dbg:
            cap_b = P.add("vector", (lambda: lambda e: [
                e.tensor_copy(out=dbg3[:, 640:896], in_=xn_tok[:, 0:256]),
                e.tensor_copy(out=dbg3[:, 896:1152], in_=xn_fm[0][:, 0:256]),
            ])(), deps=trd + [op_norm[0]], tag="cap_b")
            trd = trd + [cap_b]
        for tl4 in range(4):
            tt = c5 * 4 + tl4
            pq = psU[tl4 % 2]
            deps = [op_tr[tt], op_wload, op_qvev.get(tt - 2)] + (
                [cap_b] if c5 == 0 and dbg else [])
            op_qv[tt] = P.add("tensor", (lambda tt=tt, tl4=tl4, pq=pq,
                                         xnb=xnb: lambda e: [
                e.matmul(out=pq[:], lhsT=xnb[:, cc * 512 + tl4 * 128:
                                            cc * 512 + (tl4 + 1) * 128],
                         rhs=Wqv[:, cc * 512:(cc + 1) * 512],
                         start=(cc == 0), stop=(cc == 7))
                for cc in range(8)])(),
                deps=deps, tag=f"qv{tt}")
            op_qvev[tt] = P.add("vector", (lambda tt=tt, pq=pq: lambda e: [
                e.tensor_copy(out=q_tok[:, tt * 256:(tt + 1) * 256],
                              in_=pq[:, 0:256]),
                e.tensor_copy(
                    out=V2[:, tt * 260:(tt + 1) * 260].rearrange(
                        "p (h x) -> p h x", h=NHC)[:, :, 0:64],
                    in_=pq[:, 256:512].rearrange("p (h d) -> p h d", h=NHC)),
            ])(), deps=[op_qv[tt], op_v2ones], tag=f"qvev{tt}")
        for pair in range(2):
            kwar = op_kev.get((c5 - 1, pair))
            op_kg[(c5, pair)] = P.add("tensor", (lambda c5=c5, pair=pair,
                                                 xnb=xnb: lambda e: [
                e.matmul(out=pk[:, pair * 512:(pair + 1) * 512],
                         lhsT=Wkg[:, cc * 260 + pair * 128:
                                  cc * 260 + (pair + 1) * 128],
                         rhs=xnb[:, cc * 512:(cc + 1) * 512],
                         start=(cc == 0), stop=(cc == 7))
                for cc in range(8)])(),
                deps=trd + [op_wload, kwar], tag=f"k{c5}_{pair}")
            op_kev[(c5, pair)] = P.add("vector", (lambda c5=c5, pair=pair:
                                                  lambda e: [
                e.tensor_copy(out=Kfm[pair][:, c5 * 512:(c5 + 1) * 512],
                              in_=pk[:, pair * 512:(pair + 1) * 512])])(),
                deps=[op_kg[(c5, pair)]], tag=f"kev{c5}_{pair}")
        gwar = op_gev.get(c5 - 1) if c5 % 2 else op_lgev.get(c5 // 2 - 1)
        op_kg[(c5, "g")] = P.add("tensor", (lambda c5=c5, xnb=xnb: lambda e: [
            e.matmul(out=psU[2][0:4, 0:512],
                     lhsT=Wkg[:, cc * 260 + 256:cc * 260 + 260],
                     rhs=xnb[:, cc * 512:(cc + 1) * 512],
                     start=(cc == 0), stop=(cc == 7))
            for cc in range(8)])(),
            deps=trd + [op_wload, gwar, op_gev.get(c5 - 1)], tag=f"g{c5}")
        op_gev[c5] = P.add("vector", (lambda c5=c5: lambda e: [
            e.tensor_copy(out=E[0:4, 2048 + (c5 % 2) * 512:
                                2560 + (c5 % 2) * 512],
                          in_=psU[2][0:4, 0:512])])(),
            deps=[op_kg[(c5, "g")], (op_lgev.get(c5 // 2 - 1)
                                     if c5 % 2 == 0 else None)],
            tag=f"gev{c5}")

        if c5 % 2 == 0:
            continue
        ch = c5 // 2
        graw = E[0:4, 2048:3072]
        tscr = E[0:4, 3072:4096]
        lgr = E[0:4, 4096:5120]
        if ch == 0 and dbg:
            cap_g = P.add("vector", (lambda graw=graw: lambda e: [
                e.tensor_copy(out=dbg3[0:4, 1152:1408], in_=graw[:, 0:256])
            ])(), deps=[op_gev[c5]], tag="cap_g")
        elif ch == 0:
            cap_g = None
        op_sp1 = P.add("vector", (lambda: lambda e: [
            e.tensor_scalar(out=lgr, in0=graw, scalar1=0.0, scalar2=None,
                            op0=ALU.min),
        ])(), deps=[op_gev[c5], op_gev[c5 - 1]], tag=f"sp1_{ch}")
        op_sp2 = P.add("scalar", [
            (lambda tscr=tscr, graw=graw: lambda e:
             e.activation(out=tscr, in_=graw, func=AF.Abs))(),
            (lambda tscr=tscr: lambda e:
             e.activation(out=tscr, in_=tscr, func=AF.Exp, scale=-1.0))(),
            (lambda tscr=tscr: lambda e:
             e.activation(out=tscr, in_=tscr, func=AF.Ln,
                          bias=kcol[0:4, 1:2]))(),
        ], deps=[op_gev[c5], op_gev[c5 - 1], op_sp1, op_kcol],
            tag=f"sp2_{ch}", serial=True)
        if ch == 0 and dbg:
            cap_g2 = P.add("vector", (lambda lgr=lgr, tscr=tscr: lambda e: [
                e.tensor_copy(out=dbg3[0:4, 1408:1472], in_=lgr[:, 0:64]),
                e.tensor_copy(out=dbg3[0:4, 1472:1536], in_=tscr[:, 0:64]),
            ])(), deps=[op_sp2, op_sp1], tag="cap_g2")
        elif ch == 0:
            cap_g2 = None
        carry = E[0:4, 5120 + ch:5121 + ch]
        pcarry = E[0:4, 5119 + ch:5120 + ch]
        op_scan[ch] = P.add("vector", (
            [(lambda lgr=lgr, tscr=tscr: lambda e:
              e.tensor_sub(out=lgr, in0=lgr, in1=tscr))(),
             (lambda graw=graw, lgr=lgr, pcarry=pcarry, ch=ch: lambda e:
              e.tensor_tensor_scan(out=graw, data0=E[0:4, 5376:6400],
                                   data1=lgr,
                                   initial=(pcarry if ch > 0 else 0.0),
                                   op0=ALU.add, op1=ALU.add))()] +
            [(lambda carry=carry, graw=graw: lambda e:
              e.tensor_copy(out=carry, in_=graw[:, 1023:1024]))()]
        ), deps=[op_sp2, op_zeros, (cap_g if ch == 0 else None),
                 (cap_g2 if ch == 0 else None)],
            tag=f"scan{ch}", serial=True)
        op_lgtr = P.add("tensor", (lambda graw=graw: lambda e: [
            e.transpose(out=psA[0][:, tl2 * 4:tl2 * 4 + 4],
                        in_=graw[:, tl2 * 128:(tl2 + 1) * 128],
                        identity=s_idf[0:4, 0:4])
            for tl2 in range(8)])(),
            deps=[op_scan[ch], op_gev[c5], op_lgev.get(ch - 1)],
            tag=f"lgtr{ch}")
        op_lgev[ch] = P.add("vector", (lambda ch=ch: lambda e: [
            e.tensor_copy(out=LGcol[:, ch * 32:(ch + 1) * 32],
                          in_=psA[0][:, 0:32])])(),
            deps=[op_lgtr], tag=f"lgev{ch}")

    lg_done = op_lgev[NTCH - 1]
    a_done_v = lg_done
    a_done_t = op_kg[(NTCH - 1, "g")]

    op_cx = P.add("sync", lambda e: (
        [e.dma_start(out=crow[0:1, h * NT:(h + 1) * NT],
                     in_=LGcol[64:65, h::4][:, 0:NT],
                     allow_slow_non_contiguous=True) for h in range(NHC)] +
        [e.dma_start(out=Chalf[0:1, h * NSUP:(h + 1) * NSUP],
                     in_=LGcol[0:1, 4 + h::8][:, 0:NSUP],
                     allow_slow_non_contiguous=True) for h in range(NHC)]
    ), deps=[lg_done], dma=8, tag="cx")
    crv = crow[0:1, :].rearrange("a (h i) -> a h i", h=NHC)
    Erv = Erow[0:1, :].rearrange("a (h i) -> a h i", h=NHC)
    Chv = Chalf[0:1, :].rearrange("a (h i) -> a h i", h=NHC)
    op_erow = P.add("vector", [
        lambda e: e.tensor_sub(out=Erv[:, :, 0::2], in0=crv[:, :, 0::2],
                               in1=Chv[:]),
        lambda e: e.tensor_sub(out=Erv[:, :, 1::2], in0=crv[:, :, 1::2],
                               in1=Chv[:]),
        lambda e: e.tensor_scalar(out=Chalf[:], in0=Chalf[:], scalar1=0.5,
                                  scalar2=None, op0=ALU.mult),
    ], deps=[op_cx], tag="erow", serial=True)
    op_frow = P.add("scalar", lambda e: [
        e.activation(out=Frow[:], in_=Erow[:], func=AF.Exp, scale=0.5),
    ], deps=[op_erow], tag="frow")


    Frv = Frow[0:1, :].rearrange("a (h i) -> a h i", h=NHC)
    op_fbc = P.add("tensor", lambda e: [
        e.matmul(out=psU[0][:, 0:4 * NT], lhsT=s_ones[:], rhs=Frow[0:1, :],
                 start=True, stop=True),
        e.matmul(out=psU[1][:, 0:4 * NSUP], lhsT=s_ones[:], rhs=Chalf[0:1, :],
                 start=True, stop=True),
    ], deps=[op_frow, op_qvev[NT - 1], op_lgev[NTCH - 1], op_erow],
        tag="fbc")
    op_fbcev = P.add("vector", lambda e: [
        e.tensor_copy(out=Fbc[:], in_=psU[0][:, 0:4 * NT]),
        e.tensor_copy(out=Cbc[:], in_=psU[1][:, 0:4 * NSUP]),
    ], deps=[op_fbc], tag="fbcev")

    op_S, op_evA, op_evD, op_mask, op_PV = {}, {}, {}, {}, {}
    op_yts, op_ydma = {}, {}
    op_ssI, op_qsc, op_qtr = {}, {}, {}
    prev_yts_last = None
    prev_ydma = None
    prev_uev = None
    last_evD = {}

    def emit_prep(I):
        pi = I % 2
        jmax = 2 * I + 2
        op_ssI[I] = P.add("scalar", (lambda I=I, jmax=jmax, pi=pi: lambda e: [
            e.activation(out=SS[pi][:, h * NT:h * NT + jmax],
                         in_=LGcol[:, h::4][:, 0:jmax],
                         func=AF.Exp, scale=-0.5,
                         bias=Cbc[:, h * NSUP + I:h * NSUP + I + 1])
            for h in range(NHC)])(),
            deps=[op_fbcev, op_ssI.get(I - 2),
                  op_evA.get((I - 2, 2 * I - 3)) if I >= 2 else None,
                  last_evD.get(I - 2)],
            tag=f"ss{I}")
        op_qsc[I] = P.add("vector", (lambda I=I, pi=pi: lambda e: [
            e.tensor_scalar(
                out=qt_sc[pi][:, (h // 2) * 256 + sub * 128 + (h % 2) * 64:
                              (h // 2) * 256 + sub * 128 + (h % 2) * 64 + 64],
                in0=q_tok[:, (2 * I + sub) * 256 + h * 64:
                          (2 * I + sub) * 256 + (h + 1) * 64],
                scalar1=Fbc[:, h * NT + 2 * I + sub:h * NT + 2 * I + sub + 1],
                scalar2=None, op0=ALU.mult)
            for sub in range(2) for h in range(NHC)])(),
            deps=[op_fbcev, op_qtr.get(I - 2)], tag=f"qsc{I}")
        op_qtr[I] = P.add("scalar", (lambda I=I, pi=pi: lambda e: [
            e.dma_start(out=qfm[pi][p][:].rearrange("p (s t) -> p s t", s=2),
                        in_=qt_sc[pi][:, p * 256:(p + 1) * 256].rearrange(
                            "p (s q) -> p s q", s=2),
                        transpose=True)
            for p in range(2)])(),
            deps=[op_qsc[I],
                  op_S.get((I - 2, 2 * I - 3)) if I >= 2 else None],
            dma=2, tag=f"qtr{I}")

    def mk_S(I, j):
        jmax = 2 * I + 2
        par = j % 2
        if par == 0:
            bA = lambda hq: psA[0][:, hq * 256:(hq + 1) * 256]
            bB = lambda hq: psA[1][:, hq * 256:(hq + 1) * 256]
        else:
            bA = lambda hq: pk[:, hq * 256:(hq + 1) * 256]
            bB = lambda hq: pk[:, 512 + hq * 256:512 + (hq + 1) * 256]
        sdeps = [op_qtr[I], op_evA.get((I, j - 2)), op_evD.get((I, j - 2))]
        if j < 2 and I > 0:
            sdeps += [op_evA.get((I - 1, 2 * I - 2 + j)),
                      op_evD.get((I - 1, 2 * I - 2 + j))]
            if j == 1:
                sdeps.append(op_yts.get((I - 1, 7)))
        if I == 0 and j == 0:
            sdeps += [op_qvev[NT - 1], op_lgev[NTCH - 1]]
        pi = I % 2
        op_S[(I, j)] = P.add("tensor", (lambda I=I, j=j, bA=bA, bB=bB, pi=pi:
                                        lambda e: [
            m for p in range(2) for m in (
                e.matmul(out=bA(p),
                         lhsT=Kfm[p][0:64, j * 128:(j + 1) * 128],
                         rhs=qfm[pi][p][0:64, :], start=True, stop=True,
                         tile_position=(0, 0)),
                e.matmul(out=bB(p),
                         lhsT=Kfm[p][64:128, j * 128:(j + 1) * 128],
                         rhs=qfm[pi][p][64:128, :], start=True, stop=True,
                         tile_position=(64, 0)),
            )])(), deps=sdeps, tag=f"S{I}_{j}")
        return bA, bB

    sbanks = {}
    pair_order = []
    emit_prep(0)
    emit_prep(1)
    for I in range(NSUP):
        jmax = 2 * I + 2
        pi = I % 2
        if (I, 0) not in sbanks:
            sbanks[(I, 0)] = mk_S(I, 0)
        for j in range(jmax):
            par = len(pair_order) % 4
            pair_order.append((I, j))
            if j + 1 < jmax:
                sbanks[(I, j + 1)] = mk_S(I, j + 1)
            bA, bB = sbanks[(I, j)]
            if j == 1 and I + 1 < NSUP and (I + 1) not in op_ssI:
                emit_prep(I + 1)
            evdeps = [op_S[(I, j)]]
            if len(pair_order) >= 5:
                evdeps.append(op_PV[pair_order[-5]])
            so2 = 128 if j == 2 * I + 1 else 0
            op_evA[(I, j)] = P.add("scalar", (lambda I=I, j=j, par=par,
                                              so2=so2, bA=bA, pi=pi:
                                              lambda e: [
                e.activation(out=wt[:, par * 1024 + h * 256 + so2:
                                   par * 1024 + (h + 1) * 256],
                             in_=bA(h // 2)[:, so2:],
                             func=AF.Square,
                             scale=SS[pi][:, h * NT + j:h * NT + j + 1])
                for h in (0, 2)])(),
                deps=evdeps + [op_ssI[I]], tag=f"evA{I}_{j}")
            op_evD0 = P.add("vector", (lambda I=I, j=j, par=par,
                                       so2=so2, bB=bB, pi=pi: lambda e: [
                e.tensor_scalar(out=wt[:, par * 1024 + h * 256 + so2:
                                      par * 1024 + (h + 1) * 256],
                                in0=bB(h // 2)[:, so2:],
                                scalar1=SS[pi][:, h * NT + j:h * NT + j + 1],
                                scalar2=None, op0=ALU.mult)
                for h in (1, 3)] + [
                e.tensor_mul(out=wt[:, par * 1024 + h * 256 + so2:
                                   par * 1024 + (h + 1) * 256],
                             in0=wt[:, par * 1024 + h * 256 + so2:
                                   par * 1024 + (h + 1) * 256],
                             in1=wt[:, par * 1024 + h * 256 + so2:
                                   par * 1024 + (h + 1) * 256])
                for h in (1, 3)])(),
                deps=evdeps + [op_ssI[I]], tag=f"evD0{I}_{j}")
            op_evD[(I, j)] = op_evD0
            last_evD[I] = op_evD0
            pvdeps = [op_evA[(I, j)], op_evD0]
            if j == 2 * I:
                op_mask[(I, j)] = P.add("gpsimd", (lambda par=par: lambda e: [
                    e.tensor_mul(out=wt[:, par * 1024 + h * 256:
                                       par * 1024 + h * 256 + 128],
                                 in0=wt[:, par * 1024 + h * 256:
                                       par * 1024 + h * 256 + 128],
                                 in1=s_triu[:])
                    for h in range(NHC)])(),
                    deps=[op_evA[(I, j)], op_evD[(I, j)]], tag=f"mk{I}_{j}")
                pvdeps.append(op_mask[(I, j)])
            elif j == 2 * I + 1:
                op_mask[(I, j)] = P.add("gpsimd", (lambda par=par: lambda e: [
                    x for h in range(NHC) for x in (
                        e.memset(wt[:, par * 1024 + h * 256:
                                   par * 1024 + h * 256 + 128], 0.0),
                        e.tensor_mul(out=wt[:, par * 1024 + h * 256 + 128:
                                           par * 1024 + (h + 1) * 256],
                                     in0=wt[:, par * 1024 + h * 256 + 128:
                                           par * 1024 + (h + 1) * 256],
                                     in1=s_triu[:]),
                    )])(),
                    deps=[op_evA[(I, j)], op_evD[(I, j)]], tag=f"mk{I}_{j}")
                pvdeps.append(op_mask[(I, j)])
            if j == 0:
                pvdeps.append(prev_uev)
                pvdeps.append(op_fbcev)
                if I == 0:
                    pvdeps.append(op_lgev[NTCH - 1])
            op_PV[(I, j)] = P.add("tensor", (lambda I=I, j=j, par=par,
                                             jmax=jmax: lambda e: [
                e.matmul(out=psU[h][0:65, 0:256],
                         lhsT=V2[:, j * 260 + h * 65:j * 260 + (h + 1) * 65],
                         rhs=wt[:, par * 1024 + h * 256:
                               par * 1024 + (h + 1) * 256],
                         start=(j == 0), stop=(j == jmax - 1))
                for h in range(NHC)])(),
                deps=pvdeps, tag=f"PV{I}_{j}")

        op_uev = P.add("vector", (lambda: lambda e: [
            e.tensor_copy(out=Uev[0:65, h * 256:(h + 1) * 256],
                          in_=psU[h][0:65, 0:256])
            for h in range(NHC)])(),
            deps=[op_PV[(I, jmax - 1)]], tag=f"uev{I}")
        prev_uev = op_uev
        if I + 2 < NSUP and (I + 2) not in op_ssI:
            emit_prep(I + 2)
        if I + 1 < NSUP:
            sbanks[(I + 1, 0)] = mk_S(I + 1, 0)
        op_ut = {}
        for kidx in range(8):
            h = kidx // 2
            sub = kidx % 2
            ping = kidx % 2
            utdeps = [op_uev, op_yts.get((I, kidx - 2)),
                      (prev_yts_last if I > 0 and kidx < 2 else None),
                      op_evA.get((I, jmax - 1)), op_evD.get((I, jmax - 1))]
            op_ut[kidx] = P.add("tensor", (lambda h=h, sub=sub,
                                           ping=ping: lambda e: [
                e.transpose(out=pk[:, ping * 512:ping * 512 + 65],
                            in_=Uev[0:65, h * 256 + sub * 128:
                                    h * 256 + (sub + 1) * 128],
                            identity=s_idf[0:65, 0:65])])(),
                deps=utdeps, tag=f"ut{I}_{kidx}")
            ytsdeps = [op_ut[kidx], prev_ydma]
            op_yts[(I, kidx)] = P.add("vector", [
                (lambda ping=ping: lambda e:
                 e.reciprocal(out=rncol[:, ping:ping + 1],
                              in_=pk[:, ping * 512 + 64:ping * 512 + 65]))(),
                (lambda h=h, sub=sub, ping=ping: lambda e:
                 e.tensor_scalar(out=ybuf[:, sub * 256 + h * 64:
                                         sub * 256 + (h + 1) * 64],
                                 in0=pk[:, ping * 512:ping * 512 + 64],
                                 scalar1=rncol[:, ping:ping + 1],
                                 scalar2=None, op0=ALU.mult))(),
            ], deps=ytsdeps, tag=f"yts{I}_{kidx}", serial=True)
        prev_yts_last = op_yts[(I, 7)]
        op_ydma[I] = P.add("sync", (lambda I=I: lambda e: [
            e.dma_start(out=cc_in_h.ap()[256 * I:256 * (I + 1), :]
                        .rearrange("(i p) o -> p i o", p=128),
                        in_=ybuf[:].rearrange("p (i o) -> p i o", i=2))])(),
            deps=[op_yts[(I, k)] for k in range(8)], dma=1, tag=f"ydma{I}")
        prev_ydma = op_ydma[I]

    op_cc = P.add("gpsimd", (lambda: lambda e: [
        e.collective_compute(
            "AllGather", ALU.bypass,
            replica_groups=[[0, 1, 2, 3], [4, 5, 6, 7]],
            ins=[cc_in_h.ap().opt()],
            outs=[cc_out_h.ap().opt()])])(),
        deps=[op_ydma[NSUP - 1]], tag="cc")

    op_cap = prev_yts_last
    if dbg:
        op_cap = P.add("vector", (lambda: lambda e: [
        e.tensor_copy(out=dbg2[:, 0:NT], in_=statS[:, 0:NT]),
        e.tensor_copy(out=dbg2[:, NT:2 * NT], in_=statB[:, 0:NT]),
        e.tensor_copy(out=dbg2[:, 2 * NT:2 * NT + 256],
                      in_=q_tok[:, 0:256]),
        e.tensor_copy(out=dbg2[:, 2 * NT + 256:2 * NT + 512],
                      in_=Kfm[0][:, 0:256]),
        e.tensor_copy(out=dbg2[:, 2 * NT + 512:2 * NT + 772],
                      in_=V2[:, 0:260]),
        e.tensor_copy(out=dbg2[:, 2 * NT + 772:2 * NT + 772 + NHC * NT],
                      in_=SS[0][:]),
            e.tensor_copy(out=dbg2[:, 2 * NT + 772 + NHC * NT:
                                  2 * NT + 772 + NHC * NT + 512],
                          in_=wt[:, 0:512]),
        ])(), deps=[prev_yts_last, op_ssI[NSUP - 1]], tag="cap")

    b_last = {
        "tensor": op_ut[7], "vector": op_cap,
        "scalar": op_evA[(NSUP - 1, 2 * NSUP - 1)],
        "sync": op_ydma[NSUP - 1], "gpsimd": op_cc,
    }
    ab.close()

    cst = ExitStack()

    def sbc(name, shape, dt):
        return cst.enter_context(nc.sbuf_tensor(name, shape, dt))

    Wr_sb = sbc("Wr_sb", [128, 8 * 1024], BF16)
    Wk_sb = sbc("Wk_sb", [128, 8 * 1024], BF16)
    Wv_sb = sbc("Wv_sb", [128, 8 * 1024], BF16)
    xn2fm = sbc("xn2fm", [128, 8 * CHALF], BF16)
    prevcol = sbc("prevcol", [128, 8], BF16)
    xrfm = sbc("xrfm", [128, 8 * CHALF], BF16)
    xkfm = sbc("xkfm", [128, 8 * CHALF], BF16)
    rslab = sbc("rslab", [128, TPH * 1024], BF16)
    kact = sbc("kact", [128, 8 * CHALF], BF16)
    y_sb = sbc("y_sb", [128, TPH * 1024], BF16)
    pcR = cst.enter_context(nc.psum_tensor("pcR", [128, 1024], F32))
    pcK = cst.enter_context(nc.psum_tensor("pcK", [128, 1024], F32))
    pbnd = cst.enter_context(nc.psum_tensor("pbnd", [128, 8], F32))

    blast = list(b_last.values())
    op_wffn = P.add("gpsimd", lambda e: [
        e.dma_start(out=Wr_sb[:].rearrange("p (cc o) -> p cc o", cc=8),
                    in_=wr_d.rearrange("(cc p) o -> p cc o", p=128)),
        e.dma_start(out=Wk_sb[:].rearrange("p (cc o) -> p cc o", cc=8),
                    in_=wk_d.rearrange("(cc p) o -> p cc o", p=128)),
        e.dma_start(out=Wv_sb[:].rearrange("p (cc o) -> p cc o", cc=8),
                    in_=wv_d.rearrange("(cc p) o -> p cc o", p=128)),
    ], deps=blast, dma=3, tag="wffn")

    xn2v = xn2fm[:].rearrange("p (cc t) -> p cc t", cc=8)
    prev_fin, prev_out, prev_tr2_last, prev_o2ev = None, None, None, None
    prev_norm2 = {}
    prev_mix_last = None
    for hf in range(NHALF):
        ysdeps = [op_cc, b_last["sync"], prev_fin]

        def mk_ysb(hf=hf):
            def f(e, sem):
                nd = 4 + (4 if hf == 0 else 0)
                for arm in e.switch_core_id(n=128):
                    q = arm.logical % 4
                    t0g = q * SLICE + hf * CHALF
                    for r in range(4):
                        ins = e.dma_start(
                            out=y_sb[:].rearrange(
                                "p (i x) -> p i x",
                                i=TPH)[:, :, r * 256:(r + 1) * 256],
                            in_=cc_out_h.ap()[r * T + t0g:r * T + t0g + CHALF,
                                              :].rearrange(
                                "(i p) o -> p i o", p=128))
                        ins.then_inc(sem, 16)
                    if hf == 0:
                        for r in range(4):
                            if q == 0:
                                ins = e.dma_start(out=yscr[:],
                                                  in_=cc_out_h.ap()[r * T:
                                                                    r * T + 1, :])
                            else:
                                ins = e.dma_start(
                                    out=ybnd[0:1, r * 256:(r + 1) * 256],
                                    in_=cc_out_h.ap()[r * T + t0g - 1:
                                                      r * T + t0g, :])
                            ins.then_inc(sem, 16)
                return 16 * nd
            return f
        op_ysb = P.add("sync", mk_ysb(), deps=ysdeps + [op_ybnd0],
                       manual=16 * (8 if hf == 0 else 4), tag=f"ysb{hf}")

        if hf == 0:
            op_x1b = P.add("vector", lambda e: [
                e.tensor_add(out=x1b[:], in0=xb0s[:], in1=ybnd[:])],
                deps=[op_ysb, op_const, blast[1]], tag="x1b")
            op_bred = P.add("vector", lambda e: [
                e.tensor_reduce(out=bnds[0:1, 0:1], in_=x1b[:], axis=AX.X,
                                op=ALU.add)], deps=[op_x1b], tag="bred")
            op_bsq = P.add("scalar", lambda e: [
                e.activation(out=rslab[0:1, 0:1024], in_=x1b[:], func=AF.Square,
                             accum_out=bnds[0:1, 1:2])],
                deps=[op_x1b, b_last["scalar"]], tag="bsq")
            op_bst = P.add("vector", [
                lambda e: e.tensor_scalar(out=bnds[0:1, 2:3],
                                          in0=bnds[0:1, 0:1],
                                          scalar1=1.0 / C, scalar2=None,
                                          op0=ALU.mult),
                lambda e: e.tensor_scalar(out=bnds[0:1, 3:4],
                                          in0=bnds[0:1, 1:2],
                                          scalar1=1.0 / C, scalar2=None,
                                          op0=ALU.mult),
                lambda e: e.tensor_mul(out=bnds[0:1, 4:5], in0=bnds[0:1, 2:3],
                                       in1=bnds[0:1, 2:3]),
                lambda e: e.tensor_sub(out=bnds[0:1, 3:4], in0=bnds[0:1, 3:4],
                                       in1=bnds[0:1, 4:5]),
            ], deps=[op_bred, op_bsq], tag="bst", serial=True)
            op_bs2 = P.add("scalar", [
                lambda e: e.activation(out=bnds[0:1, 3:4], in_=bnds[0:1, 3:4],
                                       func=AF.Ln, bias=kcol[0:1, 0:1]),
                lambda e: e.activation(out=bnds[0:1, 5:6], in_=bnds[0:1, 3:4],
                                       func=AF.Exp, scale=-0.5),
            ], deps=[op_bst], tag="bs2", serial=True)
            op_bs3 = P.add("vector", [
                lambda e: e.tensor_scalar(out=bnds[0:1, 6:7],
                                          in0=bnds[0:1, 2:3], scalar1=-1.0,
                                          scalar2=None, op0=ALU.mult),
                lambda e: e.tensor_mul(out=bnds[0:1, 7:8], in0=bnds[0:1, 6:7],
                                       in1=bnds[0:1, 5:6]),
            ], deps=[op_bs2], tag="bs3", serial=True)
            op_bnrm = P.add("vector", lambda e: [
                e.tensor_scalar(out=xn2b[:], in0=x1b[:],
                                scalar1=bnds[0:1, 5:6],
                                scalar2=bnds[0:1, 7:8],
                                op0=ALU.mult, op1=ALU.add)],
                deps=[op_bs3], tag="bnrm")
            op_btr = P.add("tensor", lambda e: [
                e.transpose(out=pbnd[:, cc:cc + 1],
                            in_=xn2b[0:1, cc * 128:(cc + 1) * 128],
                            identity=s_idf[0:1, 0:1])
                for cc in range(8)],
                deps=[op_bnrm, b_last["tensor"]], tag="btr")
            op_bev = P.add("vector", lambda e: [
                e.tensor_copy(out=prevcol[:], in_=pbnd[:, 0:8])],
                deps=[op_btr], tag="bev")
        else:
            op_bev = P.add("vector", lambda e: [
                e.tensor_copy(out=prevcol[:].rearrange("p (c x) -> p c x", x=1),
                              in_=xn2v[:, :, CHALF - 1:CHALF])],
                deps=[prev_tr2_last, prev_o2ev], tag=f"bcp{hf}")

        op_x1t, op_n2, op_tr2 = {}, {}, {}
        op_r2d, op_q2d = {}, {}
        for tl in range(TPH):
            gt = hf * TPH + tl
            par = tl % 2
            xt = E[:, par * 1024:(par + 1) * 1024]
            xd = P.add("sync", (lambda xt=xt, gt=gt: lambda e: [
                e.dma_start(out=xt, in_=xsl[gt * 128:(gt + 1) * 128, :])])(),
                deps=[op_x1t.get(tl - 2) if tl >= 2 else
                      (prev_fin if hf else b_last["scalar"]),
                      (b_last["vector"] if hf == 0 and tl < 2 else None)],
                dma=1, tag=f"cx{hf}_{tl}")
            op_x1t[tl] = P.add("vector", (lambda xt=xt, tl=tl: lambda e: [
                e.tensor_add(out=y_sb[:, tl * 1024:(tl + 1) * 1024], in0=xt,
                             in1=y_sb[:, tl * 1024:(tl + 1) * 1024])])(),
                deps=[xd, op_ysb,
                      (b_last["vector"] if hf == 0 and tl < 2 else None)],
                tag=f"x1t{hf}_{tl}")
            op_r2d[tl] = P.add("vector", (lambda tl=tl: lambda e: [
                e.tensor_reduce(out=statR[:, tl:tl + 1],
                                in_=y_sb[:, tl * 1024:(tl + 1) * 1024],
                                axis=AX.X, op=ALU.add)])(),
                deps=[op_x1t[tl]], tag=f"r2{hf}_{tl}")
            op_q2d[tl] = P.add("scalar", (lambda tl=tl: lambda e: [
                e.activation(out=rslab[:, 0:1024],
                             in_=y_sb[:, tl * 1024:(tl + 1) * 1024],
                             func=AF.Square,
                             accum_out=statQ[:, tl:tl + 1])])(),
                deps=[op_x1t[tl],
                      (op_bsq if hf == 0 and tl == 0 else None)],
                tag=f"q2{hf}_{tl}")
        op_s2 = P.add("vector", [
            lambda e: e.tensor_scalar(out=statM[:, 0:TPH], in0=statR[:, 0:TPH],
                                      scalar1=1.0 / C, scalar2=None,
                                      op0=ALU.mult),
            lambda e: e.tensor_scalar(out=statQ[:, 0:TPH], in0=statQ[:, 0:TPH],
                                      scalar1=1.0 / C, scalar2=None,
                                      op0=ALU.mult),
            lambda e: e.tensor_mul(out=statR[:, 0:TPH], in0=statM[:, 0:TPH],
                                   in1=statM[:, 0:TPH]),
            lambda e: e.tensor_sub(out=statQ[:, 0:TPH], in0=statQ[:, 0:TPH],
                                   in1=statR[:, 0:TPH]),
        ], deps=[op_r2d[TPH - 1], op_q2d[TPH - 1]],
            tag=f"s2{hf}", serial=True)
        op_s2s = P.add("scalar", [
            lambda e: e.activation(out=statQ[:, 0:TPH], in_=statQ[:, 0:TPH],
                                   func=AF.Ln, bias=kcol[:, 0:1]),
            lambda e: e.activation(out=statS[:, 0:TPH], in_=statQ[:, 0:TPH],
                                   func=AF.Exp, scale=-0.5),
        ], deps=[op_s2], tag=f"s2s{hf}", serial=True)
        op_s2b = P.add("vector", [
            lambda e: e.tensor_scalar(out=statM[:, 0:TPH], in0=statM[:, 0:TPH],
                                      scalar1=-1.0, scalar2=None,
                                      op0=ALU.mult),
            lambda e: e.tensor_mul(out=statB[:, 0:TPH], in0=statM[:, 0:TPH],
                                   in1=statS[:, 0:TPH]),
        ], deps=[op_s2s], tag=f"s2b{hf}", serial=True)
        for tl in range(TPH):
            par = tl % 2
            op_n2[tl] = P.add("vector", (lambda tl=tl, par=par: lambda e: [
                e.tensor_scalar(out=xrfm[:, (1 - par) * 1024:
                                         (2 - par) * 1024],
                                in0=y_sb[:, tl * 1024:(tl + 1) * 1024],
                                scalar1=statS[:, tl:tl + 1],
                                scalar2=statB[:, tl:tl + 1],
                                op0=ALU.mult, op1=ALU.add)])(),
                deps=[op_s2b, op_tr2.get(tl - 2)], tag=f"n2{hf}_{tl}")
            trdeps = [op_n2[tl]]
            if tl == 0:
                trdeps.append(prev_mix_last)
            if tl == TPH - 1:
                trdeps.append(op_bev)
            op_tr2[tl] = P.add("sync", (lambda tl=tl, par=par: lambda e: [
                e.dma_start(out=xn2fm[:].rearrange(
                                "p (c t) -> p c t",
                                c=8)[:, :, tl * 128:(tl + 1) * 128],
                            in_=xrfm[:, (1 - par) * 1024:(2 - par) * 1024]
                            .rearrange("p (c q) -> p c q", c=8),
                            transpose=True)])(),
                deps=trdeps, dma=1, tag=f"tr2{hf}_{tl}")
        prev_norm2 = op_n2
        prev_tr2_last = op_tr2[TPH - 1]

        op_mix = {}
        for cc in range(8):
            op_mix[cc] = P.add("vector", (lambda cc=cc: lambda e: [
                e.tensor_sub(out=xrfm[:, cc * CHALF:cc * CHALF + 1],
                             in0=xn2fm[:, cc * CHALF:cc * CHALF + 1],
                             in1=prevcol[:, cc:cc + 1]),
                e.scalar_tensor_tensor(
                    out=xkfm[:, cc * CHALF:cc * CHALF + 1],
                    in0=xrfm[:, cc * CHALF:cc * CHALF + 1],
                    scalar=s_muk[:, cc:cc + 1],
                    in1=prevcol[:, cc:cc + 1], op0=ALU.mult, op1=ALU.add),
                e.scalar_tensor_tensor(
                    out=xrfm[:, cc * CHALF:cc * CHALF + 1],
                    in0=xrfm[:, cc * CHALF:cc * CHALF + 1],
                    scalar=s_mur[:, cc:cc + 1],
                    in1=prevcol[:, cc:cc + 1], op0=ALU.mult, op1=ALU.add),
                e.tensor_sub(out=xrfm[:, cc * CHALF + 1:(cc + 1) * CHALF],
                             in0=xn2fm[:, cc * CHALF + 1:(cc + 1) * CHALF],
                             in1=xn2fm[:, cc * CHALF:(cc + 1) * CHALF - 1]),
                e.scalar_tensor_tensor(
                    out=xkfm[:, cc * CHALF + 1:(cc + 1) * CHALF],
                    in0=xrfm[:, cc * CHALF + 1:(cc + 1) * CHALF],
                    scalar=s_muk[:, cc:cc + 1],
                    in1=xn2fm[:, cc * CHALF:(cc + 1) * CHALF - 1],
                    op0=ALU.mult, op1=ALU.add),
                e.scalar_tensor_tensor(
                    out=xrfm[:, cc * CHALF + 1:(cc + 1) * CHALF],
                    in0=xrfm[:, cc * CHALF + 1:(cc + 1) * CHALF],
                    scalar=s_mur[:, cc:cc + 1],
                    in1=xn2fm[:, cc * CHALF:(cc + 1) * CHALF - 1],
                    op0=ALU.mult, op1=ALU.add),
            ])(), deps=[op_tr2[tl2] for tl2 in range(TPH)] + [op_bev,
                        op_const, prev_o2ev], tag=f"mix{hf}_{cc}")
        mixall = [op_mix[cc] for cc in range(8)]
        prev_mix_last = op_mix[7]

        op_rp, op_rev = {}, {}
        for tl in range(TPH):
            pr = pcR if tl % 2 == 0 else pcK
            op_rp[tl] = P.add("tensor", (lambda tl=tl, pr=pr: lambda e: [
                m for cc in range(8) for m in (
                    e.matmul(out=pr[:, 0:512],
                             lhsT=xrfm[:, cc * CHALF + tl * 128:
                                      cc * CHALF + (tl + 1) * 128],
                             rhs=Wr_sb[:, cc * 1024:cc * 1024 + 512],
                             start=(cc == 0), stop=(cc == 7)),
                    e.matmul(out=pr[:, 512:1024],
                             lhsT=xrfm[:, cc * CHALF + tl * 128:
                                      cc * CHALF + (tl + 1) * 128],
                             rhs=Wr_sb[:, cc * 1024 + 512:(cc + 1) * 1024],
                             start=(cc == 0), stop=(cc == 7)),
                )])(), deps=mixall + [op_wffn, op_rev.get(tl - 2),
                                      (b_last["tensor"] if hf == 0 and tl < 2
                                       else None),
                                      (op_btr if hf == 0 else None)],
                tag=f"rp{hf}_{tl}")
            op_rev[tl] = P.add("scalar", (lambda tl=tl, pr=pr: lambda e: [
                e.activation(out=rslab[:, tl * 1024:(tl + 1) * 1024],
                             in_=pr[:], func=AF.Sigmoid)])(),
                deps=[op_rp[tl]], tag=f"rev{hf}_{tl}")

        op_kp, op_ksq = {}, {}
        for dc in range(8):
            pr = pcR if (TPH + dc) % 2 == 0 else pcK
            nspl = max(1, CHALF // 512)
            wspl = CHALF // nspl
            op_kp[dc] = P.add("tensor", (lambda dc=dc, pr=pr, nspl=nspl,
                                         wspl=wspl: lambda e: [
                e.matmul(out=pr[:, sp * wspl:(sp + 1) * wspl],
                         lhsT=Wk_sb[:, cc * 1024 + dc * 128:
                                   cc * 1024 + (dc + 1) * 128],
                         rhs=xkfm[:, cc * CHALF + sp * wspl:
                                 cc * CHALF + (sp + 1) * wspl],
                         start=(cc == 0), stop=(cc == 7))
                for cc in range(8) for sp in range(nspl)])(),
                deps=mixall + [op_wffn,
                                      op_ksq.get(dc - 2) or op_rev.get(TPH - 2 + dc),
                                      op_rev.get(TPH - 1) if dc == 1 else None],
                tag=f"kp{hf}_{dc}")
            op_krl = P.add("scalar", (lambda dc=dc, pr=pr: lambda e: [
                e.activation(out=kact[:, dc * CHALF:(dc + 1) * CHALF],
                             in_=pr[:, 0:CHALF], func=AF.Relu)])(),
                deps=[op_kp[dc]], tag=f"krl{hf}_{dc}")
            op_ksq[dc] = P.add("vector", (lambda dc=dc: lambda e: [
                e.tensor_mul(out=kact[:, dc * CHALF:(dc + 1) * CHALF],
                             in0=kact[:, dc * CHALF:(dc + 1) * CHALF],
                             in1=kact[:, dc * CHALF:(dc + 1) * CHALF])])(),
                deps=[op_krl], tag=f"ksq{hf}_{dc}")

        fins = {}
        po_half = prev_fouts if hf else {}
        fouts = {}
        for tl in range(TPH):
            pr = pcR if (TPH + 8 + tl) % 2 == 0 else pcK
            o2deps = [op_ksq[dc] for dc in range(8)] + [op_wffn]
            o2deps.append(prev_o2ev if tl < 2 else None)
            o2deps.append(op_krl if tl == 0 else None)
            o2deps.append(fins.get(tl - 2))
            op_o2 = P.add("tensor", (lambda tl=tl, pr=pr: lambda e: [
                m for dc in range(8) for m in (
                    e.matmul(out=pr[:, 0:512],
                             lhsT=kact[:, dc * CHALF + tl * 128:
                                      dc * CHALF + (tl + 1) * 128],
                             rhs=Wv_sb[:, dc * 1024:dc * 1024 + 512],
                             start=(dc == 0), stop=(dc == 7)),
                    e.matmul(out=pr[:, 512:1024],
                             lhsT=kact[:, dc * CHALF + tl * 128:
                                      dc * CHALF + (tl + 1) * 128],
                             rhs=Wv_sb[:, dc * 1024 + 512:(dc + 1) * 1024],
                             start=(dc == 0), stop=(dc == 7)),
                )])(), deps=o2deps, tag=f"o2{hf}_{tl}")
            fpar = tl % 2
            fb = E[:, 2048 + fpar * 1024:3072 + fpar * 1024]
            op_fin = P.add("vector", [
                (lambda tl=tl, pr=pr, fb=fb: lambda e: e.tensor_mul(
                    out=fb,
                    in0=rslab[:, tl * 1024:(tl + 1) * 1024], in1=pr[:]))(),
                (lambda tl=tl, fb=fb: lambda e: e.tensor_add(
                    out=fb, in0=fb,
                    in1=y_sb[:, tl * 1024:(tl + 1) * 1024]))(),
            ], deps=[op_o2, op_rev[tl], fouts.get(tl - 2),
                     (po_half.get(TPH - 2 + tl) if tl < 2 else None)],
                tag=f"fin{hf}_{tl}", serial=True)
            fins[tl] = op_fin
            prev_o2ev = op_fin
            gt = hf * TPH + tl
            prev_out = P.add("sync", (lambda gt=gt, fb=fb: lambda e: [
                e.dma_start(out=out[gt * 128:(gt + 1) * 128, :],
                            in_=fb)])(),
                deps=[op_fin], dma=1, tag=f"odma{hf}_{tl}")
            fouts[tl] = prev_out
            prev_fin = op_fin
        prev_fouts = fouts

    if dbg:
        cap_c = P.add("vector", lambda e: [
            e.tensor_copy(out=dbg2[:, 1400:1528], in_=xn2fm[:, 0:128]),
            e.tensor_copy(out=dbg2[:, 1528:1656], in_=xrfm[:, 0:128]),
            e.tensor_copy(out=dbg2[:, 1656:1784], in_=rslab[:, 0:128]),
            e.tensor_copy(out=dbg2[:, 1784:1912], in_=kact[:, 0:128]),
            e.tensor_copy(out=dbg2[:, 1912:2040], in_=y_sb[:, 0:128]),
            e.tensor_copy(out=dbg2[:, 2040:2042], in_=statS[:, 0:2]),
            e.tensor_copy(out=dbg2[:, 2042:2044], in_=statB[:, 0:2]),
            e.tensor_copy(out=dbg2[:, 2044:2046], in_=statM[:, 0:2]),
            e.tensor_copy(out=dbg2[:, 2046:2048], in_=statQ[:, 0:2]),
        ], deps=[prev_mix_last, op_rev[TPH - 1], op_ksq[7], prev_fin],
            tag="cap_c")
        NDW = NT * 4 + 12
        dbg_sb = cst.enter_context(nc.sbuf_tensor("dbg_sb", [128, NDW], F32))
        op_dbg = P.add("vector", lambda e: [
            e.tensor_copy(out=dbg_sb[:, 0:NT * 4], in_=LGcol[:]),
            e.tensor_copy(out=dbg_sb[:, NT * 4:NT * 4 + 12],
                          in_=LGcol[:, 0:12]),
        ], deps=[prev_yts_last, op_cc, prev_fin, cap_c], tag="dbgv")
        op_dbgd = P.add("sync", lambda e: [
            e.dma_start(out=dbgl, in_=dbg_sb[:]),
            e.dma_start(out=dbgy, in_=cc_in_h.ap()),
            e.dma_start(out=dbg2o, in_=dbg2[:]),
            e.dma_start(out=dbg3o, in_=dbg3[:]),
        ], deps=[op_dbg, op_cc], dma=4, tag="dbgd")
    else:
        op_dbgd = None
    P.add("gpsimd", lambda e: [e.engine_nop()],
          deps=[prev_out, prev_fouts.get(TPH - 2), op_dbgd], tag="done")

    P.emit(nc, blk, sems)
    cst.close()
    per.close()
    return nc


def kernel(L0_attn_x, L0_attn_W, L0_ffn_W_r, L0_ffn_W_k, L0_ffn_W_v,
           L0_ffn_mu_r, L0_ffn_mu_k, _T=None):
    x = np.ascontiguousarray(L0_attn_x, dtype=np.float32)
    W = np.asarray(L0_attn_W, dtype=np.float32)
    B, T, _ = x.shape
    if _T is not None:
        T = _T
        x = x[:, :T]
    SLICE = T // 4

    nc = build(T)

    identb = np.eye(128, dtype=BF)
    identf = np.eye(128, dtype=np.float32)
    triu = np.triu(np.ones((128, 128), dtype=np.float32)).astype(BF)
    onesrow = np.ones((1, 128), dtype=np.float32)
    mur = np.ascontiguousarray(
        np.asarray(L0_ffn_mu_r, np.float32).reshape(8, 128).T)
    muk = np.ascontiguousarray(
        np.asarray(L0_ffn_mu_k, np.float32).reshape(8, 128).T)
    wr = np.ascontiguousarray(np.asarray(L0_ffn_W_r, np.float32).T)
    wk2 = np.ascontiguousarray(np.asarray(L0_ffn_W_k, np.float32).T)
    wv2 = np.ascontiguousarray(np.asarray(L0_ffn_W_v, np.float32).T)

    in_maps = []
    for c in range(8):
        b = c // 4
        hc = c % 4
        q = c % 4
        Wq = W[256 * hc:256 * (hc + 1), :]
        Wv = W[2048 + 256 * hc:2048 + 256 * (hc + 1), :]
        Wk = W[1024 + 256 * hc:1024 + 256 * (hc + 1), :]
        Wg = W[3072 + 4 * hc:3072 + 4 * (hc + 1), :]
        wqv = np.ascontiguousarray(
            np.concatenate([Wq.T, Wv.T], axis=1))
        wkg = np.ascontiguousarray(
            np.concatenate([Wk.T, Wg.T], axis=1))
        t0 = q * SLICE
        xb0 = (np.zeros((1, C), np.float32) if t0 == 0
               else np.ascontiguousarray(x[b, t0 - 1:t0]))
        in_maps.append({
            "xb": np.ascontiguousarray(x[b]),
            "xsl": np.ascontiguousarray(x[b, t0:t0 + SLICE]),
            "xb0": xb0,
            "wqv": wqv, "wkg": wkg,
            "wr": wr, "wk2": wk2, "wv2": wv2,
            "mur": mur, "muk": muk,
            "identb": identb, "identf": identf, "triu": triu,
            "onesrow": onesrow,
        })

    res = run_bass_kernel_spmd(nc, in_maps, core_ids=list(range(8)))
    outp = np.empty((B, T, C), dtype=np.float32)
    for c in range(8):
        b = c // 4
        q = c % 4
        outp[b, q * SLICE:(q + 1) * SLICE] = res.results[c]["out"]
    return outp


if __name__ == "__main__":
    build(1024)
    print("build ok")


# version 15
# speedup vs baseline: 1.1658x; 1.0006x over previous
import numpy as np
import ml_dtypes
import concourse.bass as bass
import concourse.mybir as mybir
from concourse.bass_utils import run_bass_kernel_spmd
from contextlib import ExitStack

F32 = mybir.dt.float32
BF16 = mybir.dt.bfloat16
AF = mybir.ActivationFunctionType
ALU = mybir.AluOpType
AX = mybir.AxisListType
BF = ml_dtypes.bfloat16

C = 1024
NHC = 4
LNEPS = 1e-5
ENGS = ("sync", "gpsimd", "scalar", "vector", "tensor")


class Op:
    __slots__ = ("eng", "fn", "deps", "dma", "manual", "seq", "needed", "tag", "slot", "serial")

    def __init__(self, eng, fn, deps, dma, manual, tag, serial=False):
        self.eng, self.fn, self.deps = eng, fn, list(deps)
        self.dma, self.manual, self.tag = dma, manual, tag
        self.serial = serial
        self.seq = 0
        self.needed = False


class Plan:
    def __init__(self):
        self.ops = []

    def add(self, eng, fn, deps=(), dma=0, manual=0, tag="", serial=False):
        op = Op(eng, fn, [d for d in deps if d is not None], dma, manual, tag,
                serial)
        self.ops.append(op)
        return op

    NSLOT = 8

    def emit(self, nc, blk, sems):
        for op in self.ops:
            for d in op.deps:
                d.needed = True
        dslot = {e: 0 for e in ENGS}
        cnt = {}
        for op in self.ops:
            if op.dma or op.manual:
                op.slot = dslot[op.eng]
                dslot[op.eng] = (dslot[op.eng] + 1) % Plan.NSLOT
                k = (op.eng, 'd', op.slot)
                cnt[k] = cnt.get(k, 0) + (op.manual if op.manual
                                          else 16 * op.dma)
                op.seq = cnt[k]
            else:
                op.slot = None
                k = (op.eng, 'c')
                if op.serial:
                    cnt[k] = cnt.get(k, 0) + len(op.fn)
                elif op.needed:
                    cnt[k] = cnt.get(k, 0) + 1
                op.seq = cnt.get(k, 0)

        def body(eng_name):
            def f(eng):
                last_wait = {}
                for op in self.ops:
                    if op.eng != eng_name:
                        continue
                    for d in op.deps:
                        if d.seq == 0 or not (d.needed or d.manual):
                            continue
                        dk = ((d.eng, 'd', d.slot) if d.slot is not None
                              else (d.eng, 'c'))
                        if d.seq > last_wait.get(dk, 0):
                            eng.wait_ge(sems[dk], d.seq)
                            last_wait[dk] = d.seq
                    if op.manual:
                        n = op.fn(eng, sems[(eng_name, 'd', op.slot)])
                        assert n == op.manual, (op.tag, n, op.manual)
                    elif op.serial:
                        base = op.seq - len(op.fn)
                        semc = sems[(eng_name, 'c')]
                        for i2, f2 in enumerate(op.fn):
                            ins = f2(eng)
                            ins.then_inc(semc, 1)
                            if i2 < len(op.fn) - 1:
                                eng.wait_ge(semc, base + i2 + 1)
                        last_wait[(eng_name, 'c')] = op.seq - 1
                    else:
                        instrs = op.fn(eng)
                        if op.needed:
                            if op.dma:
                                assert len(instrs) == op.dma, (
                                    op.tag, len(instrs), op.dma)
                                for ins in instrs:
                                    ins.then_inc(
                                        sems[(eng_name, 'd', op.slot)], 16)
                            else:
                                instrs[-1].then_inc(sems[(eng_name, 'c')], 1)
            return f

        blk.sync(body("sync"))
        blk.gpsimd(body("gpsimd"))
        blk.scalar(body("scalar"))
        blk.vector(body("vector"))
        blk.tensor(body("tensor"))


def build(T=8192, dbg=None):
    if dbg is None:
        dbg = T < 8192
    NT = T // 128
    NSUP = T // 256
    NTCH = T // 1024
    SLICE = T // 4
    CHALF = min(1024, SLICE)
    NHALF = SLICE // CHALF
    TPH = CHALF // 128

    nc = bass.Bass("TRN2", target_bir_lowering=False,
                   detect_race_conditions=False)

    def din(name, shape, dt):
        return nc.dram_tensor(name, shape, dt, kind="ExternalInput").ap()
    xb = din("xb", [T, C], F32)
    xsl = din("xsl", [SLICE, C], F32)
    xb0 = din("xb0", [1, C], F32)
    wqv = din("wqv", [C, 512], F32)
    wkg = din("wkg", [C, 260], F32)
    wr_d = din("wr", [C, C], F32)
    wk_d = din("wk2", [C, C], F32)
    wv_d = din("wv2", [C, C], F32)
    mur = din("mur", [128, 8], F32)
    muk = din("muk", [128, 8], F32)
    identb = din("identb", [128, 128], BF16)
    identf = din("identf", [128, 128], F32)
    triu_d = din("triu", [128, 128], BF16)
    onesrow = din("onesrow", [1, 128], F32)
    out = nc.dram_tensor("out", [SLICE, C], F32, kind="ExternalOutput").ap()
    if dbg:
        dbgy = nc.dram_tensor("dbgy", [T, 256], BF16,
                              kind="ExternalOutput").ap()
        dbgl = nc.dram_tensor("dbgl", [128, NT * 4 + 12],
                              F32, kind="ExternalOutput").ap()
        dbg2o = nc.dram_tensor("dbg2o", [128, 2048], F32,
                               kind="ExternalOutput").ap()
        dbg3o = nc.dram_tensor("dbg3o", [128, 1536], F32,
                               kind="ExternalOutput").ap()

    cc_in_h = nc.dram_tensor("cc_in", [T, 256], BF16, kind="Internal")
    cc_out_h = nc.dram_tensor("cc_out", [4 * T, 256], BF16, kind="Internal")

    per = ExitStack()
    sems = {}
    for e in ENGS:
        sems[(e, 'c')] = per.enter_context(nc.semaphore("s_" + e + "_c"))
        for sl in range(Plan.NSLOT):
            sems[(e, 'd', sl)] = per.enter_context(
                nc.semaphore(f"s_{e}_d{sl}"))

    def sb(name, shape, dt):
        return per.enter_context(nc.sbuf_tensor(name, shape, dt))

    s_idb = sb("s_idb", [128, 128], BF16)
    s_idf = sb("s_idf", [128, 128], F32)
    s_triu = sb("s_triu", [128, 128], BF16)
    s_ones = sb("s_ones", [1, 128], F32)
    s_mur = sb("s_mur", [128, 8], F32)
    s_muk = sb("s_muk", [128, 8], F32)
    E = sb("E", [128, 6400], F32)
    xA = sb("xA", [128, 3072], F32)
    LGcol = sb("LGcol", [128, NT * 4], F32)
    statR = sb("statR", [128, 8], F32)
    statQ = sb("statQ", [128, 8], F32)
    statM = sb("statM", [128, 8], F32)
    statS = sb("statS", [128, NT], F32)
    statB = sb("statB", [128, NT], F32)
    crow = sb("crow", [1, 4 * NT], F32)
    Chalf = sb("Chalf", [1, 4 * NSUP], F32)
    Erow = sb("Erow", [1, 4 * NT], F32)
    Frow = sb("Frow", [1, 4 * NT], F32)
    fcol = sb("fcol", [128, 12], F32)
    ybnd = sb("ybnd", [1, C], BF16)
    yscr = sb("yscr", [1, 256], BF16)
    xb0s = sb("xb0s", [1, C], F32)
    x1b = sb("x1b", [1, C], F32)
    xn2b = sb("xn2b", [1, C], F32)
    bnds = sb("bnds", [1, 8], F32)
    kcol = sb("kcol", [128, 2], F32)
    dbg2 = sb("dbg2", [128, 2048], F32) if dbg else None
    dbg3 = sb("dbg3", [128, 1536], F32) if dbg else None

    blk = per.enter_context(nc.Block())
    P = Plan()

    ab = ExitStack()

    def sba(name, shape, dt):
        return ab.enter_context(nc.sbuf_tensor(name, shape, dt))

    Wqv = sba("Wqv", [128, 8 * 512], BF16)
    Wkg = sba("Wkg", [128, 8 * 260], BF16)
    q_tok = sba("q_tok", [128, NT * 256], BF16)
    Kfm = [sba("Kfm0", [128, T], BF16), sba("Kfm1", [128, T], BF16)]
    V2 = sba("V2", [128, NT * 260], BF16)
    xn_tok = sba("xn_tok", [128, 2 * 1024], BF16)
    xn_fm = [sba("xn_fm0", [128, 8 * 512], BF16),
             sba("xn_fm1", [128, 8 * 512], BF16)]
    SS = [sba("SS0", [128, NHC * NT], F32),
          sba("SS1", [128, NHC * NT], F32)]
    Fbc = sba("Fbc", [128, 4 * NT], F32)
    Cbc = sba("Cbc", [128, 4 * NSUP], F32)
    wt = sba("wt", [128, 4 * NHC * 256], BF16)
    qt_sc = [sba("qt_sc0", [128, 512], BF16),
             sba("qt_sc1", [128, 512], BF16)]
    qfm = [[sba("qfm00", [128, 256], BF16), sba("qfm01", [128, 256], BF16)],
           [sba("qfm10", [128, 256], BF16), sba("qfm11", [128, 256], BF16)]]
    Uev = sba("Uev", [65, 1024], F32)
    rncol = sba("rncol", [128, 2], F32)
    ybuf = sba("ybuf", [128, 512], BF16)

    psA = [ab.enter_context(nc.psum_tensor(f"psA{i}", [128, 512], F32))
           for i in range(2)]
    psU = [ab.enter_context(nc.psum_tensor(f"psU{i}", [128, 512], F32))
           for i in range(4)]
    pk = ab.enter_context(nc.psum_tensor("pk", [128, 1024], F32))

    v2v = V2[:].rearrange("p (j h x) -> p j h x", j=NT, h=NHC)

    op_const = P.add("sync", lambda e: [
        e.dma_start(out=s_idb[:], in_=identb),
        e.dma_start(out=s_idf[:], in_=identf),
        e.dma_start(out=s_triu[:], in_=triu_d),
        e.dma_start(out=s_ones[:], in_=onesrow),
        e.dma_start(out=s_mur[:], in_=mur),
        e.dma_start(out=s_muk[:], in_=muk),
        e.dma_start(out=xb0s[:], in_=xb0),
    ], dma=7, tag="consts")
    op_wload = P.add("gpsimd", lambda e: [
        e.dma_start(out=Wqv[:].rearrange("p (cc o) -> p cc o", cc=8),
                    in_=wqv.rearrange("(cc p) o -> p cc o", p=128)),
        e.dma_start(out=Wkg[:].rearrange("p (cc o) -> p cc o", cc=8),
                    in_=wkg.rearrange("(cc p) o -> p cc o", p=128)),
    ], dma=2, tag="wload")
    op_v2ones = P.add("gpsimd", lambda e: [
        e.memset(v2v[:, :, :, 64:65], 1.0)], tag="v2ones")
    op_ybnd0 = P.add("gpsimd", lambda e: [e.memset(ybnd[:], 0.0)], tag="ybnd0")
    op_kcol = P.add("gpsimd", lambda e: [
        e.memset(kcol[:, 0:1], LNEPS), e.memset(kcol[:, 1:2], 1.0),
        e.memset(SS[0][:], 0.0), e.memset(SS[1][:], 0.0)] +
        ([e.memset(dbg2[:], 0.0), e.memset(dbg3[:], 0.0)] if dbg else []),
        tag="kcol")

    NCH = T // 512
    op_norm, op_tr, op_qv, op_qvev = {}, {}, {}, {}
    op_kg, op_kev, op_gev, op_lgev = {}, {}, {}, {}
    op_scan = {}
    op_red, op_sq, op_a2b = {}, {}, {}
    cap_a = None
    op_zeros = P.add("vector", lambda e: [e.memset(E[0:4, 5376:6400], 0.0)],
                     tag="scan_zeros")
    xq = [E[:, 0:1024], E[:, 1024:2048], xA[:, 0:1024], xA[:, 1024:2048],
          xA[:, 2048:3072]]
    for c5 in range(NCH):
        buf = c5 % 2
        xnb = xn_fm[buf]
        for tl4 in range(4):
            tt = c5 * 4 + tl4
            xt = xq[tt % 5]
            dd = ([op_norm[tt - 5], op_sq[tt - 5]] if tt >= 5 else [])
            xd = P.add("gpsimd" if tt % 2 == 0 else "sync",
                       (lambda xt=xt, tt=tt: lambda e: [
                e.dma_start(out=xt, in_=xb[tt * 128:(tt + 1) * 128, :])])(),
                deps=dd, dma=1, tag=f"x_{tt}")
            op_red[tt] = P.add("vector", (lambda xt=xt, tt=tt: lambda e: [
                e.tensor_reduce(out=statR[:, tt % 8:tt % 8 + 1], in_=xt,
                                axis=AX.X,
                                op=ALU.add)])(), deps=[xd], tag=f"red{tt}")
            op_sq[tt] = P.add("scalar", (lambda xt=xt, tt=tt: lambda e: [
                e.activation(out=wt[:, 0:1024], in_=xt, func=AF.Square,
                             accum_out=statQ[:, tt % 8:tt % 8 + 1])])(),
                deps=[xd, op_sq.get(tt - 1)], tag=f"sq{tt}")
        t0c, t1c = c5 * 4, c5 * 4 + 4
        q0, q1 = (c5 % 2) * 4, (c5 % 2) * 4 + 4
        if dbg and c5 == 0:
            cap_a = P.add("vector", lambda e: [
                e.tensor_copy(out=dbg3[:, 0:256], in_=E[:, 0:256]),
                e.tensor_copy(out=dbg3[:, 256:260], in_=statR[:, 0:4]),
                e.tensor_copy(out=dbg3[:, 256 + NT:256 + NT + 4],
                              in_=statQ[:, 0:4]),
            ], deps=[op_red[3], op_sq[3], op_kcol], tag="cap_a")
        op_a2v = P.add("vector", [
            (lambda q0=q0, q1=q1: lambda e: e.tensor_scalar(
                out=statM[:, q0:q1], in0=statR[:, q0:q1],
                scalar1=1.0 / C, scalar2=None, op0=ALU.mult))(),
            (lambda q0=q0, q1=q1: lambda e: e.tensor_scalar(
                out=statQ[:, q0:q1], in0=statQ[:, q0:q1],
                scalar1=1.0 / C, scalar2=None, op0=ALU.mult))(),
            (lambda q0=q0, q1=q1: lambda e: e.tensor_mul(
                out=statR[:, q0:q1], in0=statM[:, q0:q1],
                in1=statM[:, q0:q1]))(),
            (lambda q0=q0, q1=q1: lambda e: e.tensor_sub(
                out=statQ[:, q0:q1], in0=statQ[:, q0:q1],
                in1=statR[:, q0:q1]))(),
        ], deps=[op_red[t1c - 1], op_sq[t1c - 1], op_a2b.get(c5 - 2),
                 (cap_a if c5 == 0 else None)],
            tag=f"a2v{c5}", serial=True)
        op_a2s = P.add("scalar", [
            (lambda q0=q0, q1=q1: lambda e: e.activation(
                out=statQ[:, q0:q1], in_=statQ[:, q0:q1],
                func=AF.Ln, bias=kcol[:, 0:1]))(),
            (lambda t0c=t0c, t1c=t1c, q0=q0, q1=q1: lambda e: e.activation(
                out=statS[:, t0c:t1c], in_=statQ[:, q0:q1],
                func=AF.Exp, scale=-0.5))(),
        ], deps=[op_a2v, op_kcol], tag=f"a2s{c5}", serial=True)
        op_a2b[c5] = P.add("vector", [
            (lambda q0=q0, q1=q1: lambda e: e.tensor_scalar(
                out=statM[:, q0:q1], in0=statM[:, q0:q1],
                scalar1=-1.0, scalar2=None, op0=ALU.mult))(),
            (lambda t0c=t0c, t1c=t1c, q0=q0, q1=q1: lambda e: e.tensor_mul(
                out=statB[:, t0c:t1c], in0=statM[:, q0:q1],
                in1=statS[:, t0c:t1c]))(),
        ], deps=[op_a2s], tag=f"a2b{c5}", serial=True)
        for tl4 in range(4):
            tt = c5 * 4 + tl4
            par = tt % 2
            op_norm[tt] = P.add("vector", (lambda tl4=tl4, tt=tt, par=par:
                                           lambda e: [
                e.tensor_scalar(out=xn_tok[:, par * 1024:(par + 1) * 1024],
                                in0=xq[tt % 5], scalar1=statS[:, tt:tt + 1],
                                scalar2=statB[:, tt:tt + 1],
                                op0=ALU.mult, op1=ALU.add)])(),
                deps=[op_a2b[c5], op_tr.get(tt - 2)], tag=f"norm{tt}")
            trdeps = [op_norm[tt]]
            if c5 >= 2:
                trdeps.append(op_kg[(c5 - 2, "g")])
            op_tr[tt] = P.add("sync" if tt % 2 == 0 else "scalar",
                              (lambda tt=tt, tl4=tl4, par=par, xnb=xnb:
                               lambda e: [
                e.dma_start(out=xnb[:].rearrange(
                                "p (c t) -> p c t",
                                c=8)[:, :, tl4 * 128:(tl4 + 1) * 128],
                            in_=xn_tok[:, par * 1024:(par + 1) * 1024]
                            .rearrange("p (c q) -> p c q", c=8),
                            transpose=True)])(),
                deps=trdeps, dma=1, tag=f"tr{tt}")
        trd = [op_tr[c5 * 4 + t2] for t2 in range(4)]
        if c5 == 0 and dbg:
            cap_b = P.add("vector", (lambda: lambda e: [
                e.tensor_copy(out=dbg3[:, 640:896], in_=xn_tok[:, 0:256]),
                e.tensor_copy(out=dbg3[:, 896:1152], in_=xn_fm[0][:, 0:256]),
            ])(), deps=trd + [op_norm[0]], tag="cap_b")
            trd = trd + [cap_b]
        for tl4 in range(4):
            tt = c5 * 4 + tl4
            pq = psU[tl4 % 2]
            deps = [op_tr[tt], op_wload, op_qvev.get(tt - 2)] + (
                [cap_b] if c5 == 0 and dbg else [])
            op_qv[tt] = P.add("tensor", (lambda tt=tt, tl4=tl4, pq=pq,
                                         xnb=xnb: lambda e: [
                e.matmul(out=pq[:], lhsT=xnb[:, cc * 512 + tl4 * 128:
                                            cc * 512 + (tl4 + 1) * 128],
                         rhs=Wqv[:, cc * 512:(cc + 1) * 512],
                         start=(cc == 0), stop=(cc == 7))
                for cc in range(8)])(),
                deps=deps, tag=f"qv{tt}")
            op_qvev[tt] = P.add("vector", (lambda tt=tt, pq=pq: lambda e: [
                e.tensor_copy(out=q_tok[:, tt * 256:(tt + 1) * 256],
                              in_=pq[:, 0:256]),
                e.tensor_copy(
                    out=V2[:, tt * 260:(tt + 1) * 260].rearrange(
                        "p (h x) -> p h x", h=NHC)[:, :, 0:64],
                    in_=pq[:, 256:512].rearrange("p (h d) -> p h d", h=NHC)),
            ])(), deps=[op_qv[tt], op_v2ones], tag=f"qvev{tt}")
        for pair in range(2):
            kwar = op_kev.get((c5 - 1, pair))
            op_kg[(c5, pair)] = P.add("tensor", (lambda c5=c5, pair=pair,
                                                 xnb=xnb: lambda e: [
                e.matmul(out=pk[:, pair * 512:(pair + 1) * 512],
                         lhsT=Wkg[:, cc * 260 + pair * 128:
                                  cc * 260 + (pair + 1) * 128],
                         rhs=xnb[:, cc * 512:(cc + 1) * 512],
                         start=(cc == 0), stop=(cc == 7))
                for cc in range(8)])(),
                deps=trd + [op_wload, kwar], tag=f"k{c5}_{pair}")
            op_kev[(c5, pair)] = P.add("vector", (lambda c5=c5, pair=pair:
                                                  lambda e: [
                e.tensor_copy(out=Kfm[pair][:, c5 * 512:(c5 + 1) * 512],
                              in_=pk[:, pair * 512:(pair + 1) * 512])])(),
                deps=[op_kg[(c5, pair)]], tag=f"kev{c5}_{pair}")
        gwar = op_gev.get(c5 - 1) if c5 % 2 else op_lgev.get(c5 // 2 - 1)
        op_kg[(c5, "g")] = P.add("tensor", (lambda c5=c5, xnb=xnb: lambda e: [
            e.matmul(out=psU[2][0:4, 0:512],
                     lhsT=Wkg[:, cc * 260 + 256:cc * 260 + 260],
                     rhs=xnb[:, cc * 512:(cc + 1) * 512],
                     start=(cc == 0), stop=(cc == 7))
            for cc in range(8)])(),
            deps=trd + [op_wload, gwar, op_gev.get(c5 - 1)], tag=f"g{c5}")
        op_gev[c5] = P.add("vector", (lambda c5=c5: lambda e: [
            e.tensor_copy(out=E[0:4, 2048 + (c5 % 2) * 512:
                                2560 + (c5 % 2) * 512],
                          in_=psU[2][0:4, 0:512])])(),
            deps=[op_kg[(c5, "g")], (op_lgev.get(c5 // 2 - 1)
                                     if c5 % 2 == 0 else None)],
            tag=f"gev{c5}")

        if c5 % 2 == 0:
            continue
        ch = c5 // 2
        graw = E[0:4, 2048:3072]
        tscr = E[0:4, 3072:4096]
        lgr = E[0:4, 4096:5120]
        if ch == 0 and dbg:
            cap_g = P.add("vector", (lambda graw=graw: lambda e: [
                e.tensor_copy(out=dbg3[0:4, 1152:1408], in_=graw[:, 0:256])
            ])(), deps=[op_gev[c5]], tag="cap_g")
        elif ch == 0:
            cap_g = None
        op_sp1 = P.add("vector", (lambda: lambda e: [
            e.tensor_scalar(out=lgr, in0=graw, scalar1=0.0, scalar2=None,
                            op0=ALU.min),
        ])(), deps=[op_gev[c5], op_gev[c5 - 1]], tag=f"sp1_{ch}")
        op_sp2 = P.add("scalar", [
            (lambda tscr=tscr, graw=graw: lambda e:
             e.activation(out=tscr, in_=graw, func=AF.Abs))(),
            (lambda tscr=tscr: lambda e:
             e.activation(out=tscr, in_=tscr, func=AF.Exp, scale=-1.0))(),
            (lambda tscr=tscr: lambda e:
             e.activation(out=tscr, in_=tscr, func=AF.Ln,
                          bias=kcol[0:4, 1:2]))(),
        ], deps=[op_gev[c5], op_gev[c5 - 1], op_sp1, op_kcol],
            tag=f"sp2_{ch}", serial=True)
        if ch == 0 and dbg:
            cap_g2 = P.add("vector", (lambda lgr=lgr, tscr=tscr: lambda e: [
                e.tensor_copy(out=dbg3[0:4, 1408:1472], in_=lgr[:, 0:64]),
                e.tensor_copy(out=dbg3[0:4, 1472:1536], in_=tscr[:, 0:64]),
            ])(), deps=[op_sp2, op_sp1], tag="cap_g2")
        elif ch == 0:
            cap_g2 = None
        carry = E[0:4, 5120 + ch:5121 + ch]
        pcarry = E[0:4, 5119 + ch:5120 + ch]
        op_scan[ch] = P.add("vector", (
            [(lambda lgr=lgr, tscr=tscr: lambda e:
              e.tensor_sub(out=lgr, in0=lgr, in1=tscr))(),
             (lambda graw=graw, lgr=lgr, pcarry=pcarry, ch=ch: lambda e:
              e.tensor_tensor_scan(out=graw, data0=E[0:4, 5376:6400],
                                   data1=lgr,
                                   initial=(pcarry if ch > 0 else 0.0),
                                   op0=ALU.add, op1=ALU.add))()] +
            [(lambda carry=carry, graw=graw: lambda e:
              e.tensor_copy(out=carry, in_=graw[:, 1023:1024]))()]
        ), deps=[op_sp2, op_zeros, (cap_g if ch == 0 else None),
                 (cap_g2 if ch == 0 else None)],
            tag=f"scan{ch}", serial=True)
        op_lgtr = P.add("tensor", (lambda graw=graw: lambda e: [
            e.transpose(out=psA[0][:, tl2 * 4:tl2 * 4 + 4],
                        in_=graw[:, tl2 * 128:(tl2 + 1) * 128],
                        identity=s_idf[0:4, 0:4])
            for tl2 in range(8)])(),
            deps=[op_scan[ch], op_gev[c5], op_lgev.get(ch - 1)],
            tag=f"lgtr{ch}")
        op_lgev[ch] = P.add("vector", (lambda ch=ch: lambda e: [
            e.tensor_copy(out=LGcol[:, ch * 32:(ch + 1) * 32],
                          in_=psA[0][:, 0:32])])(),
            deps=[op_lgtr], tag=f"lgev{ch}")

    lg_done = op_lgev[NTCH - 1]
    a_done_v = lg_done
    a_done_t = op_kg[(NTCH - 1, "g")]

    op_cx = P.add("sync", lambda e: (
        [e.dma_start(out=crow[0:1, h * NT:(h + 1) * NT],
                     in_=LGcol[64:65, h::4][:, 0:NT],
                     allow_slow_non_contiguous=True) for h in range(NHC)] +
        [e.dma_start(out=Chalf[0:1, h * NSUP:(h + 1) * NSUP],
                     in_=LGcol[0:1, 4 + h::8][:, 0:NSUP],
                     allow_slow_non_contiguous=True) for h in range(NHC)]
    ), deps=[lg_done], dma=8, tag="cx")
    crv = crow[0:1, :].rearrange("a (h i) -> a h i", h=NHC)
    Erv = Erow[0:1, :].rearrange("a (h i) -> a h i", h=NHC)
    Chv = Chalf[0:1, :].rearrange("a (h i) -> a h i", h=NHC)
    op_erow = P.add("vector", [
        lambda e: e.tensor_sub(out=Erv[:, :, 0::2], in0=crv[:, :, 0::2],
                               in1=Chv[:]),
        lambda e: e.tensor_sub(out=Erv[:, :, 1::2], in0=crv[:, :, 1::2],
                               in1=Chv[:]),
        lambda e: e.tensor_scalar(out=Chalf[:], in0=Chalf[:], scalar1=0.5,
                                  scalar2=None, op0=ALU.mult),
    ], deps=[op_cx], tag="erow", serial=True)
    op_frow = P.add("scalar", lambda e: [
        e.activation(out=Frow[:], in_=Erow[:], func=AF.Exp, scale=0.5),
    ], deps=[op_erow], tag="frow")


    Frv = Frow[0:1, :].rearrange("a (h i) -> a h i", h=NHC)
    op_fbc = P.add("tensor", lambda e: [
        e.matmul(out=psU[0][:, 0:4 * NT], lhsT=s_ones[:], rhs=Frow[0:1, :],
                 start=True, stop=True),
        e.matmul(out=psU[1][:, 0:4 * NSUP], lhsT=s_ones[:], rhs=Chalf[0:1, :],
                 start=True, stop=True),
    ], deps=[op_frow, op_qvev[NT - 1], op_lgev[NTCH - 1], op_erow],
        tag="fbc")
    op_fbcev = P.add("vector", lambda e: [
        e.tensor_copy(out=Fbc[:], in_=psU[0][:, 0:4 * NT]),
        e.tensor_copy(out=Cbc[:], in_=psU[1][:, 0:4 * NSUP]),
    ], deps=[op_fbc], tag="fbcev")

    op_S, op_evA, op_evD, op_mask, op_PV = {}, {}, {}, {}, {}
    op_yr, op_ysc, op_ydma = {}, {}, {}
    op_ssI, op_qsc, op_qtr = {}, {}, {}
    prev_yr_last, prev_ysc_last = None, None
    prev_ydma = None
    prev_uev = None
    last_evD = {}

    def emit_prep(I):
        pi = I % 2
        jmax = 2 * I + 2
        op_ssI[I] = P.add("scalar", (lambda I=I, jmax=jmax, pi=pi: lambda e: [
            e.activation(out=SS[pi][:, h * NT:h * NT + jmax],
                         in_=LGcol[:, h::4][:, 0:jmax],
                         func=AF.Exp, scale=-0.5,
                         bias=Cbc[:, h * NSUP + I:h * NSUP + I + 1])
            for h in range(NHC)])(),
            deps=[op_fbcev, op_ssI.get(I - 2),
                  op_evA.get((I - 2, 2 * I - 3)) if I >= 2 else None,
                  last_evD.get(I - 2)],
            tag=f"ss{I}")
        op_qsc[I] = P.add("vector", (lambda I=I, pi=pi: lambda e: [
            e.tensor_scalar(
                out=qt_sc[pi][:, (h // 2) * 256 + sub * 128 + (h % 2) * 64:
                              (h // 2) * 256 + sub * 128 + (h % 2) * 64 + 64],
                in0=q_tok[:, (2 * I + sub) * 256 + h * 64:
                          (2 * I + sub) * 256 + (h + 1) * 64],
                scalar1=Fbc[:, h * NT + 2 * I + sub:h * NT + 2 * I + sub + 1],
                scalar2=None, op0=ALU.mult)
            for sub in range(2) for h in range(NHC)])(),
            deps=[op_fbcev, op_qtr.get(I - 2)], tag=f"qsc{I}")
        op_qtr[I] = P.add("scalar", (lambda I=I, pi=pi: lambda e: [
            e.dma_start(out=qfm[pi][p][:].rearrange("p (s t) -> p s t", s=2),
                        in_=qt_sc[pi][:, p * 256:(p + 1) * 256].rearrange(
                            "p (s q) -> p s q", s=2),
                        transpose=True)
            for p in range(2)])(),
            deps=[op_qsc[I],
                  op_S.get((I - 2, 2 * I - 3)) if I >= 2 else None],
            dma=2, tag=f"qtr{I}")

    def mk_S(I, j):
        jmax = 2 * I + 2
        par = j % 2
        if par == 0:
            bA = lambda hq: psA[0][:, hq * 256:(hq + 1) * 256]
            bB = lambda hq: psA[1][:, hq * 256:(hq + 1) * 256]
        else:
            bA = lambda hq: pk[:, hq * 256:(hq + 1) * 256]
            bB = lambda hq: pk[:, 512 + hq * 256:512 + (hq + 1) * 256]
        sdeps = [op_qtr[I], op_evA.get((I, j - 2)), op_evD.get((I, j - 2))]
        if j < 2 and I > 0:
            sdeps += [op_evA.get((I - 1, 2 * I - 2 + j)),
                      op_evD.get((I - 1, 2 * I - 2 + j))]
            if j == 1:
                sdeps.append(op_yr.get((I - 1, 7)))
                sdeps.append(op_ysc.get((I - 1, 7)))
        if I == 0 and j == 0:
            sdeps += [op_qvev[NT - 1], op_lgev[NTCH - 1]]
        pi = I % 2
        op_S[(I, j)] = P.add("tensor", (lambda I=I, j=j, bA=bA, bB=bB, pi=pi:
                                        lambda e: [
            m for p in range(2) for m in (
                e.matmul(out=bA(p),
                         lhsT=Kfm[p][0:64, j * 128:(j + 1) * 128],
                         rhs=qfm[pi][p][0:64, :], start=True, stop=True,
                         tile_position=(0, 0)),
                e.matmul(out=bB(p),
                         lhsT=Kfm[p][64:128, j * 128:(j + 1) * 128],
                         rhs=qfm[pi][p][64:128, :], start=True, stop=True,
                         tile_position=(64, 0)),
            )])(), deps=sdeps, tag=f"S{I}_{j}")
        return bA, bB

    sbanks = {}
    pair_order = []
    emit_prep(0)
    emit_prep(1)
    for I in range(NSUP):
        jmax = 2 * I + 2
        pi = I % 2
        if (I, 0) not in sbanks:
            sbanks[(I, 0)] = mk_S(I, 0)
        for j in range(jmax):
            par = len(pair_order) % 4
            pair_order.append((I, j))
            if j + 1 < jmax:
                sbanks[(I, j + 1)] = mk_S(I, j + 1)
            bA, bB = sbanks[(I, j)]
            if j == 1 and I + 1 < NSUP and (I + 1) not in op_ssI:
                emit_prep(I + 1)
            evdeps = [op_S[(I, j)]]
            if len(pair_order) >= 5:
                evdeps.append(op_PV[pair_order[-5]])
            so2 = 128 if j == 2 * I + 1 else 0
            op_evA[(I, j)] = P.add("scalar", (lambda I=I, j=j, par=par,
                                              so2=so2, bA=bA, pi=pi:
                                              lambda e: [
                e.activation(out=wt[:, par * 1024 + h * 256 + so2:
                                   par * 1024 + (h + 1) * 256],
                             in_=bA(h // 2)[:, so2:],
                             func=AF.Square,
                             scale=SS[pi][:, h * NT + j:h * NT + j + 1])
                for h in (0, 2)])(),
                deps=evdeps + [op_ssI[I]], tag=f"evA{I}_{j}")
            op_evD0 = P.add("vector", (lambda I=I, j=j, par=par,
                                       so2=so2, bB=bB, pi=pi: lambda e: [
                e.tensor_scalar(out=wt[:, par * 1024 + h * 256 + so2:
                                      par * 1024 + (h + 1) * 256],
                                in0=bB(h // 2)[:, so2:],
                                scalar1=SS[pi][:, h * NT + j:h * NT + j + 1],
                                scalar2=None, op0=ALU.mult)
                for h in (1, 3)] + [
                e.tensor_mul(out=wt[:, par * 1024 + h * 256 + so2:
                                   par * 1024 + (h + 1) * 256],
                             in0=wt[:, par * 1024 + h * 256 + so2:
                                   par * 1024 + (h + 1) * 256],
                             in1=wt[:, par * 1024 + h * 256 + so2:
                                   par * 1024 + (h + 1) * 256])
                for h in (1, 3)])(),
                deps=evdeps + [op_ssI[I]], tag=f"evD0{I}_{j}")
            op_evD[(I, j)] = op_evD0
            last_evD[I] = op_evD0
            pvdeps = [op_evA[(I, j)], op_evD0]
            if j == 2 * I:
                op_mask[(I, j)] = P.add("gpsimd", (lambda par=par: lambda e: [
                    e.tensor_mul(out=wt[:, par * 1024 + h * 256:
                                       par * 1024 + h * 256 + 128],
                                 in0=wt[:, par * 1024 + h * 256:
                                       par * 1024 + h * 256 + 128],
                                 in1=s_triu[:])
                    for h in range(NHC)])(),
                    deps=[op_evA[(I, j)], op_evD[(I, j)]], tag=f"mk{I}_{j}")
                pvdeps.append(op_mask[(I, j)])
            elif j == 2 * I + 1:
                op_mask[(I, j)] = P.add("gpsimd", (lambda par=par: lambda e: [
                    x for h in range(NHC) for x in (
                        e.memset(wt[:, par * 1024 + h * 256:
                                   par * 1024 + h * 256 + 128], 0.0),
                        e.tensor_mul(out=wt[:, par * 1024 + h * 256 + 128:
                                           par * 1024 + (h + 1) * 256],
                                     in0=wt[:, par * 1024 + h * 256 + 128:
                                           par * 1024 + (h + 1) * 256],
                                     in1=s_triu[:]),
                    )])(),
                    deps=[op_evA[(I, j)], op_evD[(I, j)]], tag=f"mk{I}_{j}")
                pvdeps.append(op_mask[(I, j)])
            if j == 0:
                pvdeps.append(prev_uev)
                pvdeps.append(op_fbcev)
                if I == 0:
                    pvdeps.append(op_lgev[NTCH - 1])
            op_PV[(I, j)] = P.add("tensor", (lambda I=I, j=j, par=par,
                                             jmax=jmax: lambda e: [
                e.matmul(out=psU[h][0:65, 0:256],
                         lhsT=V2[:, j * 260 + h * 65:j * 260 + (h + 1) * 65],
                         rhs=wt[:, par * 1024 + h * 256:
                               par * 1024 + (h + 1) * 256],
                         start=(j == 0), stop=(j == jmax - 1))
                for h in range(NHC)])(),
                deps=pvdeps, tag=f"PV{I}_{j}")

        op_uev = P.add("scalar", (lambda: lambda e: [
            e.activation(out=Uev[0:65, h * 256:(h + 1) * 256],
                         in_=psU[h][0:65, 0:256], func=AF.Copy)
            for h in range(NHC)])(),
            deps=[op_PV[(I, jmax - 1)]], tag=f"uev{I}")
        prev_uev = op_uev
        if I + 2 < NSUP and (I + 2) not in op_ssI:
            emit_prep(I + 2)
        if I + 1 < NSUP:
            sbanks[(I + 1, 0)] = mk_S(I + 1, 0)
        op_ut = {}
        for kidx in range(8):
            h = kidx // 2
            sub = kidx % 2
            ping = kidx % 2
            utdeps = [op_uev, op_yr.get((I, kidx - 2)),
                      op_ysc.get((I, kidx - 2)),
                      (prev_yr_last if I > 0 and kidx < 2 else None),
                      (prev_ysc_last if I > 0 and kidx < 2 else None),
                      op_evA.get((I, jmax - 1)), op_evD.get((I, jmax - 1))]
            op_ut[kidx] = P.add("tensor", (lambda h=h, sub=sub,
                                           ping=ping: lambda e: [
                e.transpose(out=pk[:, ping * 512:ping * 512 + 65],
                            in_=Uev[0:65, h * 256 + sub * 128:
                                    h * 256 + (sub + 1) * 128],
                            identity=s_idf[0:65, 0:65])])(),
                deps=utdeps, tag=f"ut{I}_{kidx}")
            op_yr[(I, kidx)] = P.add("vector", (lambda ping=ping: lambda e: [
                e.reciprocal(out=rncol[:, ping:ping + 1],
                             in_=pk[:, ping * 512 + 64:ping * 512 + 65])])(),
                deps=[op_ut[kidx], op_ysc.get((I, kidx - 2))],
                tag=f"yr{I}_{kidx}")
            op_ysc[(I, kidx)] = P.add("scalar", (lambda h=h, sub=sub,
                                                 ping=ping: lambda e: [
                e.activation(out=ybuf[:, sub * 256 + h * 64:
                                     sub * 256 + (h + 1) * 64],
                             in_=pk[:, ping * 512:ping * 512 + 64],
                             func=AF.Copy,
                             scale=rncol[:, ping:ping + 1])])(),
                deps=[op_yr[(I, kidx)], prev_ydma], tag=f"ysc{I}_{kidx}")
        prev_yr_last = op_yr[(I, 7)]
        prev_ysc_last = op_ysc[(I, 7)]
        op_ydma[I] = P.add("sync", (lambda I=I: lambda e: [
            e.dma_start(out=cc_in_h.ap()[256 * I:256 * (I + 1), :]
                        .rearrange("(i p) o -> p i o", p=128),
                        in_=ybuf[:].rearrange("p (i o) -> p i o", i=2))])(),
            deps=[op_ysc[(I, k)] for k in range(8)], dma=1, tag=f"ydma{I}")
        prev_ydma = op_ydma[I]

    op_cc = P.add("gpsimd", (lambda: lambda e: [
        e.collective_compute(
            "AllGather", ALU.bypass,
            replica_groups=[[0, 1, 2, 3], [4, 5, 6, 7]],
            ins=[cc_in_h.ap().opt()],
            outs=[cc_out_h.ap().opt()])])(),
        deps=[op_ydma[NSUP - 1]], tag="cc")

    op_cap = prev_yr_last
    if dbg:
        op_cap = P.add("vector", (lambda: lambda e: [
        e.tensor_copy(out=dbg2[:, 0:NT], in_=statS[:, 0:NT]),
        e.tensor_copy(out=dbg2[:, NT:2 * NT], in_=statB[:, 0:NT]),
        e.tensor_copy(out=dbg2[:, 2 * NT:2 * NT + 256],
                      in_=q_tok[:, 0:256]),
        e.tensor_copy(out=dbg2[:, 2 * NT + 256:2 * NT + 512],
                      in_=Kfm[0][:, 0:256]),
        e.tensor_copy(out=dbg2[:, 2 * NT + 512:2 * NT + 772],
                      in_=V2[:, 0:260]),
        e.tensor_copy(out=dbg2[:, 2 * NT + 772:2 * NT + 772 + NHC * NT],
                      in_=SS[0][:]),
            e.tensor_copy(out=dbg2[:, 2 * NT + 772 + NHC * NT:
                                  2 * NT + 772 + NHC * NT + 512],
                          in_=wt[:, 0:512]),
        ])(), deps=[prev_yr_last, op_ssI[NSUP - 1]], tag="cap")

    b_last = {
        "tensor": op_ut[7], "vector": op_cap,
        "scalar": op_ysc[(NSUP - 1, 7)],
        "sync": op_ydma[NSUP - 1], "gpsimd": op_cc,
    }
    ab.close()

    cst = ExitStack()

    def sbc(name, shape, dt):
        return cst.enter_context(nc.sbuf_tensor(name, shape, dt))

    Wr_sb = sbc("Wr_sb", [128, 8 * 1024], BF16)
    Wk_sb = sbc("Wk_sb", [128, 8 * 1024], BF16)
    Wv_sb = sbc("Wv_sb", [128, 8 * 1024], BF16)
    xn2fm = sbc("xn2fm", [128, 8 * CHALF], BF16)
    prevcol = sbc("prevcol", [128, 8], BF16)
    xrfm = sbc("xrfm", [128, 8 * CHALF], BF16)
    xkfm = sbc("xkfm", [128, 8 * CHALF], BF16)
    rslab = sbc("rslab", [128, TPH * 1024], BF16)
    kact = sbc("kact", [128, 8 * CHALF], BF16)
    y_sb = sbc("y_sb", [128, TPH * 1024], BF16)
    pcR = cst.enter_context(nc.psum_tensor("pcR", [128, 1024], F32))
    pcK = cst.enter_context(nc.psum_tensor("pcK", [128, 1024], F32))
    pbnd = cst.enter_context(nc.psum_tensor("pbnd", [128, 8], F32))

    blast = list(b_last.values())
    op_wffn = P.add("gpsimd", lambda e: [
        e.dma_start(out=Wr_sb[:].rearrange("p (cc o) -> p cc o", cc=8),
                    in_=wr_d.rearrange("(cc p) o -> p cc o", p=128)),
        e.dma_start(out=Wk_sb[:].rearrange("p (cc o) -> p cc o", cc=8),
                    in_=wk_d.rearrange("(cc p) o -> p cc o", p=128)),
        e.dma_start(out=Wv_sb[:].rearrange("p (cc o) -> p cc o", cc=8),
                    in_=wv_d.rearrange("(cc p) o -> p cc o", p=128)),
    ], deps=blast, dma=3, tag="wffn")

    xn2v = xn2fm[:].rearrange("p (cc t) -> p cc t", cc=8)
    prev_fin, prev_out, prev_tr2_last, prev_o2ev = None, None, None, None
    prev_norm2 = {}
    prev_mix_last = None
    for hf in range(NHALF):
        ysdeps = [op_cc, b_last["sync"], prev_fin]

        def mk_ysb(hf=hf):
            def f(e, sem):
                nd = 4 + (4 if hf == 0 else 0)
                for arm in e.switch_core_id(n=128):
                    q = arm.logical % 4
                    t0g = q * SLICE + hf * CHALF
                    for r in range(4):
                        ins = e.dma_start(
                            out=y_sb[:].rearrange(
                                "p (i x) -> p i x",
                                i=TPH)[:, :, r * 256:(r + 1) * 256],
                            in_=cc_out_h.ap()[r * T + t0g:r * T + t0g + CHALF,
                                              :].rearrange(
                                "(i p) o -> p i o", p=128))
                        ins.then_inc(sem, 16)
                    if hf == 0:
                        for r in range(4):
                            if q == 0:
                                ins = e.dma_start(out=yscr[:],
                                                  in_=cc_out_h.ap()[r * T:
                                                                    r * T + 1, :])
                            else:
                                ins = e.dma_start(
                                    out=ybnd[0:1, r * 256:(r + 1) * 256],
                                    in_=cc_out_h.ap()[r * T + t0g - 1:
                                                      r * T + t0g, :])
                            ins.then_inc(sem, 16)
                return 16 * nd
            return f
        op_ysb = P.add("sync", mk_ysb(), deps=ysdeps + [op_ybnd0],
                       manual=16 * (8 if hf == 0 else 4), tag=f"ysb{hf}")

        if hf == 0:
            op_x1b = P.add("vector", lambda e: [
                e.tensor_add(out=x1b[:], in0=xb0s[:], in1=ybnd[:])],
                deps=[op_ysb, op_const, blast[1]], tag="x1b")
            op_bred = P.add("vector", lambda e: [
                e.tensor_reduce(out=bnds[0:1, 0:1], in_=x1b[:], axis=AX.X,
                                op=ALU.add)], deps=[op_x1b], tag="bred")
            op_bsq = P.add("scalar", lambda e: [
                e.activation(out=rslab[0:1, 0:1024], in_=x1b[:], func=AF.Square,
                             accum_out=bnds[0:1, 1:2])],
                deps=[op_x1b, b_last["scalar"]], tag="bsq")
            op_bst = P.add("vector", [
                lambda e: e.tensor_scalar(out=bnds[0:1, 2:3],
                                          in0=bnds[0:1, 0:1],
                                          scalar1=1.0 / C, scalar2=None,
                                          op0=ALU.mult),
                lambda e: e.tensor_scalar(out=bnds[0:1, 3:4],
                                          in0=bnds[0:1, 1:2],
                                          scalar1=1.0 / C, scalar2=None,
                                          op0=ALU.mult),
                lambda e: e.tensor_mul(out=bnds[0:1, 4:5], in0=bnds[0:1, 2:3],
                                       in1=bnds[0:1, 2:3]),
                lambda e: e.tensor_sub(out=bnds[0:1, 3:4], in0=bnds[0:1, 3:4],
                                       in1=bnds[0:1, 4:5]),
            ], deps=[op_bred, op_bsq], tag="bst", serial=True)
            op_bs2 = P.add("scalar", [
                lambda e: e.activation(out=bnds[0:1, 3:4], in_=bnds[0:1, 3:4],
                                       func=AF.Ln, bias=kcol[0:1, 0:1]),
                lambda e: e.activation(out=bnds[0:1, 5:6], in_=bnds[0:1, 3:4],
                                       func=AF.Exp, scale=-0.5),
            ], deps=[op_bst], tag="bs2", serial=True)
            op_bs3 = P.add("vector", [
                lambda e: e.tensor_scalar(out=bnds[0:1, 6:7],
                                          in0=bnds[0:1, 2:3], scalar1=-1.0,
                                          scalar2=None, op0=ALU.mult),
                lambda e: e.tensor_mul(out=bnds[0:1, 7:8], in0=bnds[0:1, 6:7],
                                       in1=bnds[0:1, 5:6]),
            ], deps=[op_bs2], tag="bs3", serial=True)
            op_bnrm = P.add("vector", lambda e: [
                e.tensor_scalar(out=xn2b[:], in0=x1b[:],
                                scalar1=bnds[0:1, 5:6],
                                scalar2=bnds[0:1, 7:8],
                                op0=ALU.mult, op1=ALU.add)],
                deps=[op_bs3], tag="bnrm")
            op_btr = P.add("tensor", lambda e: [
                e.transpose(out=pbnd[:, cc:cc + 1],
                            in_=xn2b[0:1, cc * 128:(cc + 1) * 128],
                            identity=s_idf[0:1, 0:1])
                for cc in range(8)],
                deps=[op_bnrm, b_last["tensor"]], tag="btr")
            op_bev = P.add("vector", lambda e: [
                e.tensor_copy(out=prevcol[:], in_=pbnd[:, 0:8])],
                deps=[op_btr], tag="bev")
        else:
            op_bev = P.add("vector", lambda e: [
                e.tensor_copy(out=prevcol[:].rearrange("p (c x) -> p c x", x=1),
                              in_=xn2v[:, :, CHALF - 1:CHALF])],
                deps=[prev_tr2_last, prev_o2ev], tag=f"bcp{hf}")

        op_x1t, op_n2, op_tr2 = {}, {}, {}
        op_r2d, op_q2d = {}, {}
        for tl in range(TPH):
            gt = hf * TPH + tl
            par = tl % 2
            xt = E[:, par * 1024:(par + 1) * 1024]
            xd = P.add("sync", (lambda xt=xt, gt=gt: lambda e: [
                e.dma_start(out=xt, in_=xsl[gt * 128:(gt + 1) * 128, :])])(),
                deps=[op_x1t.get(tl - 2) if tl >= 2 else
                      (prev_fin if hf else b_last["scalar"]),
                      (b_last["vector"] if hf == 0 and tl < 2 else None)],
                dma=1, tag=f"cx{hf}_{tl}")
            op_x1t[tl] = P.add("vector", (lambda xt=xt, tl=tl: lambda e: [
                e.tensor_add(out=y_sb[:, tl * 1024:(tl + 1) * 1024], in0=xt,
                             in1=y_sb[:, tl * 1024:(tl + 1) * 1024])])(),
                deps=[xd, op_ysb,
                      (b_last["vector"] if hf == 0 and tl < 2 else None)],
                tag=f"x1t{hf}_{tl}")
            op_r2d[tl] = P.add("vector", (lambda tl=tl: lambda e: [
                e.tensor_reduce(out=statR[:, tl:tl + 1],
                                in_=y_sb[:, tl * 1024:(tl + 1) * 1024],
                                axis=AX.X, op=ALU.add)])(),
                deps=[op_x1t[tl]], tag=f"r2{hf}_{tl}")
            op_q2d[tl] = P.add("scalar", (lambda tl=tl: lambda e: [
                e.activation(out=rslab[:, 0:1024],
                             in_=y_sb[:, tl * 1024:(tl + 1) * 1024],
                             func=AF.Square,
                             accum_out=statQ[:, tl:tl + 1])])(),
                deps=[op_x1t[tl],
                      (op_bsq if hf == 0 and tl == 0 else None)],
                tag=f"q2{hf}_{tl}")
        op_s2 = P.add("vector", [
            lambda e: e.tensor_scalar(out=statM[:, 0:TPH], in0=statR[:, 0:TPH],
                                      scalar1=1.0 / C, scalar2=None,
                                      op0=ALU.mult),
            lambda e: e.tensor_scalar(out=statQ[:, 0:TPH], in0=statQ[:, 0:TPH],
                                      scalar1=1.0 / C, scalar2=None,
                                      op0=ALU.mult),
            lambda e: e.tensor_mul(out=statR[:, 0:TPH], in0=statM[:, 0:TPH],
                                   in1=statM[:, 0:TPH]),
            lambda e: e.tensor_sub(out=statQ[:, 0:TPH], in0=statQ[:, 0:TPH],
                                   in1=statR[:, 0:TPH]),
        ], deps=[op_r2d[TPH - 1], op_q2d[TPH - 1]],
            tag=f"s2{hf}", serial=True)
        op_s2s = P.add("scalar", [
            lambda e: e.activation(out=statQ[:, 0:TPH], in_=statQ[:, 0:TPH],
                                   func=AF.Ln, bias=kcol[:, 0:1]),
            lambda e: e.activation(out=statS[:, 0:TPH], in_=statQ[:, 0:TPH],
                                   func=AF.Exp, scale=-0.5),
        ], deps=[op_s2], tag=f"s2s{hf}", serial=True)
        op_s2b = P.add("vector", [
            lambda e: e.tensor_scalar(out=statM[:, 0:TPH], in0=statM[:, 0:TPH],
                                      scalar1=-1.0, scalar2=None,
                                      op0=ALU.mult),
            lambda e: e.tensor_mul(out=statB[:, 0:TPH], in0=statM[:, 0:TPH],
                                   in1=statS[:, 0:TPH]),
        ], deps=[op_s2s], tag=f"s2b{hf}", serial=True)
        for tl in range(TPH):
            par = tl % 2
            op_n2[tl] = P.add("vector", (lambda tl=tl, par=par: lambda e: [
                e.tensor_scalar(out=xrfm[:, (1 - par) * 1024:
                                         (2 - par) * 1024],
                                in0=y_sb[:, tl * 1024:(tl + 1) * 1024],
                                scalar1=statS[:, tl:tl + 1],
                                scalar2=statB[:, tl:tl + 1],
                                op0=ALU.mult, op1=ALU.add)])(),
                deps=[op_s2b, op_tr2.get(tl - 2)], tag=f"n2{hf}_{tl}")
            trdeps = [op_n2[tl]]
            if tl == 0:
                trdeps.append(prev_mix_last)
            if tl == TPH - 1:
                trdeps.append(op_bev)
            op_tr2[tl] = P.add("sync", (lambda tl=tl, par=par: lambda e: [
                e.dma_start(out=xn2fm[:].rearrange(
                                "p (c t) -> p c t",
                                c=8)[:, :, tl * 128:(tl + 1) * 128],
                            in_=xrfm[:, (1 - par) * 1024:(2 - par) * 1024]
                            .rearrange("p (c q) -> p c q", c=8),
                            transpose=True)])(),
                deps=trdeps, dma=1, tag=f"tr2{hf}_{tl}")
        prev_norm2 = op_n2
        prev_tr2_last = op_tr2[TPH - 1]

        op_mix = {}
        for cc in range(8):
            op_mix[cc] = P.add("vector", (lambda cc=cc: lambda e: [
                e.tensor_sub(out=xrfm[:, cc * CHALF:cc * CHALF + 1],
                             in0=xn2fm[:, cc * CHALF:cc * CHALF + 1],
                             in1=prevcol[:, cc:cc + 1]),
                e.scalar_tensor_tensor(
                    out=xkfm[:, cc * CHALF:cc * CHALF + 1],
                    in0=xrfm[:, cc * CHALF:cc * CHALF + 1],
                    scalar=s_muk[:, cc:cc + 1],
                    in1=prevcol[:, cc:cc + 1], op0=ALU.mult, op1=ALU.add),
                e.scalar_tensor_tensor(
                    out=xrfm[:, cc * CHALF:cc * CHALF + 1],
                    in0=xrfm[:, cc * CHALF:cc * CHALF + 1],
                    scalar=s_mur[:, cc:cc + 1],
                    in1=prevcol[:, cc:cc + 1], op0=ALU.mult, op1=ALU.add),
                e.tensor_sub(out=xrfm[:, cc * CHALF + 1:(cc + 1) * CHALF],
                             in0=xn2fm[:, cc * CHALF + 1:(cc + 1) * CHALF],
                             in1=xn2fm[:, cc * CHALF:(cc + 1) * CHALF - 1]),
                e.scalar_tensor_tensor(
                    out=xkfm[:, cc * CHALF + 1:(cc + 1) * CHALF],
                    in0=xrfm[:, cc * CHALF + 1:(cc + 1) * CHALF],
                    scalar=s_muk[:, cc:cc + 1],
                    in1=xn2fm[:, cc * CHALF:(cc + 1) * CHALF - 1],
                    op0=ALU.mult, op1=ALU.add),
                e.scalar_tensor_tensor(
                    out=xrfm[:, cc * CHALF + 1:(cc + 1) * CHALF],
                    in0=xrfm[:, cc * CHALF + 1:(cc + 1) * CHALF],
                    scalar=s_mur[:, cc:cc + 1],
                    in1=xn2fm[:, cc * CHALF:(cc + 1) * CHALF - 1],
                    op0=ALU.mult, op1=ALU.add),
            ])(), deps=[op_tr2[tl2] for tl2 in range(TPH)] + [op_bev,
                        op_const, prev_o2ev], tag=f"mix{hf}_{cc}")
        mixall = [op_mix[cc] for cc in range(8)]
        prev_mix_last = op_mix[7]

        op_rp, op_rev = {}, {}
        for tl in range(TPH):
            pr = pcR if tl % 2 == 0 else pcK
            op_rp[tl] = P.add("tensor", (lambda tl=tl, pr=pr: lambda e: [
                m for cc in range(8) for m in (
                    e.matmul(out=pr[:, 0:512],
                             lhsT=xrfm[:, cc * CHALF + tl * 128:
                                      cc * CHALF + (tl + 1) * 128],
                             rhs=Wr_sb[:, cc * 1024:cc * 1024 + 512],
                             start=(cc == 0), stop=(cc == 7)),
                    e.matmul(out=pr[:, 512:1024],
                             lhsT=xrfm[:, cc * CHALF + tl * 128:
                                      cc * CHALF + (tl + 1) * 128],
                             rhs=Wr_sb[:, cc * 1024 + 512:(cc + 1) * 1024],
                             start=(cc == 0), stop=(cc == 7)),
                )])(), deps=mixall + [op_wffn, op_rev.get(tl - 2),
                                      (b_last["tensor"] if hf == 0 and tl < 2
                                       else None),
                                      (op_btr if hf == 0 else None)],
                tag=f"rp{hf}_{tl}")
            op_rev[tl] = P.add("scalar", (lambda tl=tl, pr=pr: lambda e: [
                e.activation(out=rslab[:, tl * 1024:(tl + 1) * 1024],
                             in_=pr[:], func=AF.Sigmoid)])(),
                deps=[op_rp[tl]], tag=f"rev{hf}_{tl}")

        op_kp, op_ksq = {}, {}
        for dc in range(8):
            pr = pcR if (TPH + dc) % 2 == 0 else pcK
            nspl = max(1, CHALF // 512)
            wspl = CHALF // nspl
            op_kp[dc] = P.add("tensor", (lambda dc=dc, pr=pr, nspl=nspl,
                                         wspl=wspl: lambda e: [
                e.matmul(out=pr[:, sp * wspl:(sp + 1) * wspl],
                         lhsT=Wk_sb[:, cc * 1024 + dc * 128:
                                   cc * 1024 + (dc + 1) * 128],
                         rhs=xkfm[:, cc * CHALF + sp * wspl:
                                 cc * CHALF + (sp + 1) * wspl],
                         start=(cc == 0), stop=(cc == 7))
                for cc in range(8) for sp in range(nspl)])(),
                deps=mixall + [op_wffn,
                                      op_ksq.get(dc - 2) or op_rev.get(TPH - 2 + dc),
                                      op_rev.get(TPH - 1) if dc == 1 else None],
                tag=f"kp{hf}_{dc}")
            op_krl = P.add("scalar", (lambda dc=dc, pr=pr: lambda e: [
                e.activation(out=kact[:, dc * CHALF:(dc + 1) * CHALF],
                             in_=pr[:, 0:CHALF], func=AF.Relu)])(),
                deps=[op_kp[dc]], tag=f"krl{hf}_{dc}")
            op_ksq[dc] = P.add("vector", (lambda dc=dc: lambda e: [
                e.tensor_mul(out=kact[:, dc * CHALF:(dc + 1) * CHALF],
                             in0=kact[:, dc * CHALF:(dc + 1) * CHALF],
                             in1=kact[:, dc * CHALF:(dc + 1) * CHALF])])(),
                deps=[op_krl], tag=f"ksq{hf}_{dc}")

        fins = {}
        po_half = prev_fouts if hf else {}
        fouts = {}
        for tl in range(TPH):
            pr = pcR if (TPH + 8 + tl) % 2 == 0 else pcK
            o2deps = [op_ksq[dc] for dc in range(8)] + [op_wffn]
            o2deps.append(prev_o2ev if tl < 2 else None)
            o2deps.append(op_krl if tl == 0 else None)
            o2deps.append(fins.get(tl - 2))
            op_o2 = P.add("tensor", (lambda tl=tl, pr=pr: lambda e: [
                m for dc in range(8) for m in (
                    e.matmul(out=pr[:, 0:512],
                             lhsT=kact[:, dc * CHALF + tl * 128:
                                      dc * CHALF + (tl + 1) * 128],
                             rhs=Wv_sb[:, dc * 1024:dc * 1024 + 512],
                             start=(dc == 0), stop=(dc == 7)),
                    e.matmul(out=pr[:, 512:1024],
                             lhsT=kact[:, dc * CHALF + tl * 128:
                                      dc * CHALF + (tl + 1) * 128],
                             rhs=Wv_sb[:, dc * 1024 + 512:(dc + 1) * 1024],
                             start=(dc == 0), stop=(dc == 7)),
                )])(), deps=o2deps, tag=f"o2{hf}_{tl}")
            fpar = tl % 2
            fb = E[:, 2048 + fpar * 1024:3072 + fpar * 1024]
            op_fin = P.add("vector", [
                (lambda tl=tl, pr=pr, fb=fb: lambda e: e.tensor_mul(
                    out=fb,
                    in0=rslab[:, tl * 1024:(tl + 1) * 1024], in1=pr[:]))(),
                (lambda tl=tl, fb=fb: lambda e: e.tensor_add(
                    out=fb, in0=fb,
                    in1=y_sb[:, tl * 1024:(tl + 1) * 1024]))(),
            ], deps=[op_o2, op_rev[tl], fouts.get(tl - 2),
                     (po_half.get(TPH - 2 + tl) if tl < 2 else None)],
                tag=f"fin{hf}_{tl}", serial=True)
            fins[tl] = op_fin
            prev_o2ev = op_fin
            gt = hf * TPH + tl
            prev_out = P.add("sync", (lambda gt=gt, fb=fb: lambda e: [
                e.dma_start(out=out[gt * 128:(gt + 1) * 128, :],
                            in_=fb)])(),
                deps=[op_fin], dma=1, tag=f"odma{hf}_{tl}")
            fouts[tl] = prev_out
            prev_fin = op_fin
        prev_fouts = fouts

    if dbg:
        cap_c = P.add("vector", lambda e: [
            e.tensor_copy(out=dbg2[:, 1400:1528], in_=xn2fm[:, 0:128]),
            e.tensor_copy(out=dbg2[:, 1528:1656], in_=xrfm[:, 0:128]),
            e.tensor_copy(out=dbg2[:, 1656:1784], in_=rslab[:, 0:128]),
            e.tensor_copy(out=dbg2[:, 1784:1912], in_=kact[:, 0:128]),
            e.tensor_copy(out=dbg2[:, 1912:2040], in_=y_sb[:, 0:128]),
            e.tensor_copy(out=dbg2[:, 2040:2042], in_=statS[:, 0:2]),
            e.tensor_copy(out=dbg2[:, 2042:2044], in_=statB[:, 0:2]),
            e.tensor_copy(out=dbg2[:, 2044:2046], in_=statM[:, 0:2]),
            e.tensor_copy(out=dbg2[:, 2046:2048], in_=statQ[:, 0:2]),
        ], deps=[prev_mix_last, op_rev[TPH - 1], op_ksq[7], prev_fin],
            tag="cap_c")
        NDW = NT * 4 + 12
        dbg_sb = cst.enter_context(nc.sbuf_tensor("dbg_sb", [128, NDW], F32))
        op_dbg = P.add("vector", lambda e: [
            e.tensor_copy(out=dbg_sb[:, 0:NT * 4], in_=LGcol[:]),
            e.tensor_copy(out=dbg_sb[:, NT * 4:NT * 4 + 12],
                          in_=LGcol[:, 0:12]),
        ], deps=[prev_yr_last, op_cc, prev_fin, cap_c], tag="dbgv")
        op_dbgd = P.add("sync", lambda e: [
            e.dma_start(out=dbgl, in_=dbg_sb[:]),
            e.dma_start(out=dbgy, in_=cc_in_h.ap()),
            e.dma_start(out=dbg2o, in_=dbg2[:]),
            e.dma_start(out=dbg3o, in_=dbg3[:]),
        ], deps=[op_dbg, op_cc], dma=4, tag="dbgd")
    else:
        op_dbgd = None
    P.add("gpsimd", lambda e: [e.engine_nop()],
          deps=[prev_out, prev_fouts.get(TPH - 2), op_dbgd], tag="done")

    P.emit(nc, blk, sems)
    cst.close()
    per.close()
    return nc


def kernel(L0_attn_x, L0_attn_W, L0_ffn_W_r, L0_ffn_W_k, L0_ffn_W_v,
           L0_ffn_mu_r, L0_ffn_mu_k, _T=None):
    x = np.ascontiguousarray(L0_attn_x, dtype=np.float32)
    W = np.asarray(L0_attn_W, dtype=np.float32)
    B, T, _ = x.shape
    if _T is not None:
        T = _T
        x = x[:, :T]
    SLICE = T // 4

    nc = build(T)

    identb = np.eye(128, dtype=BF)
    identf = np.eye(128, dtype=np.float32)
    triu = np.triu(np.ones((128, 128), dtype=np.float32)).astype(BF)
    onesrow = np.ones((1, 128), dtype=np.float32)
    mur = np.ascontiguousarray(
        np.asarray(L0_ffn_mu_r, np.float32).reshape(8, 128).T)
    muk = np.ascontiguousarray(
        np.asarray(L0_ffn_mu_k, np.float32).reshape(8, 128).T)
    wr = np.ascontiguousarray(np.asarray(L0_ffn_W_r, np.float32).T)
    wk2 = np.ascontiguousarray(np.asarray(L0_ffn_W_k, np.float32).T)
    wv2 = np.ascontiguousarray(np.asarray(L0_ffn_W_v, np.float32).T)

    in_maps = []
    for c in range(8):
        b = c // 4
        hc = c % 4
        q = c % 4
        Wq = W[256 * hc:256 * (hc + 1), :]
        Wv = W[2048 + 256 * hc:2048 + 256 * (hc + 1), :]
        Wk = W[1024 + 256 * hc:1024 + 256 * (hc + 1), :]
        Wg = W[3072 + 4 * hc:3072 + 4 * (hc + 1), :]
        wqv = np.ascontiguousarray(
            np.concatenate([Wq.T, Wv.T], axis=1))
        wkg = np.ascontiguousarray(
            np.concatenate([Wk.T, Wg.T], axis=1))
        t0 = q * SLICE
        xb0 = (np.zeros((1, C), np.float32) if t0 == 0
               else np.ascontiguousarray(x[b, t0 - 1:t0]))
        in_maps.append({
            "xb": np.ascontiguousarray(x[b]),
            "xsl": np.ascontiguousarray(x[b, t0:t0 + SLICE]),
            "xb0": xb0,
            "wqv": wqv, "wkg": wkg,
            "wr": wr, "wk2": wk2, "wv2": wv2,
            "mur": mur, "muk": muk,
            "identb": identb, "identf": identf, "triu": triu,
            "onesrow": onesrow,
        })

    res = run_bass_kernel_spmd(nc, in_maps, core_ids=list(range(8)))
    outp = np.empty((B, T, C), dtype=np.float32)
    for c in range(8):
        b = c // 4
        q = c % 4
        outp[b, q * SLICE:(q + 1) * SLICE] = res.results[c]["out"]
    return outp


if __name__ == "__main__":
    build(1024)
    print("build ok")


# version 16
# speedup vs baseline: 1.1743x; 1.0073x over previous
import numpy as np
import ml_dtypes
import concourse.bass as bass
import concourse.mybir as mybir
from concourse.bass_utils import run_bass_kernel_spmd
from contextlib import ExitStack

F32 = mybir.dt.float32
BF16 = mybir.dt.bfloat16
AF = mybir.ActivationFunctionType
ALU = mybir.AluOpType
AX = mybir.AxisListType
BF = ml_dtypes.bfloat16

C = 1024
NHC = 4
LNEPS = 1e-5
ENGS = ("sync", "gpsimd", "scalar", "vector", "tensor")


class Op:
    __slots__ = ("eng", "fn", "deps", "dma", "manual", "seq", "needed", "tag", "slot", "serial")

    def __init__(self, eng, fn, deps, dma, manual, tag, serial=False):
        self.eng, self.fn, self.deps = eng, fn, list(deps)
        self.dma, self.manual, self.tag = dma, manual, tag
        self.serial = serial
        self.seq = 0
        self.needed = False


class Plan:
    def __init__(self):
        self.ops = []

    def add(self, eng, fn, deps=(), dma=0, manual=0, tag="", serial=False):
        op = Op(eng, fn, [d for d in deps if d is not None], dma, manual, tag,
                serial)
        self.ops.append(op)
        return op

    NSLOT = 8

    def emit(self, nc, blk, sems):
        for op in self.ops:
            for d in op.deps:
                d.needed = True
        dslot = {e: 0 for e in ENGS}
        cnt = {}
        for op in self.ops:
            if op.dma or op.manual:
                op.slot = dslot[op.eng]
                dslot[op.eng] = (dslot[op.eng] + 1) % Plan.NSLOT
                k = (op.eng, 'd', op.slot)
                cnt[k] = cnt.get(k, 0) + (op.manual if op.manual
                                          else 16 * op.dma)
                op.seq = cnt[k]
            else:
                op.slot = None
                k = (op.eng, 'c')
                if op.serial:
                    cnt[k] = cnt.get(k, 0) + len(op.fn)
                elif op.needed:
                    cnt[k] = cnt.get(k, 0) + 1
                op.seq = cnt.get(k, 0)

        def body(eng_name):
            def f(eng):
                last_wait = {}
                for op in self.ops:
                    if op.eng != eng_name:
                        continue
                    for d in op.deps:
                        if d.seq == 0 or not (d.needed or d.manual):
                            continue
                        dk = ((d.eng, 'd', d.slot) if d.slot is not None
                              else (d.eng, 'c'))
                        if d.seq > last_wait.get(dk, 0):
                            eng.wait_ge(sems[dk], d.seq)
                            last_wait[dk] = d.seq
                    if op.manual:
                        n = op.fn(eng, sems[(eng_name, 'd', op.slot)])
                        assert n == op.manual, (op.tag, n, op.manual)
                    elif op.serial:
                        base = op.seq - len(op.fn)
                        semc = sems[(eng_name, 'c')]
                        for i2, f2 in enumerate(op.fn):
                            ins = f2(eng)
                            ins.then_inc(semc, 1)
                            if i2 < len(op.fn) - 1:
                                eng.wait_ge(semc, base + i2 + 1)
                        last_wait[(eng_name, 'c')] = op.seq - 1
                    else:
                        instrs = op.fn(eng)
                        if op.needed:
                            if op.dma:
                                assert len(instrs) == op.dma, (
                                    op.tag, len(instrs), op.dma)
                                for ins in instrs:
                                    ins.then_inc(
                                        sems[(eng_name, 'd', op.slot)], 16)
                            else:
                                instrs[-1].then_inc(sems[(eng_name, 'c')], 1)
            return f

        blk.sync(body("sync"))
        blk.gpsimd(body("gpsimd"))
        blk.scalar(body("scalar"))
        blk.vector(body("vector"))
        blk.tensor(body("tensor"))


def build(T=8192, dbg=None):
    if dbg is None:
        dbg = T < 8192
    NT = T // 128
    NSUP = T // 256
    NTCH = T // 1024
    SLICE = T // 4
    CHALF = min(1024, SLICE)
    NHALF = SLICE // CHALF
    TPH = CHALF // 128

    nc = bass.Bass("TRN2", target_bir_lowering=False,
                   detect_race_conditions=False)

    def din(name, shape, dt):
        return nc.dram_tensor(name, shape, dt, kind="ExternalInput").ap()
    xb = din("xb", [T, C], F32)
    xsl = din("xsl", [SLICE, C], F32)
    xb0 = din("xb0", [1, C], F32)
    wqv = din("wqv", [C, 512], F32)
    wkg = din("wkg", [C, 260], F32)
    wr_d = din("wr", [C, C], F32)
    wk_d = din("wk2", [C, C], F32)
    wv_d = din("wv2", [C, C], F32)
    mur = din("mur", [128, 8], F32)
    muk = din("muk", [128, 8], F32)
    identb = din("identb", [128, 128], BF16)
    identf = din("identf", [128, 128], F32)
    triu_d = din("triu", [128, 128], BF16)
    onesrow = din("onesrow", [1, 128], F32)
    out = nc.dram_tensor("out", [SLICE, C], F32, kind="ExternalOutput").ap()
    if dbg:
        dbgy = nc.dram_tensor("dbgy", [T, 256], BF16,
                              kind="ExternalOutput").ap()
        dbgl = nc.dram_tensor("dbgl", [128, NT * 4 + 12],
                              F32, kind="ExternalOutput").ap()
        dbg2o = nc.dram_tensor("dbg2o", [128, 2048], F32,
                               kind="ExternalOutput").ap()
        dbg3o = nc.dram_tensor("dbg3o", [128, 1536], F32,
                               kind="ExternalOutput").ap()

    cc_in_h = nc.dram_tensor("cc_in", [T, 256], BF16, kind="Internal")
    cc_out_h = nc.dram_tensor("cc_out", [4 * T, 256], BF16, kind="Internal")

    per = ExitStack()
    sems = {}
    for e in ENGS:
        sems[(e, 'c')] = per.enter_context(nc.semaphore("s_" + e + "_c"))
        for sl in range(Plan.NSLOT):
            sems[(e, 'd', sl)] = per.enter_context(
                nc.semaphore(f"s_{e}_d{sl}"))

    def sb(name, shape, dt):
        return per.enter_context(nc.sbuf_tensor(name, shape, dt))

    s_idb = sb("s_idb", [128, 128], BF16)
    s_idf = sb("s_idf", [128, 128], F32)
    s_triu = sb("s_triu", [128, 128], BF16)
    s_ones = sb("s_ones", [1, 128], F32)
    s_mur = sb("s_mur", [128, 8], F32)
    s_muk = sb("s_muk", [128, 8], F32)
    E = sb("E", [128, 6400], F32)
    xA = sb("xA", [128, 3072], F32)
    LGcol = sb("LGcol", [128, NT * 4], F32)
    statR = sb("statR", [128, 8], F32)
    statQ = sb("statQ", [128, 8], F32)
    statM = sb("statM", [128, 8], F32)
    statS = sb("statS", [128, NT], F32)
    statB = sb("statB", [128, NT], F32)
    crow = sb("crow", [1, 4 * NT], F32)
    Chalf = sb("Chalf", [1, 4 * NSUP], F32)
    Erow = sb("Erow", [1, 4 * NT], F32)
    Frow = sb("Frow", [1, 4 * NT], F32)
    fcol = sb("fcol", [128, 12], F32)
    ybnd = sb("ybnd", [1, C], BF16)
    yscr = sb("yscr", [1, 256], BF16)
    xb0s = sb("xb0s", [1, C], F32)
    x1b = sb("x1b", [1, C], F32)
    xn2b = sb("xn2b", [1, C], F32)
    bnds = sb("bnds", [1, 8], F32)
    kcol = sb("kcol", [128, 2], F32)
    dbg2 = sb("dbg2", [128, 2048], F32) if dbg else None
    dbg3 = sb("dbg3", [128, 1536], F32) if dbg else None

    blk = per.enter_context(nc.Block())
    P = Plan()

    ab = ExitStack()

    def sba(name, shape, dt):
        return ab.enter_context(nc.sbuf_tensor(name, shape, dt))

    Wqv = sba("Wqv", [128, 8 * 512], BF16)
    Wkg = sba("Wkg", [128, 8 * 260], BF16)
    q_tok = sba("q_tok", [128, NT * 256], BF16)
    Kfm = [sba("Kfm0", [128, T], BF16), sba("Kfm1", [128, T], BF16)]
    V2 = sba("V2", [128, NT * 260], BF16)
    xn_tok = sba("xn_tok", [128, 2 * 1024], BF16)
    xn_fm = [sba("xn_fm0", [128, 8 * 512], BF16),
             sba("xn_fm1", [128, 8 * 512], BF16)]
    SS = [sba("SS0", [128, NHC * NT], F32),
          sba("SS1", [128, NHC * NT], F32)]
    Fbc = sba("Fbc", [128, 4 * NT], F32)
    Cbc = sba("Cbc", [128, 4 * NSUP], F32)
    wt = sba("wt", [128, 4 * NHC * 256], BF16)
    qt_sc = [sba("qt_sc0", [128, 512], BF16),
             sba("qt_sc1", [128, 512], BF16)]
    qfm = [[sba("qfm00", [128, 256], BF16), sba("qfm01", [128, 256], BF16)],
           [sba("qfm10", [128, 256], BF16), sba("qfm11", [128, 256], BF16)]]
    Uev = sba("Uev", [65, 1024], F32)
    rncol = sba("rncol", [128, 2], F32)
    ybuf = sba("ybuf", [128, 512], BF16)

    psA = [ab.enter_context(nc.psum_tensor(f"psA{i}", [128, 512], F32))
           for i in range(2)]
    psU = [ab.enter_context(nc.psum_tensor(f"psU{i}", [128, 512], F32))
           for i in range(4)]
    pk = ab.enter_context(nc.psum_tensor("pk", [128, 1024], F32))

    v2v = V2[:].rearrange("p (j h x) -> p j h x", j=NT, h=NHC)

    op_const = P.add("sync", lambda e: [
        e.dma_start(out=s_idb[:], in_=identb),
        e.dma_start(out=s_idf[:], in_=identf),
        e.dma_start(out=s_triu[:], in_=triu_d),
        e.dma_start(out=s_ones[:], in_=onesrow),
        e.dma_start(out=s_mur[:], in_=mur),
        e.dma_start(out=s_muk[:], in_=muk),
        e.dma_start(out=xb0s[:], in_=xb0),
    ], dma=7, tag="consts")
    op_wload = P.add("gpsimd", lambda e: [
        e.dma_start(out=Wqv[:].rearrange("p (cc o) -> p cc o", cc=8),
                    in_=wqv.rearrange("(cc p) o -> p cc o", p=128)),
        e.dma_start(out=Wkg[:].rearrange("p (cc o) -> p cc o", cc=8),
                    in_=wkg.rearrange("(cc p) o -> p cc o", p=128)),
    ], dma=2, tag="wload")
    op_v2ones = P.add("gpsimd", lambda e: [
        e.memset(v2v[:, :, :, 64:65], 1.0)], tag="v2ones")
    op_ybnd0 = P.add("gpsimd", lambda e: [e.memset(ybnd[:], 0.0)], tag="ybnd0")
    op_kcol = P.add("gpsimd", lambda e: [
        e.memset(kcol[:, 0:1], LNEPS), e.memset(kcol[:, 1:2], 1.0),
        e.memset(SS[0][:], 0.0), e.memset(SS[1][:], 0.0)] +
        ([e.memset(dbg2[:], 0.0), e.memset(dbg3[:], 0.0)] if dbg else []),
        tag="kcol")

    NCH = T // 512
    op_norm, op_tr, op_qv, op_qvev = {}, {}, {}, {}
    op_kg, op_kev, op_gev, op_lgev = {}, {}, {}, {}
    op_scan = {}
    op_red, op_sq, op_a2b = {}, {}, {}
    cap_a = None
    op_zeros = P.add("vector", lambda e: [e.memset(E[0:4, 5376:6400], 0.0)],
                     tag="scan_zeros")
    xq = [E[:, 0:1024], E[:, 1024:2048], xA[:, 0:1024], xA[:, 1024:2048],
          xA[:, 2048:3072]]
    for c5 in range(NCH):
        buf = c5 % 2
        xnb = xn_fm[buf]
        for tl4 in range(4):
            tt = c5 * 4 + tl4
            xt = xq[tt % 5]
            dd = ([op_norm[tt - 5], op_sq[tt - 5]] if tt >= 5 else [])
            xqs = ["gpsimd", "sync", "scalar"]
            xda = P.add(xqs[(2 * tt) % 3], (lambda xt=xt, tt=tt: lambda e: [
                e.dma_start(out=xt[:, 0:512],
                            in_=xb[tt * 128:(tt + 1) * 128, 0:512])])(),
                deps=dd, dma=1, tag=f"xa_{tt}")
            xdb = P.add(xqs[(2 * tt + 1) % 3], (lambda xt=xt, tt=tt:
                                                lambda e: [
                e.dma_start(out=xt[:, 512:1024],
                            in_=xb[tt * 128:(tt + 1) * 128, 512:1024])])(),
                deps=dd, dma=1, tag=f"xb_{tt}")
            op_red[tt] = P.add("vector", (lambda xt=xt, tt=tt: lambda e: [
                e.tensor_reduce(out=statR[:, tt % 8:tt % 8 + 1], in_=xt,
                                axis=AX.X,
                                op=ALU.add)])(), deps=[xda, xdb],
                tag=f"red{tt}")
            op_sq[tt] = P.add("scalar", (lambda xt=xt, tt=tt: lambda e: [
                e.activation(out=wt[:, 0:1024], in_=xt, func=AF.Square,
                             accum_out=statQ[:, tt % 8:tt % 8 + 1])])(),
                deps=[xda, xdb, op_sq.get(tt - 1)], tag=f"sq{tt}")
        t0c, t1c = c5 * 4, c5 * 4 + 4
        q0, q1 = (c5 % 2) * 4, (c5 % 2) * 4 + 4
        if dbg and c5 == 0:
            cap_a = P.add("vector", lambda e: [
                e.tensor_copy(out=dbg3[:, 0:256], in_=E[:, 0:256]),
                e.tensor_copy(out=dbg3[:, 256:260], in_=statR[:, 0:4]),
                e.tensor_copy(out=dbg3[:, 256 + NT:256 + NT + 4],
                              in_=statQ[:, 0:4]),
            ], deps=[op_red[3], op_sq[3], op_kcol], tag="cap_a")
        op_a2v = P.add("vector", [
            (lambda q0=q0, q1=q1: lambda e: e.tensor_scalar(
                out=statM[:, q0:q1], in0=statR[:, q0:q1],
                scalar1=1.0 / C, scalar2=None, op0=ALU.mult))(),
            (lambda q0=q0, q1=q1: lambda e: e.tensor_scalar(
                out=statQ[:, q0:q1], in0=statQ[:, q0:q1],
                scalar1=1.0 / C, scalar2=None, op0=ALU.mult))(),
            (lambda q0=q0, q1=q1: lambda e: e.tensor_mul(
                out=statR[:, q0:q1], in0=statM[:, q0:q1],
                in1=statM[:, q0:q1]))(),
            (lambda q0=q0, q1=q1: lambda e: e.tensor_sub(
                out=statQ[:, q0:q1], in0=statQ[:, q0:q1],
                in1=statR[:, q0:q1]))(),
        ], deps=[op_red[t1c - 1], op_sq[t1c - 1], op_a2b.get(c5 - 2),
                 (cap_a if c5 == 0 else None)],
            tag=f"a2v{c5}", serial=True)
        op_a2s = P.add("scalar", [
            (lambda q0=q0, q1=q1: lambda e: e.activation(
                out=statQ[:, q0:q1], in_=statQ[:, q0:q1],
                func=AF.Ln, bias=kcol[:, 0:1]))(),
            (lambda t0c=t0c, t1c=t1c, q0=q0, q1=q1: lambda e: e.activation(
                out=statS[:, t0c:t1c], in_=statQ[:, q0:q1],
                func=AF.Exp, scale=-0.5))(),
        ], deps=[op_a2v, op_kcol], tag=f"a2s{c5}", serial=True)
        op_a2b[c5] = P.add("vector", [
            (lambda q0=q0, q1=q1: lambda e: e.tensor_scalar(
                out=statM[:, q0:q1], in0=statM[:, q0:q1],
                scalar1=-1.0, scalar2=None, op0=ALU.mult))(),
            (lambda t0c=t0c, t1c=t1c, q0=q0, q1=q1: lambda e: e.tensor_mul(
                out=statB[:, t0c:t1c], in0=statM[:, q0:q1],
                in1=statS[:, t0c:t1c]))(),
        ], deps=[op_a2s], tag=f"a2b{c5}", serial=True)
        for tl4 in range(4):
            tt = c5 * 4 + tl4
            par = tt % 2
            op_norm[tt] = P.add("vector", (lambda tl4=tl4, tt=tt, par=par:
                                           lambda e: [
                e.tensor_scalar(out=xn_tok[:, par * 1024:(par + 1) * 1024],
                                in0=xq[tt % 5], scalar1=statS[:, tt:tt + 1],
                                scalar2=statB[:, tt:tt + 1],
                                op0=ALU.mult, op1=ALU.add)])(),
                deps=[op_a2b[c5], op_tr.get(tt - 2)], tag=f"norm{tt}")
            trdeps = [op_norm[tt]]
            if c5 >= 2:
                trdeps.append(op_kg[(c5 - 2, "g")])
            op_tr[tt] = P.add("sync" if tt % 2 == 0 else "scalar",
                              (lambda tt=tt, tl4=tl4, par=par, xnb=xnb:
                               lambda e: [
                e.dma_start(out=xnb[:].rearrange(
                                "p (c t) -> p c t",
                                c=8)[:, :, tl4 * 128:(tl4 + 1) * 128],
                            in_=xn_tok[:, par * 1024:(par + 1) * 1024]
                            .rearrange("p (c q) -> p c q", c=8),
                            transpose=True)])(),
                deps=trdeps, dma=1, tag=f"tr{tt}")
        trd = [op_tr[c5 * 4 + t2] for t2 in range(4)]
        if c5 == 0 and dbg:
            cap_b = P.add("vector", (lambda: lambda e: [
                e.tensor_copy(out=dbg3[:, 640:896], in_=xn_tok[:, 0:256]),
                e.tensor_copy(out=dbg3[:, 896:1152], in_=xn_fm[0][:, 0:256]),
            ])(), deps=trd + [op_norm[0]], tag="cap_b")
            trd = trd + [cap_b]
        for tl4 in range(4):
            tt = c5 * 4 + tl4
            pq = psU[tl4 % 2]
            deps = [op_tr[tt], op_wload, op_qvev.get(tt - 2)] + (
                [cap_b] if c5 == 0 and dbg else [])
            op_qv[tt] = P.add("tensor", (lambda tt=tt, tl4=tl4, pq=pq,
                                         xnb=xnb: lambda e: [
                e.matmul(out=pq[:], lhsT=xnb[:, cc * 512 + tl4 * 128:
                                            cc * 512 + (tl4 + 1) * 128],
                         rhs=Wqv[:, cc * 512:(cc + 1) * 512],
                         start=(cc == 0), stop=(cc == 7))
                for cc in range(8)])(),
                deps=deps, tag=f"qv{tt}")
            op_qvev[tt] = P.add("vector", (lambda tt=tt, pq=pq: lambda e: [
                e.tensor_copy(out=q_tok[:, tt * 256:(tt + 1) * 256],
                              in_=pq[:, 0:256]),
                e.tensor_copy(
                    out=V2[:, tt * 260:(tt + 1) * 260].rearrange(
                        "p (h x) -> p h x", h=NHC)[:, :, 0:64],
                    in_=pq[:, 256:512].rearrange("p (h d) -> p h d", h=NHC)),
            ])(), deps=[op_qv[tt], op_v2ones], tag=f"qvev{tt}")
        for pair in range(2):
            kwar = op_kev.get((c5 - 1, pair))
            op_kg[(c5, pair)] = P.add("tensor", (lambda c5=c5, pair=pair,
                                                 xnb=xnb: lambda e: [
                e.matmul(out=pk[:, pair * 512:(pair + 1) * 512],
                         lhsT=Wkg[:, cc * 260 + pair * 128:
                                  cc * 260 + (pair + 1) * 128],
                         rhs=xnb[:, cc * 512:(cc + 1) * 512],
                         start=(cc == 0), stop=(cc == 7))
                for cc in range(8)])(),
                deps=trd + [op_wload, kwar], tag=f"k{c5}_{pair}")
            op_kev[(c5, pair)] = P.add("vector", (lambda c5=c5, pair=pair:
                                                  lambda e: [
                e.tensor_copy(out=Kfm[pair][:, c5 * 512:(c5 + 1) * 512],
                              in_=pk[:, pair * 512:(pair + 1) * 512])])(),
                deps=[op_kg[(c5, pair)]], tag=f"kev{c5}_{pair}")
        gwar = op_gev.get(c5 - 1) if c5 % 2 else op_lgev.get(c5 // 2 - 1)
        op_kg[(c5, "g")] = P.add("tensor", (lambda c5=c5, xnb=xnb: lambda e: [
            e.matmul(out=psU[2][0:4, 0:512],
                     lhsT=Wkg[:, cc * 260 + 256:cc * 260 + 260],
                     rhs=xnb[:, cc * 512:(cc + 1) * 512],
                     start=(cc == 0), stop=(cc == 7))
            for cc in range(8)])(),
            deps=trd + [op_wload, gwar, op_gev.get(c5 - 1)], tag=f"g{c5}")
        op_gev[c5] = P.add("vector", (lambda c5=c5: lambda e: [
            e.tensor_copy(out=E[0:4, 2048 + (c5 % 2) * 512:
                                2560 + (c5 % 2) * 512],
                          in_=psU[2][0:4, 0:512])])(),
            deps=[op_kg[(c5, "g")], (op_lgev.get(c5 // 2 - 1)
                                     if c5 % 2 == 0 else None)],
            tag=f"gev{c5}")

        if c5 % 2 == 0:
            continue
        ch = c5 // 2
        graw = E[0:4, 2048:3072]
        tscr = E[0:4, 3072:4096]
        lgr = E[0:4, 4096:5120]
        if ch == 0 and dbg:
            cap_g = P.add("vector", (lambda graw=graw: lambda e: [
                e.tensor_copy(out=dbg3[0:4, 1152:1408], in_=graw[:, 0:256])
            ])(), deps=[op_gev[c5]], tag="cap_g")
        elif ch == 0:
            cap_g = None
        op_sp1 = P.add("vector", (lambda: lambda e: [
            e.tensor_scalar(out=lgr, in0=graw, scalar1=0.0, scalar2=None,
                            op0=ALU.min),
        ])(), deps=[op_gev[c5], op_gev[c5 - 1]], tag=f"sp1_{ch}")
        op_sp2 = P.add("scalar", [
            (lambda tscr=tscr, graw=graw: lambda e:
             e.activation(out=tscr, in_=graw, func=AF.Abs))(),
            (lambda tscr=tscr: lambda e:
             e.activation(out=tscr, in_=tscr, func=AF.Exp, scale=-1.0))(),
            (lambda tscr=tscr: lambda e:
             e.activation(out=tscr, in_=tscr, func=AF.Ln,
                          bias=kcol[0:4, 1:2]))(),
        ], deps=[op_gev[c5], op_gev[c5 - 1], op_sp1, op_kcol],
            tag=f"sp2_{ch}", serial=True)
        if ch == 0 and dbg:
            cap_g2 = P.add("vector", (lambda lgr=lgr, tscr=tscr: lambda e: [
                e.tensor_copy(out=dbg3[0:4, 1408:1472], in_=lgr[:, 0:64]),
                e.tensor_copy(out=dbg3[0:4, 1472:1536], in_=tscr[:, 0:64]),
            ])(), deps=[op_sp2, op_sp1], tag="cap_g2")
        elif ch == 0:
            cap_g2 = None
        carry = E[0:4, 5120 + ch:5121 + ch]
        pcarry = E[0:4, 5119 + ch:5120 + ch]
        op_scan[ch] = P.add("vector", (
            [(lambda lgr=lgr, tscr=tscr: lambda e:
              e.tensor_sub(out=lgr, in0=lgr, in1=tscr))(),
             (lambda graw=graw, lgr=lgr, pcarry=pcarry, ch=ch: lambda e:
              e.tensor_tensor_scan(out=graw, data0=E[0:4, 5376:6400],
                                   data1=lgr,
                                   initial=(pcarry if ch > 0 else 0.0),
                                   op0=ALU.add, op1=ALU.add))()] +
            [(lambda carry=carry, graw=graw: lambda e:
              e.tensor_copy(out=carry, in_=graw[:, 1023:1024]))()]
        ), deps=[op_sp2, op_zeros, (cap_g if ch == 0 else None),
                 (cap_g2 if ch == 0 else None)],
            tag=f"scan{ch}", serial=True)
        op_lgtr = P.add("tensor", (lambda graw=graw: lambda e: [
            e.transpose(out=psA[0][:, tl2 * 4:tl2 * 4 + 4],
                        in_=graw[:, tl2 * 128:(tl2 + 1) * 128],
                        identity=s_idf[0:4, 0:4])
            for tl2 in range(8)])(),
            deps=[op_scan[ch], op_gev[c5], op_lgev.get(ch - 1)],
            tag=f"lgtr{ch}")
        op_lgev[ch] = P.add("vector", (lambda ch=ch: lambda e: [
            e.tensor_copy(out=LGcol[:, ch * 32:(ch + 1) * 32],
                          in_=psA[0][:, 0:32])])(),
            deps=[op_lgtr], tag=f"lgev{ch}")

    lg_done = op_lgev[NTCH - 1]
    a_done_v = lg_done
    a_done_t = op_kg[(NTCH - 1, "g")]

    op_cx = P.add("sync", lambda e: (
        [e.dma_start(out=crow[0:1, h * NT:(h + 1) * NT],
                     in_=LGcol[64:65, h::4][:, 0:NT],
                     allow_slow_non_contiguous=True) for h in range(NHC)] +
        [e.dma_start(out=Chalf[0:1, h * NSUP:(h + 1) * NSUP],
                     in_=LGcol[0:1, 4 + h::8][:, 0:NSUP],
                     allow_slow_non_contiguous=True) for h in range(NHC)]
    ), deps=[lg_done], dma=8, tag="cx")
    crv = crow[0:1, :].rearrange("a (h i) -> a h i", h=NHC)
    Erv = Erow[0:1, :].rearrange("a (h i) -> a h i", h=NHC)
    Chv = Chalf[0:1, :].rearrange("a (h i) -> a h i", h=NHC)
    op_erow = P.add("vector", [
        lambda e: e.tensor_sub(out=Erv[:, :, 0::2], in0=crv[:, :, 0::2],
                               in1=Chv[:]),
        lambda e: e.tensor_sub(out=Erv[:, :, 1::2], in0=crv[:, :, 1::2],
                               in1=Chv[:]),
        lambda e: e.tensor_scalar(out=Chalf[:], in0=Chalf[:], scalar1=0.5,
                                  scalar2=None, op0=ALU.mult),
    ], deps=[op_cx], tag="erow", serial=True)
    op_frow = P.add("scalar", lambda e: [
        e.activation(out=Frow[:], in_=Erow[:], func=AF.Exp, scale=0.5),
    ], deps=[op_erow], tag="frow")


    Frv = Frow[0:1, :].rearrange("a (h i) -> a h i", h=NHC)
    op_fbc = P.add("tensor", lambda e: [
        e.matmul(out=psU[0][:, 0:4 * NT], lhsT=s_ones[:], rhs=Frow[0:1, :],
                 start=True, stop=True),
        e.matmul(out=psU[1][:, 0:4 * NSUP], lhsT=s_ones[:], rhs=Chalf[0:1, :],
                 start=True, stop=True),
    ], deps=[op_frow, op_qvev[NT - 1], op_lgev[NTCH - 1], op_erow],
        tag="fbc")
    op_fbcev = P.add("vector", lambda e: [
        e.tensor_copy(out=Fbc[:], in_=psU[0][:, 0:4 * NT]),
        e.tensor_copy(out=Cbc[:], in_=psU[1][:, 0:4 * NSUP]),
    ], deps=[op_fbc], tag="fbcev")

    op_S, op_evA, op_evD, op_mask, op_PV = {}, {}, {}, {}, {}
    op_yr, op_ysc, op_ydma = {}, {}, {}
    op_ssI, op_qsc, op_qtr = {}, {}, {}
    prev_yr_last, prev_ysc_last = None, None
    prev_ydma = None
    prev_uev = None
    last_evD = {}

    def emit_prep(I):
        pi = I % 2
        jmax = 2 * I + 2
        op_ssI[I] = P.add("scalar", (lambda I=I, jmax=jmax, pi=pi: lambda e: [
            e.activation(out=SS[pi][:, h * NT:h * NT + jmax],
                         in_=LGcol[:, h::4][:, 0:jmax],
                         func=AF.Exp, scale=-0.5,
                         bias=Cbc[:, h * NSUP + I:h * NSUP + I + 1])
            for h in range(NHC)])(),
            deps=[op_fbcev, op_ssI.get(I - 2),
                  op_evA.get((I - 2, 2 * I - 3)) if I >= 2 else None,
                  last_evD.get(I - 2)],
            tag=f"ss{I}")
        op_qsc[I] = P.add("vector", (lambda I=I, pi=pi: lambda e: [
            e.tensor_scalar(
                out=qt_sc[pi][:, (h // 2) * 256 + sub * 128 + (h % 2) * 64:
                              (h // 2) * 256 + sub * 128 + (h % 2) * 64 + 64],
                in0=q_tok[:, (2 * I + sub) * 256 + h * 64:
                          (2 * I + sub) * 256 + (h + 1) * 64],
                scalar1=Fbc[:, h * NT + 2 * I + sub:h * NT + 2 * I + sub + 1],
                scalar2=None, op0=ALU.mult)
            for sub in range(2) for h in range(NHC)])(),
            deps=[op_fbcev, op_qtr.get(I - 2)], tag=f"qsc{I}")
        op_qtr[I] = P.add("scalar", (lambda I=I, pi=pi: lambda e: [
            e.dma_start(out=qfm[pi][p][:].rearrange("p (s t) -> p s t", s=2),
                        in_=qt_sc[pi][:, p * 256:(p + 1) * 256].rearrange(
                            "p (s q) -> p s q", s=2),
                        transpose=True)
            for p in range(2)])(),
            deps=[op_qsc[I],
                  op_S.get((I - 2, 2 * I - 3)) if I >= 2 else None],
            dma=2, tag=f"qtr{I}")

    def mk_S(I, j):
        jmax = 2 * I + 2
        par = j % 2
        if par == 0:
            bA = lambda hq: psA[0][:, hq * 256:(hq + 1) * 256]
            bB = lambda hq: psA[1][:, hq * 256:(hq + 1) * 256]
        else:
            bA = lambda hq: pk[:, hq * 256:(hq + 1) * 256]
            bB = lambda hq: pk[:, 512 + hq * 256:512 + (hq + 1) * 256]
        sdeps = [op_qtr[I], op_evA.get((I, j - 2)), op_evD.get((I, j - 2))]
        if j < 2 and I > 0:
            sdeps += [op_evA.get((I - 1, 2 * I - 2 + j)),
                      op_evD.get((I - 1, 2 * I - 2 + j))]
            if j == 1:
                sdeps.append(op_yr.get((I - 1, 7)))
                sdeps.append(op_ysc.get((I - 1, 7)))
        if I == 0 and j == 0:
            sdeps += [op_qvev[NT - 1], op_lgev[NTCH - 1]]
        pi = I % 2
        op_S[(I, j)] = P.add("tensor", (lambda I=I, j=j, bA=bA, bB=bB, pi=pi:
                                        lambda e: [
            m for p in range(2) for m in (
                e.matmul(out=bA(p),
                         lhsT=Kfm[p][0:64, j * 128:(j + 1) * 128],
                         rhs=qfm[pi][p][0:64, :], start=True, stop=True,
                         tile_position=(0, 0)),
                e.matmul(out=bB(p),
                         lhsT=Kfm[p][64:128, j * 128:(j + 1) * 128],
                         rhs=qfm[pi][p][64:128, :], start=True, stop=True,
                         tile_position=(64, 0)),
            )])(), deps=sdeps, tag=f"S{I}_{j}")
        return bA, bB

    sbanks = {}
    pair_order = []
    emit_prep(0)
    emit_prep(1)
    for I in range(NSUP):
        jmax = 2 * I + 2
        pi = I % 2
        if (I, 0) not in sbanks:
            sbanks[(I, 0)] = mk_S(I, 0)
        for j in range(jmax):
            par = len(pair_order) % 4
            pair_order.append((I, j))
            if j + 1 < jmax:
                sbanks[(I, j + 1)] = mk_S(I, j + 1)
            bA, bB = sbanks[(I, j)]
            if j == 1 and I + 1 < NSUP and (I + 1) not in op_ssI:
                emit_prep(I + 1)
            evdeps = [op_S[(I, j)]]
            if len(pair_order) >= 5:
                evdeps.append(op_PV[pair_order[-5]])
            so2 = 128 if j == 2 * I + 1 else 0
            op_evA[(I, j)] = P.add("scalar", (lambda I=I, j=j, par=par,
                                              so2=so2, bA=bA, pi=pi:
                                              lambda e: [
                e.activation(out=wt[:, par * 1024 + h * 256 + so2:
                                   par * 1024 + (h + 1) * 256],
                             in_=bA(h // 2)[:, so2:],
                             func=AF.Square,
                             scale=SS[pi][:, h * NT + j:h * NT + j + 1])
                for h in (0, 2)])(),
                deps=evdeps + [op_ssI[I]], tag=f"evA{I}_{j}")
            op_evD0 = P.add("vector", (lambda I=I, j=j, par=par,
                                       so2=so2, bB=bB, pi=pi: lambda e: [
                e.tensor_scalar(out=wt[:, par * 1024 + h * 256 + so2:
                                      par * 1024 + (h + 1) * 256],
                                in0=bB(h // 2)[:, so2:],
                                scalar1=SS[pi][:, h * NT + j:h * NT + j + 1],
                                scalar2=None, op0=ALU.mult)
                for h in (1, 3)] + [
                e.tensor_mul(out=wt[:, par * 1024 + h * 256 + so2:
                                   par * 1024 + (h + 1) * 256],
                             in0=wt[:, par * 1024 + h * 256 + so2:
                                   par * 1024 + (h + 1) * 256],
                             in1=wt[:, par * 1024 + h * 256 + so2:
                                   par * 1024 + (h + 1) * 256])
                for h in (1, 3)])(),
                deps=evdeps + [op_ssI[I]], tag=f"evD0{I}_{j}")
            op_evD[(I, j)] = op_evD0
            last_evD[I] = op_evD0
            pvdeps = [op_evA[(I, j)], op_evD0]
            if j == 2 * I:
                op_mask[(I, j)] = P.add("gpsimd", (lambda par=par: lambda e: [
                    e.tensor_mul(out=wt[:, par * 1024 + h * 256:
                                       par * 1024 + h * 256 + 128],
                                 in0=wt[:, par * 1024 + h * 256:
                                       par * 1024 + h * 256 + 128],
                                 in1=s_triu[:])
                    for h in range(NHC)])(),
                    deps=[op_evA[(I, j)], op_evD[(I, j)]], tag=f"mk{I}_{j}")
                pvdeps.append(op_mask[(I, j)])
            elif j == 2 * I + 1:
                op_mask[(I, j)] = P.add("gpsimd", (lambda par=par: lambda e: [
                    x for h in range(NHC) for x in (
                        e.memset(wt[:, par * 1024 + h * 256:
                                   par * 1024 + h * 256 + 128], 0.0),
                        e.tensor_mul(out=wt[:, par * 1024 + h * 256 + 128:
                                           par * 1024 + (h + 1) * 256],
                                     in0=wt[:, par * 1024 + h * 256 + 128:
                                           par * 1024 + (h + 1) * 256],
                                     in1=s_triu[:]),
                    )])(),
                    deps=[op_evA[(I, j)], op_evD[(I, j)]], tag=f"mk{I}_{j}")
                pvdeps.append(op_mask[(I, j)])
            if j == 0:
                pvdeps.append(prev_uev)
                pvdeps.append(op_fbcev)
                if I == 0:
                    pvdeps.append(op_lgev[NTCH - 1])
            op_PV[(I, j)] = P.add("tensor", (lambda I=I, j=j, par=par,
                                             jmax=jmax: lambda e: [
                e.matmul(out=psU[h][0:65, 0:256],
                         lhsT=V2[:, j * 260 + h * 65:j * 260 + (h + 1) * 65],
                         rhs=wt[:, par * 1024 + h * 256:
                               par * 1024 + (h + 1) * 256],
                         start=(j == 0), stop=(j == jmax - 1))
                for h in range(NHC)])(),
                deps=pvdeps, tag=f"PV{I}_{j}")

        op_uev = P.add("scalar", (lambda: lambda e: [
            e.activation(out=Uev[0:65, h * 256:(h + 1) * 256],
                         in_=psU[h][0:65, 0:256], func=AF.Copy)
            for h in range(NHC)])(),
            deps=[op_PV[(I, jmax - 1)]], tag=f"uev{I}")
        prev_uev = op_uev
        if I + 2 < NSUP and (I + 2) not in op_ssI:
            emit_prep(I + 2)
        if I + 1 < NSUP:
            sbanks[(I + 1, 0)] = mk_S(I + 1, 0)
        op_ut = {}
        for kidx in range(8):
            h = kidx // 2
            sub = kidx % 2
            ping = kidx % 2
            utdeps = [op_uev, op_yr.get((I, kidx - 2)),
                      op_ysc.get((I, kidx - 2)),
                      (prev_yr_last if I > 0 and kidx < 2 else None),
                      (prev_ysc_last if I > 0 and kidx < 2 else None),
                      op_evA.get((I, jmax - 1)), op_evD.get((I, jmax - 1))]
            op_ut[kidx] = P.add("tensor", (lambda h=h, sub=sub,
                                           ping=ping: lambda e: [
                e.transpose(out=pk[:, ping * 512:ping * 512 + 65],
                            in_=Uev[0:65, h * 256 + sub * 128:
                                    h * 256 + (sub + 1) * 128],
                            identity=s_idf[0:65, 0:65])])(),
                deps=utdeps, tag=f"ut{I}_{kidx}")
            op_yr[(I, kidx)] = P.add("vector", (lambda ping=ping: lambda e: [
                e.reciprocal(out=rncol[:, ping:ping + 1],
                             in_=pk[:, ping * 512 + 64:ping * 512 + 65])])(),
                deps=[op_ut[kidx], op_ysc.get((I, kidx - 2))],
                tag=f"yr{I}_{kidx}")
            op_ysc[(I, kidx)] = P.add("scalar", (lambda h=h, sub=sub,
                                                 ping=ping: lambda e: [
                e.activation(out=ybuf[:, sub * 256 + h * 64:
                                     sub * 256 + (h + 1) * 64],
                             in_=pk[:, ping * 512:ping * 512 + 64],
                             func=AF.Copy,
                             scale=rncol[:, ping:ping + 1])])(),
                deps=[op_yr[(I, kidx)], prev_ydma], tag=f"ysc{I}_{kidx}")
        prev_yr_last = op_yr[(I, 7)]
        prev_ysc_last = op_ysc[(I, 7)]
        op_ydma[I] = P.add("sync", (lambda I=I: lambda e: [
            e.dma_start(out=cc_in_h.ap()[256 * I:256 * (I + 1), :]
                        .rearrange("(i p) o -> p i o", p=128),
                        in_=ybuf[:].rearrange("p (i o) -> p i o", i=2))])(),
            deps=[op_ysc[(I, k)] for k in range(8)], dma=1, tag=f"ydma{I}")
        prev_ydma = op_ydma[I]

    op_cc = P.add("gpsimd", (lambda: lambda e: [
        e.collective_compute(
            "AllGather", ALU.bypass,
            replica_groups=[[0, 1, 2, 3], [4, 5, 6, 7]],
            ins=[cc_in_h.ap().opt()],
            outs=[cc_out_h.ap().opt()])])(),
        deps=[op_ydma[NSUP - 1]], tag="cc")

    op_cap = prev_yr_last
    if dbg:
        op_cap = P.add("vector", (lambda: lambda e: [
        e.tensor_copy(out=dbg2[:, 0:NT], in_=statS[:, 0:NT]),
        e.tensor_copy(out=dbg2[:, NT:2 * NT], in_=statB[:, 0:NT]),
        e.tensor_copy(out=dbg2[:, 2 * NT:2 * NT + 256],
                      in_=q_tok[:, 0:256]),
        e.tensor_copy(out=dbg2[:, 2 * NT + 256:2 * NT + 512],
                      in_=Kfm[0][:, 0:256]),
        e.tensor_copy(out=dbg2[:, 2 * NT + 512:2 * NT + 772],
                      in_=V2[:, 0:260]),
        e.tensor_copy(out=dbg2[:, 2 * NT + 772:2 * NT + 772 + NHC * NT],
                      in_=SS[0][:]),
            e.tensor_copy(out=dbg2[:, 2 * NT + 772 + NHC * NT:
                                  2 * NT + 772 + NHC * NT + 512],
                          in_=wt[:, 0:512]),
        ])(), deps=[prev_yr_last, op_ssI[NSUP - 1]], tag="cap")

    b_last = {
        "tensor": op_ut[7], "vector": op_cap,
        "scalar": op_ysc[(NSUP - 1, 7)],
        "sync": op_ydma[NSUP - 1], "gpsimd": op_cc,
    }
    ab.close()

    cst = ExitStack()

    def sbc(name, shape, dt):
        return cst.enter_context(nc.sbuf_tensor(name, shape, dt))

    Wr_sb = sbc("Wr_sb", [128, 8 * 1024], BF16)
    Wk_sb = sbc("Wk_sb", [128, 8 * 1024], BF16)
    Wv_sb = sbc("Wv_sb", [128, 8 * 1024], BF16)
    xn2fm = sbc("xn2fm", [128, 8 * CHALF], BF16)
    prevcol = sbc("prevcol", [128, 8], BF16)
    xrfm = sbc("xrfm", [128, 8 * CHALF], BF16)
    xkfm = sbc("xkfm", [128, 8 * CHALF], BF16)
    rslab = sbc("rslab", [128, TPH * 1024], BF16)
    kact = sbc("kact", [128, 8 * CHALF], BF16)
    y_sb = sbc("y_sb", [128, TPH * 1024], BF16)
    pcR = cst.enter_context(nc.psum_tensor("pcR", [128, 1024], F32))
    pcK = cst.enter_context(nc.psum_tensor("pcK", [128, 1024], F32))
    pbnd = cst.enter_context(nc.psum_tensor("pbnd", [128, 8], F32))

    blast = list(b_last.values())
    op_wffn = P.add("gpsimd", lambda e: [
        e.dma_start(out=Wr_sb[:].rearrange("p (cc o) -> p cc o", cc=8),
                    in_=wr_d.rearrange("(cc p) o -> p cc o", p=128)),
        e.dma_start(out=Wk_sb[:].rearrange("p (cc o) -> p cc o", cc=8),
                    in_=wk_d.rearrange("(cc p) o -> p cc o", p=128)),
        e.dma_start(out=Wv_sb[:].rearrange("p (cc o) -> p cc o", cc=8),
                    in_=wv_d.rearrange("(cc p) o -> p cc o", p=128)),
    ], deps=blast, dma=3, tag="wffn")

    xn2v = xn2fm[:].rearrange("p (cc t) -> p cc t", cc=8)
    prev_fin, prev_out, prev_tr2_last, prev_o2ev = None, None, None, None
    prev_norm2 = {}
    prev_mix_last = None
    for hf in range(NHALF):
        ysdeps = [op_cc, b_last["sync"], prev_fin]

        def mk_ysb(hf=hf):
            def f(e, sem):
                nd = 4 + (4 if hf == 0 else 0)
                for arm in e.switch_core_id(n=128):
                    q = arm.logical % 4
                    t0g = q * SLICE + hf * CHALF
                    for r in range(4):
                        ins = e.dma_start(
                            out=y_sb[:].rearrange(
                                "p (i x) -> p i x",
                                i=TPH)[:, :, r * 256:(r + 1) * 256],
                            in_=cc_out_h.ap()[r * T + t0g:r * T + t0g + CHALF,
                                              :].rearrange(
                                "(i p) o -> p i o", p=128))
                        ins.then_inc(sem, 16)
                    if hf == 0:
                        for r in range(4):
                            if q == 0:
                                ins = e.dma_start(out=yscr[:],
                                                  in_=cc_out_h.ap()[r * T:
                                                                    r * T + 1, :])
                            else:
                                ins = e.dma_start(
                                    out=ybnd[0:1, r * 256:(r + 1) * 256],
                                    in_=cc_out_h.ap()[r * T + t0g - 1:
                                                      r * T + t0g, :])
                            ins.then_inc(sem, 16)
                return 16 * nd
            return f
        op_ysb = P.add("sync", mk_ysb(), deps=ysdeps + [op_ybnd0],
                       manual=16 * (8 if hf == 0 else 4), tag=f"ysb{hf}")

        if hf == 0:
            op_x1b = P.add("vector", lambda e: [
                e.tensor_add(out=x1b[:], in0=xb0s[:], in1=ybnd[:])],
                deps=[op_ysb, op_const, blast[1]], tag="x1b")
            op_bred = P.add("vector", lambda e: [
                e.tensor_reduce(out=bnds[0:1, 0:1], in_=x1b[:], axis=AX.X,
                                op=ALU.add)], deps=[op_x1b], tag="bred")
            op_bsq = P.add("scalar", lambda e: [
                e.activation(out=rslab[0:1, 0:1024], in_=x1b[:], func=AF.Square,
                             accum_out=bnds[0:1, 1:2])],
                deps=[op_x1b, b_last["scalar"]], tag="bsq")
            op_bst = P.add("vector", [
                lambda e: e.tensor_scalar(out=bnds[0:1, 2:3],
                                          in0=bnds[0:1, 0:1],
                                          scalar1=1.0 / C, scalar2=None,
                                          op0=ALU.mult),
                lambda e: e.tensor_scalar(out=bnds[0:1, 3:4],
                                          in0=bnds[0:1, 1:2],
                                          scalar1=1.0 / C, scalar2=None,
                                          op0=ALU.mult),
                lambda e: e.tensor_mul(out=bnds[0:1, 4:5], in0=bnds[0:1, 2:3],
                                       in1=bnds[0:1, 2:3]),
                lambda e: e.tensor_sub(out=bnds[0:1, 3:4], in0=bnds[0:1, 3:4],
                                       in1=bnds[0:1, 4:5]),
            ], deps=[op_bred, op_bsq], tag="bst", serial=True)
            op_bs2 = P.add("scalar", [
                lambda e: e.activation(out=bnds[0:1, 3:4], in_=bnds[0:1, 3:4],
                                       func=AF.Ln, bias=kcol[0:1, 0:1]),
                lambda e: e.activation(out=bnds[0:1, 5:6], in_=bnds[0:1, 3:4],
                                       func=AF.Exp, scale=-0.5),
            ], deps=[op_bst], tag="bs2", serial=True)
            op_bs3 = P.add("vector", [
                lambda e: e.tensor_scalar(out=bnds[0:1, 6:7],
                                          in0=bnds[0:1, 2:3], scalar1=-1.0,
                                          scalar2=None, op0=ALU.mult),
                lambda e: e.tensor_mul(out=bnds[0:1, 7:8], in0=bnds[0:1, 6:7],
                                       in1=bnds[0:1, 5:6]),
            ], deps=[op_bs2], tag="bs3", serial=True)
            op_bnrm = P.add("vector", lambda e: [
                e.tensor_scalar(out=xn2b[:], in0=x1b[:],
                                scalar1=bnds[0:1, 5:6],
                                scalar2=bnds[0:1, 7:8],
                                op0=ALU.mult, op1=ALU.add)],
                deps=[op_bs3], tag="bnrm")
            op_btr = P.add("tensor", lambda e: [
                e.transpose(out=pbnd[:, cc:cc + 1],
                            in_=xn2b[0:1, cc * 128:(cc + 1) * 128],
                            identity=s_idf[0:1, 0:1])
                for cc in range(8)],
                deps=[op_bnrm, b_last["tensor"]], tag="btr")
            op_bev = P.add("vector", lambda e: [
                e.tensor_copy(out=prevcol[:], in_=pbnd[:, 0:8])],
                deps=[op_btr], tag="bev")
        else:
            op_bev = P.add("vector", lambda e: [
                e.tensor_copy(out=prevcol[:].rearrange("p (c x) -> p c x", x=1),
                              in_=xn2v[:, :, CHALF - 1:CHALF])],
                deps=[prev_tr2_last, prev_o2ev], tag=f"bcp{hf}")

        op_x1t, op_n2, op_tr2 = {}, {}, {}
        op_r2d, op_q2d = {}, {}
        for tl in range(TPH):
            gt = hf * TPH + tl
            par = tl % 2
            xt = E[:, par * 1024:(par + 1) * 1024]
            xd = P.add("sync", (lambda xt=xt, gt=gt: lambda e: [
                e.dma_start(out=xt, in_=xsl[gt * 128:(gt + 1) * 128, :])])(),
                deps=[op_x1t.get(tl - 2) if tl >= 2 else
                      (prev_fin if hf else b_last["scalar"]),
                      (b_last["vector"] if hf == 0 and tl < 2 else None)],
                dma=1, tag=f"cx{hf}_{tl}")
            op_x1t[tl] = P.add("vector", (lambda xt=xt, tl=tl: lambda e: [
                e.tensor_add(out=y_sb[:, tl * 1024:(tl + 1) * 1024], in0=xt,
                             in1=y_sb[:, tl * 1024:(tl + 1) * 1024])])(),
                deps=[xd, op_ysb,
                      (b_last["vector"] if hf == 0 and tl < 2 else None)],
                tag=f"x1t{hf}_{tl}")
            op_r2d[tl] = P.add("vector", (lambda tl=tl: lambda e: [
                e.tensor_reduce(out=statR[:, tl:tl + 1],
                                in_=y_sb[:, tl * 1024:(tl + 1) * 1024],
                                axis=AX.X, op=ALU.add)])(),
                deps=[op_x1t[tl]], tag=f"r2{hf}_{tl}")
            op_q2d[tl] = P.add("scalar", (lambda tl=tl: lambda e: [
                e.activation(out=rslab[:, 0:1024],
                             in_=y_sb[:, tl * 1024:(tl + 1) * 1024],
                             func=AF.Square,
                             accum_out=statQ[:, tl:tl + 1])])(),
                deps=[op_x1t[tl],
                      (op_bsq if hf == 0 and tl == 0 else None)],
                tag=f"q2{hf}_{tl}")
        op_s2 = P.add("vector", [
            lambda e: e.tensor_scalar(out=statM[:, 0:TPH], in0=statR[:, 0:TPH],
                                      scalar1=1.0 / C, scalar2=None,
                                      op0=ALU.mult),
            lambda e: e.tensor_scalar(out=statQ[:, 0:TPH], in0=statQ[:, 0:TPH],
                                      scalar1=1.0 / C, scalar2=None,
                                      op0=ALU.mult),
            lambda e: e.tensor_mul(out=statR[:, 0:TPH], in0=statM[:, 0:TPH],
                                   in1=statM[:, 0:TPH]),
            lambda e: e.tensor_sub(out=statQ[:, 0:TPH], in0=statQ[:, 0:TPH],
                                   in1=statR[:, 0:TPH]),
        ], deps=[op_r2d[TPH - 1], op_q2d[TPH - 1]],
            tag=f"s2{hf}", serial=True)
        op_s2s = P.add("scalar", [
            lambda e: e.activation(out=statQ[:, 0:TPH], in_=statQ[:, 0:TPH],
                                   func=AF.Ln, bias=kcol[:, 0:1]),
            lambda e: e.activation(out=statS[:, 0:TPH], in_=statQ[:, 0:TPH],
                                   func=AF.Exp, scale=-0.5),
        ], deps=[op_s2], tag=f"s2s{hf}", serial=True)
        op_s2b = P.add("vector", [
            lambda e: e.tensor_scalar(out=statM[:, 0:TPH], in0=statM[:, 0:TPH],
                                      scalar1=-1.0, scalar2=None,
                                      op0=ALU.mult),
            lambda e: e.tensor_mul(out=statB[:, 0:TPH], in0=statM[:, 0:TPH],
                                   in1=statS[:, 0:TPH]),
        ], deps=[op_s2s], tag=f"s2b{hf}", serial=True)
        for tl in range(TPH):
            par = tl % 2
            op_n2[tl] = P.add("vector", (lambda tl=tl, par=par: lambda e: [
                e.tensor_scalar(out=xrfm[:, (1 - par) * 1024:
                                         (2 - par) * 1024],
                                in0=y_sb[:, tl * 1024:(tl + 1) * 1024],
                                scalar1=statS[:, tl:tl + 1],
                                scalar2=statB[:, tl:tl + 1],
                                op0=ALU.mult, op1=ALU.add)])(),
                deps=[op_s2b, op_tr2.get(tl - 2)], tag=f"n2{hf}_{tl}")
            trdeps = [op_n2[tl]]
            if tl == 0:
                trdeps.append(prev_mix_last)
            if tl == TPH - 1:
                trdeps.append(op_bev)
            op_tr2[tl] = P.add("sync", (lambda tl=tl, par=par: lambda e: [
                e.dma_start(out=xn2fm[:].rearrange(
                                "p (c t) -> p c t",
                                c=8)[:, :, tl * 128:(tl + 1) * 128],
                            in_=xrfm[:, (1 - par) * 1024:(2 - par) * 1024]
                            .rearrange("p (c q) -> p c q", c=8),
                            transpose=True)])(),
                deps=trdeps, dma=1, tag=f"tr2{hf}_{tl}")
        prev_norm2 = op_n2
        prev_tr2_last = op_tr2[TPH - 1]

        op_mix = {}
        for cc in range(8):
            op_mix[cc] = P.add("vector", (lambda cc=cc: lambda e: [
                e.tensor_sub(out=xrfm[:, cc * CHALF:cc * CHALF + 1],
                             in0=xn2fm[:, cc * CHALF:cc * CHALF + 1],
                             in1=prevcol[:, cc:cc + 1]),
                e.scalar_tensor_tensor(
                    out=xkfm[:, cc * CHALF:cc * CHALF + 1],
                    in0=xrfm[:, cc * CHALF:cc * CHALF + 1],
                    scalar=s_muk[:, cc:cc + 1],
                    in1=prevcol[:, cc:cc + 1], op0=ALU.mult, op1=ALU.add),
                e.scalar_tensor_tensor(
                    out=xrfm[:, cc * CHALF:cc * CHALF + 1],
                    in0=xrfm[:, cc * CHALF:cc * CHALF + 1],
                    scalar=s_mur[:, cc:cc + 1],
                    in1=prevcol[:, cc:cc + 1], op0=ALU.mult, op1=ALU.add),
                e.tensor_sub(out=xrfm[:, cc * CHALF + 1:(cc + 1) * CHALF],
                             in0=xn2fm[:, cc * CHALF + 1:(cc + 1) * CHALF],
                             in1=xn2fm[:, cc * CHALF:(cc + 1) * CHALF - 1]),
                e.scalar_tensor_tensor(
                    out=xkfm[:, cc * CHALF + 1:(cc + 1) * CHALF],
                    in0=xrfm[:, cc * CHALF + 1:(cc + 1) * CHALF],
                    scalar=s_muk[:, cc:cc + 1],
                    in1=xn2fm[:, cc * CHALF:(cc + 1) * CHALF - 1],
                    op0=ALU.mult, op1=ALU.add),
                e.scalar_tensor_tensor(
                    out=xrfm[:, cc * CHALF + 1:(cc + 1) * CHALF],
                    in0=xrfm[:, cc * CHALF + 1:(cc + 1) * CHALF],
                    scalar=s_mur[:, cc:cc + 1],
                    in1=xn2fm[:, cc * CHALF:(cc + 1) * CHALF - 1],
                    op0=ALU.mult, op1=ALU.add),
            ])(), deps=[op_tr2[tl2] for tl2 in range(TPH)] + [op_bev,
                        op_const, prev_o2ev], tag=f"mix{hf}_{cc}")
        mixall = [op_mix[cc] for cc in range(8)]
        prev_mix_last = op_mix[7]

        op_rp, op_rev = {}, {}
        for tl in range(TPH):
            pr = pcR if tl % 2 == 0 else pcK
            op_rp[tl] = P.add("tensor", (lambda tl=tl, pr=pr: lambda e: [
                m for cc in range(8) for m in (
                    e.matmul(out=pr[:, 0:512],
                             lhsT=xrfm[:, cc * CHALF + tl * 128:
                                      cc * CHALF + (tl + 1) * 128],
                             rhs=Wr_sb[:, cc * 1024:cc * 1024 + 512],
                             start=(cc == 0), stop=(cc == 7)),
                    e.matmul(out=pr[:, 512:1024],
                             lhsT=xrfm[:, cc * CHALF + tl * 128:
                                      cc * CHALF + (tl + 1) * 128],
                             rhs=Wr_sb[:, cc * 1024 + 512:(cc + 1) * 1024],
                             start=(cc == 0), stop=(cc == 7)),
                )])(), deps=mixall + [op_wffn, op_rev.get(tl - 2),
                                      (b_last["tensor"] if hf == 0 and tl < 2
                                       else None),
                                      (op_btr if hf == 0 else None)],
                tag=f"rp{hf}_{tl}")
            op_rev[tl] = P.add("scalar", (lambda tl=tl, pr=pr: lambda e: [
                e.activation(out=rslab[:, tl * 1024:(tl + 1) * 1024],
                             in_=pr[:], func=AF.Sigmoid)])(),
                deps=[op_rp[tl]], tag=f"rev{hf}_{tl}")

        op_kp, op_ksq = {}, {}
        for dc in range(8):
            pr = pcR if (TPH + dc) % 2 == 0 else pcK
            nspl = max(1, CHALF // 512)
            wspl = CHALF // nspl
            op_kp[dc] = P.add("tensor", (lambda dc=dc, pr=pr, nspl=nspl,
                                         wspl=wspl: lambda e: [
                e.matmul(out=pr[:, sp * wspl:(sp + 1) * wspl],
                         lhsT=Wk_sb[:, cc * 1024 + dc * 128:
                                   cc * 1024 + (dc + 1) * 128],
                         rhs=xkfm[:, cc * CHALF + sp * wspl:
                                 cc * CHALF + (sp + 1) * wspl],
                         start=(cc == 0), stop=(cc == 7))
                for cc in range(8) for sp in range(nspl)])(),
                deps=mixall + [op_wffn,
                                      op_ksq.get(dc - 2) or op_rev.get(TPH - 2 + dc),
                                      op_rev.get(TPH - 1) if dc == 1 else None],
                tag=f"kp{hf}_{dc}")
            op_krl = P.add("scalar", (lambda dc=dc, pr=pr: lambda e: [
                e.activation(out=kact[:, dc * CHALF:(dc + 1) * CHALF],
                             in_=pr[:, 0:CHALF], func=AF.Relu)])(),
                deps=[op_kp[dc]], tag=f"krl{hf}_{dc}")
            op_ksq[dc] = P.add("vector", (lambda dc=dc: lambda e: [
                e.tensor_mul(out=kact[:, dc * CHALF:(dc + 1) * CHALF],
                             in0=kact[:, dc * CHALF:(dc + 1) * CHALF],
                             in1=kact[:, dc * CHALF:(dc + 1) * CHALF])])(),
                deps=[op_krl], tag=f"ksq{hf}_{dc}")

        fins = {}
        po_half = prev_fouts if hf else {}
        fouts = {}
        for tl in range(TPH):
            pr = pcR if (TPH + 8 + tl) % 2 == 0 else pcK
            o2deps = [op_ksq[dc] for dc in range(8)] + [op_wffn]
            o2deps.append(prev_o2ev if tl < 2 else None)
            o2deps.append(op_krl if tl == 0 else None)
            o2deps.append(fins.get(tl - 2))
            op_o2 = P.add("tensor", (lambda tl=tl, pr=pr: lambda e: [
                m for dc in range(8) for m in (
                    e.matmul(out=pr[:, 0:512],
                             lhsT=kact[:, dc * CHALF + tl * 128:
                                      dc * CHALF + (tl + 1) * 128],
                             rhs=Wv_sb[:, dc * 1024:dc * 1024 + 512],
                             start=(dc == 0), stop=(dc == 7)),
                    e.matmul(out=pr[:, 512:1024],
                             lhsT=kact[:, dc * CHALF + tl * 128:
                                      dc * CHALF + (tl + 1) * 128],
                             rhs=Wv_sb[:, dc * 1024 + 512:(dc + 1) * 1024],
                             start=(dc == 0), stop=(dc == 7)),
                )])(), deps=o2deps, tag=f"o2{hf}_{tl}")
            fpar = tl % 2
            fb = E[:, 2048 + fpar * 1024:3072 + fpar * 1024]
            op_fin = P.add("vector", [
                (lambda tl=tl, pr=pr, fb=fb: lambda e: e.tensor_mul(
                    out=fb,
                    in0=rslab[:, tl * 1024:(tl + 1) * 1024], in1=pr[:]))(),
                (lambda tl=tl, fb=fb: lambda e: e.tensor_add(
                    out=fb, in0=fb,
                    in1=y_sb[:, tl * 1024:(tl + 1) * 1024]))(),
            ], deps=[op_o2, op_rev[tl], fouts.get(tl - 2),
                     (po_half.get(TPH - 2 + tl) if tl < 2 else None)],
                tag=f"fin{hf}_{tl}", serial=True)
            fins[tl] = op_fin
            prev_o2ev = op_fin
            gt = hf * TPH + tl
            prev_out = P.add("sync", (lambda gt=gt, fb=fb: lambda e: [
                e.dma_start(out=out[gt * 128:(gt + 1) * 128, :],
                            in_=fb)])(),
                deps=[op_fin], dma=1, tag=f"odma{hf}_{tl}")
            fouts[tl] = prev_out
            prev_fin = op_fin
        prev_fouts = fouts

    if dbg:
        cap_c = P.add("vector", lambda e: [
            e.tensor_copy(out=dbg2[:, 1400:1528], in_=xn2fm[:, 0:128]),
            e.tensor_copy(out=dbg2[:, 1528:1656], in_=xrfm[:, 0:128]),
            e.tensor_copy(out=dbg2[:, 1656:1784], in_=rslab[:, 0:128]),
            e.tensor_copy(out=dbg2[:, 1784:1912], in_=kact[:, 0:128]),
            e.tensor_copy(out=dbg2[:, 1912:2040], in_=y_sb[:, 0:128]),
            e.tensor_copy(out=dbg2[:, 2040:2042], in_=statS[:, 0:2]),
            e.tensor_copy(out=dbg2[:, 2042:2044], in_=statB[:, 0:2]),
            e.tensor_copy(out=dbg2[:, 2044:2046], in_=statM[:, 0:2]),
            e.tensor_copy(out=dbg2[:, 2046:2048], in_=statQ[:, 0:2]),
        ], deps=[prev_mix_last, op_rev[TPH - 1], op_ksq[7], prev_fin],
            tag="cap_c")
        NDW = NT * 4 + 12
        dbg_sb = cst.enter_context(nc.sbuf_tensor("dbg_sb", [128, NDW], F32))
        op_dbg = P.add("vector", lambda e: [
            e.tensor_copy(out=dbg_sb[:, 0:NT * 4], in_=LGcol[:]),
            e.tensor_copy(out=dbg_sb[:, NT * 4:NT * 4 + 12],
                          in_=LGcol[:, 0:12]),
        ], deps=[prev_yr_last, op_cc, prev_fin, cap_c], tag="dbgv")
        op_dbgd = P.add("sync", lambda e: [
            e.dma_start(out=dbgl, in_=dbg_sb[:]),
            e.dma_start(out=dbgy, in_=cc_in_h.ap()),
            e.dma_start(out=dbg2o, in_=dbg2[:]),
            e.dma_start(out=dbg3o, in_=dbg3[:]),
        ], deps=[op_dbg, op_cc], dma=4, tag="dbgd")
    else:
        op_dbgd = None
    P.add("gpsimd", lambda e: [e.engine_nop()],
          deps=[prev_out, prev_fouts.get(TPH - 2), op_dbgd], tag="done")

    P.emit(nc, blk, sems)
    cst.close()
    per.close()
    return nc


def kernel(L0_attn_x, L0_attn_W, L0_ffn_W_r, L0_ffn_W_k, L0_ffn_W_v,
           L0_ffn_mu_r, L0_ffn_mu_k, _T=None):
    x = np.ascontiguousarray(L0_attn_x, dtype=np.float32)
    W = np.asarray(L0_attn_W, dtype=np.float32)
    B, T, _ = x.shape
    if _T is not None:
        T = _T
        x = x[:, :T]
    SLICE = T // 4

    nc = build(T)

    identb = np.eye(128, dtype=BF)
    identf = np.eye(128, dtype=np.float32)
    triu = np.triu(np.ones((128, 128), dtype=np.float32)).astype(BF)
    onesrow = np.ones((1, 128), dtype=np.float32)
    mur = np.ascontiguousarray(
        np.asarray(L0_ffn_mu_r, np.float32).reshape(8, 128).T)
    muk = np.ascontiguousarray(
        np.asarray(L0_ffn_mu_k, np.float32).reshape(8, 128).T)
    wr = np.ascontiguousarray(np.asarray(L0_ffn_W_r, np.float32).T)
    wk2 = np.ascontiguousarray(np.asarray(L0_ffn_W_k, np.float32).T)
    wv2 = np.ascontiguousarray(np.asarray(L0_ffn_W_v, np.float32).T)

    in_maps = []
    for c in range(8):
        b = c // 4
        hc = c % 4
        q = c % 4
        Wq = W[256 * hc:256 * (hc + 1), :]
        Wv = W[2048 + 256 * hc:2048 + 256 * (hc + 1), :]
        Wk = W[1024 + 256 * hc:1024 + 256 * (hc + 1), :]
        Wg = W[3072 + 4 * hc:3072 + 4 * (hc + 1), :]
        wqv = np.ascontiguousarray(
            np.concatenate([Wq.T, Wv.T], axis=1))
        wkg = np.ascontiguousarray(
            np.concatenate([Wk.T, Wg.T], axis=1))
        t0 = q * SLICE
        xb0 = (np.zeros((1, C), np.float32) if t0 == 0
               else np.ascontiguousarray(x[b, t0 - 1:t0]))
        in_maps.append({
            "xb": np.ascontiguousarray(x[b]),
            "xsl": np.ascontiguousarray(x[b, t0:t0 + SLICE]),
            "xb0": xb0,
            "wqv": wqv, "wkg": wkg,
            "wr": wr, "wk2": wk2, "wv2": wv2,
            "mur": mur, "muk": muk,
            "identb": identb, "identf": identf, "triu": triu,
            "onesrow": onesrow,
        })

    res = run_bass_kernel_spmd(nc, in_maps, core_ids=list(range(8)))
    outp = np.empty((B, T, C), dtype=np.float32)
    for c in range(8):
        b = c // 4
        q = c % 4
        outp[b, q * SLICE:(q + 1) * SLICE] = res.results[c]["out"]
    return outp


if __name__ == "__main__":
    build(1024)
    print("build ok")
